# version 13
# speedup vs baseline: 1.0349x; 1.0015x over previous
import numpy as np
import ml_dtypes
import concourse.bass as bass
import concourse.mybir as mybir
from concourse.bass_utils import run_bass_kernel_spmd

F32 = mybir.dt.float32
BF16 = mybir.dt.bfloat16
AF = mybir.ActivationFunctionType
ALU = mybir.AluOpType
AXX = mybir.AxisListType.X
BF = ml_dtypes.bfloat16

T = 2048
D = 1024
NT = 16
ND = 8
H = 16
VA = 65
NEG = -1.0e38
EPS = 1e-5
CAP = 2560.0
CSC = 0.1250625


class Prog:

    def __init__(self, nc):
        self.nc = nc
        self.recs = []
        self.cnt = {}
        self.sem_names = []
        self.uid = 0
        self._mgr_map = {}
        self._stacks = {"s": [], "p": []}
        self._kind = {}
        self._fence = ()

    def sem(self, name):
        self.sem_names.append(name)
        self.cnt[name] = 0

    def fence(self):
        self._fence = tuple((s, self.cnt[s]) for s in self.sem_names if self.cnt[s] > 0)

    def op(self, eng, fn, wait=(), inc=None, n=1):
        w = tuple(t for t in wait if t is not None) + self._fence
        tok = None
        if inc is not None:
            self.cnt[inc] += n
            tok = (inc, self.cnt[inc])
        self.recs.append((eng, fn, w, inc, n))
        return tok

    def t(self, fn, wait=(), inc="pe", n=1):
        return self.op("tensor", fn, wait, inc, n)

    def v(self, fn, wait=(), inc="dve", n=1):
        return self.op("vector", fn, wait, inc, n)

    def a(self, fn, wait=(), inc="act", n=1):
        return self.op("scalar", fn, wait, inc, n)

    def g(self, fn, wait=(), inc="dg", n=16):
        w = tuple(wait) + (((inc, self.cnt[inc]),) if self.cnt.get(inc) else ())
        return self.op("gpsimd", fn, w, inc, n)

    def s(self, fn, wait=(), inc="ds", n=16):
        w = tuple(wait) + (((inc, self.cnt[inc]),) if self.cnt.get(inc) else ())
        return self.op("sync", fn, w, inc, n)

    def name(self, base):
        self.uid += 1
        return f"{base}_{self.uid}"

    def sbuf(self, base, shape, dtype):
        mgr = self.nc.sbuf_tensor(self.name(base), shape, dtype)
        h = mgr.__enter__()
        self._mgr_map[id(h)] = mgr
        self._kind[id(h)] = "s"
        self._stacks["s"].append(id(h))
        return h

    def psum(self, base, shape, dtype=F32):
        mgr = self.nc.psum_tensor(self.name(base), shape, dtype)
        h = mgr.__enter__()
        self._mgr_map[id(h)] = mgr
        self._kind[id(h)] = "p"
        self._stacks["p"].append(id(h))
        return h

    def free(self, *handles):
        for kind in ("s", "p"):
            ids = {id(h) for h in handles if self._kind.get(id(h)) == kind}
            if not ids:
                continue
            st = self._stacks[kind]
            assert ids == set(st[-len(ids):]), f"non-stack free ({kind})"
            for _ in range(len(ids)):
                hid = st.pop()
                self._mgr_map.pop(hid).__exit__(None, None, None)
                del self._kind[hid]

    def push(self):
        return (len(self._stacks["s"]), len(self._stacks["p"]))

    def pop(self, marker):
        ms, mp = marker
        for kind, m in (("p", mp), ("s", ms)):
            st = self._stacks[kind]
            while len(st) > m:
                hid = st.pop()
                self._mgr_map.pop(hid).__exit__(None, None, None)
                del self._kind[hid]

    def finish(self):
        sems = {}
        sem_mgrs = []
        for nm in self.sem_names:
            mgr = self.nc.semaphore(nm)
            sems[nm] = mgr.__enter__()
            sem_mgrs.append(mgr)
        blk_mgr = self.nc.Block()
        block = blk_mgr.__enter__()
        waited = {}

        def emitter(eng_name):
            def _(engine):
                for (e, fn, waits, inc, n) in self.recs:
                    if e != eng_name:
                        continue
                    for (snm, c) in waits:
                        if waited.get((eng_name, snm), 0) < c:
                            engine.wait_ge(sems[snm], c)
                            waited[(eng_name, snm)] = c
                    if fn is None:
                        continue
                    inst = fn(engine)
                    if inc is not None:
                        inst.then_inc(sems[inc], n)
            _.__name__ = "_"
            return _

        block.tensor(emitter("tensor"))
        block.vector(emitter("vector"))
        block.scalar(emitter("scalar"))
        block.gpsimd(emitter("gpsimd"))
        block.sync(emitter("sync"))
        blk_mgr.__exit__(None, None, None)
        for mgr in reversed(sem_mgrs):
            mgr.__exit__(None, None, None)


def host_layouts(inputs):
    f = np.float32

    def C(x):
        return np.ascontiguousarray(np.asarray(x, dtype=f))

    ins = {}
    ins["qk_w"] = C(inputs["L0_attn_qk_w"])
    ins["qk_bT"] = C(np.asarray(inputs["L0_attn_qk_b"]).reshape(16, 128).T)
    ins["v_w"] = C(inputs["L0_attn_v_w"])
    ins["cproj_w"] = C(inputs["L0_attn_cproj_w"])
    for nm, key in (("resid64", "L0_attn_resid_gain"), ("skip64", "L0_attn_skip_gain"),
                    ("centre64", "L0_attn_centre_gain")):
        g = np.asarray(inputs[key]).reshape(H)
        ins[nm] = C(np.broadcast_to(g[None, :], (64, H)))
    ins["rw"] = C(inputs["L0_ffn_router_w"])
    rb = np.asarray(inputs["L0_ffn_router_b"], dtype=f)
    ins["rb_rep"] = C(np.broadcast_to(rb[:, None, :], (4, 128, 4)))
    ins["ln_gT"] = C(np.asarray(inputs["L0_ffn_ln_g"]).reshape(4, 4, 8, 128).transpose(0, 1, 3, 2))
    ins["ln_bT"] = C(np.asarray(inputs["L0_ffn_ln_b"]).reshape(4, 4, 8, 128).transpose(0, 1, 3, 2))
    ins["w1"] = C(inputs["L0_ffn_w1"])
    ins["b1T"] = C(np.asarray(inputs["L0_ffn_b1"]).reshape(4, 4, 32, 128).transpose(0, 1, 3, 2))
    ins["w2"] = C(inputs["L0_ffn_w2"])
    ins["b2r"] = C(np.asarray(inputs["L0_ffn_b2"]).reshape(4, 4, 1, 1024))
    ins["qkv_wT"] = C(np.asarray(inputs["L1_attn_qkv_w"]).T)
    for nm, key in (("qn_w128", "L1_attn_qn_w"), ("qn_b128", "L1_attn_qn_b"),
                    ("kn_w128", "L1_attn_kn_w"), ("kn_b128", "L1_attn_kn_b")):
        v = np.asarray(inputs[key], dtype=f)
        ins[nm] = C(np.concatenate([v, v])[:, None])
    ins["sinks"] = C(np.asarray(inputs["L1_attn_sinks"])[None, :])
    ins["proj_wT"] = C(np.asarray(inputs["L1_attn_proj_w"]).T)
    ins["projb_rep"] = C(np.broadcast_to(
        np.asarray(inputs["L1_attn_proj_b"], dtype=f)[None, :], (128, D)))
    ins["gate_wT"] = C(np.asarray(inputs["L1_ffn_gate_w"]).T)
    ins["gpwT"] = C(np.asarray(inputs["L1_ffn_gate_proj_w"]).transpose(0, 2, 1))
    ins["upwT"] = C(np.asarray(inputs["L1_ffn_up_proj_w"]).transpose(0, 2, 1))
    ins["downT"] = C(np.asarray(inputs["L1_ffn_down_proj_w"]).transpose(0, 2, 1))
    ins["identity"] = np.eye(128, dtype=f)
    b = np.arange(128)[:, None]
    a = np.arange(128)[None, :]
    ins["mask_kgtq"] = np.where(b > a, NEG, 0.0).astype(f)
    ins["mask_kleq"] = np.where(b <= a, NEG, 0.0).astype(f)
    inv_idx = (1.0 / (np.arange(T, dtype=np.float64) + 1.0)).astype(f)
    ins["invidx64"] = C(np.broadcast_to(inv_idx[None, :], (64, T)))
    invf = (1.0 / (10000.0 ** (np.arange(0, 64, 2).astype(np.float64) / 64.0))).astype(f)
    emb = np.arange(T, dtype=f)[:, None] * invf[None, :]
    cos = np.cos(np.concatenate([emb, emb], 1)).astype(f).T
    sin = np.sin(np.concatenate([emb, emb], 1)).astype(f).T
    sgn = np.concatenate([-np.ones(32, f), np.ones(32, f)])[:, None]
    ins["cos128"] = C(np.concatenate([cos, cos], 0))
    ins["sin128"] = C(np.concatenate([sin * sgn, sin * sgn], 0))
    ins["ones_row"] = np.ones((1, 128), dtype=f)
    ins["ones_col"] = np.ones((128, 1), dtype=f)
    blk2 = np.zeros((128, 2), dtype=f)
    blk2[0:64, 0] = 1.0
    blk2[64:128, 1] = 1.0
    ins["blk2"] = blk2
    ins["blk2T"] = C(blk2.T)
    return ins


L0_KEYS = ["qk_w", "qk_bT", "v_w", "cproj_w", "resid64", "skip64", "centre64",
           "rw", "rb_rep", "ln_gT", "ln_bT", "w1", "b1T", "w2", "b2r"]
L1_KEYS = ["qkv_wT", "qn_w128", "qn_b128", "kn_w128", "kn_b128", "sinks",
           "proj_wT", "projb_rep", "gate_wT", "gpwT", "upwT", "downT"]
CONST_KEYS = ["identity", "mask_kgtq", "mask_kleq", "invidx64", "cos128", "sin128",
              "ones_row", "ones_col", "blk2", "blk2T"]


class Core:
    def __init__(self, mode, shapes, dbg=()):
        self.mode = mode
        self.dbg = set(dbg)
        self.dbg_names = []
        nc = bass.Bass("TRN2", target_bir_lowering=False)
        self.nc = nc
        p = Prog(nc)
        self.p = p
        for s in ("pe", "act", "dve", "dg", "ds", "cc", "dwa", "dwb", "dw2",
                  "dua", "dub", "dga", "dgb", "ddn", "dpw", "dcn", "dcg",
                  "dsa", "dsb", "dsc", "dsd", "dse"):
            p.sem(s)
        need = list(CONST_KEYS)
        if mode in ("l0", "fused"):
            need += L0_KEYS
        if mode == "counts":
            need += ["gate_wT"]
        if mode in ("l1", "fused"):
            need += L1_KEYS
        self.inp = {}
        for k in need:
            self.inp[k] = nc.dram_tensor(k, list(shapes[k]), F32, kind="ExternalInput").ap()
        if mode in ("l0", "fused"):
            self.inp["x"] = nc.dram_tensor("x", [T, D], F32, kind="ExternalInput").ap()
        if mode in ("counts", "l1"):
            self.inp["x1_in"] = nc.dram_tensor("x1_in", [T, D], F32, kind="ExternalInput").ap()
        if mode == "l1":
            self.inp["allcounts"] = nc.dram_tensor("allcounts", [8, 16], F32,
                                                   kind="ExternalInput").ap()
        if mode in ("l1", "fused"):
            self.inp["selmask"] = nc.dram_tensor("selmask", [8, 1], F32,
                                                 kind="ExternalInput").ap()
        if mode == "l0":
            self.out_x1 = nc.dram_tensor("x1", [T, D], F32, kind="ExternalOutput").ap()
        if mode == "counts":
            self.out_counts = nc.dram_tensor("counts", [1, 16], F32, kind="ExternalOutput").ap()
        if mode in ("l1", "fused"):
            self.out = nc.dram_tensor("out", [T, D], F32, kind="ExternalOutput").ap()
        if mode in ("l0", "fused"):
            self.acc0_hbm = nc.dram_tensor("acc0_hbm", [T, D], F32).ap()
            self.accf_hbm = nc.dram_tensor("accf_hbm", [T, D], F32).ap()
        if mode == "fused":
            self.x1_hbm = nc.dram_tensor("x1_hbm", [T, D], F32).ap()
            self.cc_in = nc.dram_tensor("cc_in", [1, 16], F32).ap()
            self.cc_out = nc.dram_tensor("cc_out", [8, 16], F32).ap()
        self.B1 = p.sbuf("B1", [128, ND * T], BF16)
        self.B2 = p.sbuf("B2", [128, NT * D], BF16)
        self.ident = p.sbuf("ident", [128, 128], BF16)
        self.ident_f = p.sbuf("identf", [128, 128], F32)
        self.m_kgtq = p.sbuf("mkgtq", [128, 128], F32)
        self.m_kleq = p.sbuf("mkleq", [128, 128], F32)
        self.ones_row_f = p.sbuf("onesrf", [1, 128], F32)
        self.ones_row_bf = p.sbuf("onesrb", [1, 128], BF16)
        self.ones_col_bf = p.sbuf("onescb", [128, 1], BF16)
        self.ones_col_f = p.sbuf("onescf", [128, 1], F32)
        p.v(lambda e: e.memset(self.ones_col_f[:, :], 1.0))
        self._joinbuf = p.sbuf("joinb", [1, 1], F32)
        self.epsb = p.sbuf("epsb", [128, 1], F32)
        self.c0125 = p.sbuf("c0125", [128, 1], F32)
        p.v(lambda e: e.memset(self.epsb[:, :], float(EPS)))
        p.v(lambda e: e.memset(self.c0125[:, :], 0.125))
        self.dma_c(self.ident[:, :], self.inp["identity"])
        self.dma_f(self.ident_f[:, :], self.inp["identity"])
        self.dma_f(self.m_kgtq[:, :], self.inp["mask_kgtq"])
        self.dma_f(self.m_kleq[:, :], self.inp["mask_kleq"])
        self.dma_f(self.ones_row_f[:, :], self.inp["ones_row"])
        self.dma_c(self.ones_row_bf[:, :], self.inp["ones_row"])
        self.dma_c(self.ones_col_bf[:, :], self.inp["ones_col"])
        p.fence()

    def dma_f(self, dst, src, wait=(), q="ds"):
        return self.p.s(lambda e: e.dma_start(out=dst, in_=src), wait=wait, inc=q)

    def dma_c(self, dst, src, wait=(), q="dg"):
        return self.p.g(lambda e: e.dma_start(out=dst, in_=src), wait=wait, inc=q)

    def dma_w(self, dst2d, src2d, nt_, wait=(), q="dg"):
        src3 = src2d.rearrange("(dt p) c -> p dt c", p=128)
        dst3 = dst2d.rearrange("p (dt c) -> p dt c", dt=nt_)
        return self.p.g(lambda e: e.dma_start(out=dst3, in_=src3), wait=wait, inc=q)

    def dump(self, name, src_ap, shape, dtype=F32):
        if name not in self.dbg:
            return
        t = self.nc.dram_tensor("dbg_" + name, list(shape), dtype, kind="ExternalOutput").ap()
        self.dbg_names.append(name)
        self.p.fence()
        self.dma_f(t, src_ap)
        self.p.fence()

    def ln_pass(self, src, src2=None, copy_out=None,
                tm_out=None, T_out=None, tiles=range(NT)):
        p = self.p
        tm_out = tm_out if tm_out is not None else self.B2
        T_out = T_out if T_out is not None else self.B1
        mark = p.push()
        NS = 4
        xa = p.sbuf("lnx", [128, NS * D], F32)
        xb = p.sbuf("lnx2", [128, NS * D], F32) if src2 is not None else None
        stat = [p.sbuf(f"lnst{i}", [128, 12], F32) for i in range(NS)]
        mv = [p.sbuf(f"lnmv{i}", [128, 2], F32) for i in range(NS)]
        rs = [p.sbuf(f"lnrs{i}", [128, 1], F32) for i in range(NS)]
        sd = [p.sbuf(f"lnsd{i}", [128, 1], F32) for i in range(NS)]
        nmr = [p.sbuf(f"lnnm{i}", [128, 1], F32) for i in range(NS)]
        ptr = [p.psum("lntr0", [128, 512], BF16), p.psum("lntr1", [128, 512], BF16)]
        free_tok = [None] * NS
        tr_free = [None, None]
        stat_free = [None] * NS
        for t in tiles:
            sl = t % NS
            xap = xa[:, sl * D:(sl + 1) * D]
            tok = self.dma_f(xap, src[t * 128:(t + 1) * 128, :],
                             wait=free_tok[sl] or (),
                             q=("dsa", "dsb", "dsa", "dsb")[sl])
            if src2 is not None:
                bap = xb[:, sl * D:(sl + 1) * D]
                tok2 = self.dma_f(bap, src2[t * 128:(t + 1) * 128, :],
                                  wait=free_tok[sl] or (),
                                  q=("dsc", "dsd", "dsc", "dsd")[sl])
                tok = p.v(lambda e, xap=xap, bap=bap: e.tensor_tensor(
                    xap, xap, bap, ALU.add), wait=[tok, tok2])
                cp_tok = None
                if copy_out is not None:
                    cp_tok = self.dma_f(copy_out[t * 128:(t + 1) * 128, :], xap,
                                        wait=[tok])
            p.v(lambda e, xap=xap, sl=sl: e.bn_stats(stat[sl][:, 0:6], xap[:, 0:512]),
                wait=[tok] + (stat_free[sl] or []))
            t_st = p.v(lambda e, xap=xap, sl=sl: e.bn_stats(stat[sl][:, 6:12],
                                                            xap[:, 512:1024]),
                       wait=[tok] + (stat_free[sl] or []))
            t_mv = p.v(lambda e, sl=sl: e.bn_aggr(mv[sl][:, :], stat[sl][:, :]),
                       wait=[t_st])
            t_sd = p.a(lambda e, sl=sl: e.activation(sd[sl][:, :], mv[sl][:, 1:2],
                                                     AF.Sqrt, bias=self.epsb[:, :]),
                       wait=[t_mv])
            t_rs = p.v(lambda e, sl=sl: e.reciprocal(rs[sl][:, :], sd[sl][:, :]),
                       wait=[t_sd])
            t_nm = p.v(lambda e, sl=sl: e.tensor_scalar(nmr[sl][:, :], mv[sl][:, 0:1],
                                                        rs[sl][:, :], -1.0,
                                                        ALU.mult, ALU.mult),
                       wait=[t_rs])
            t_xn = p.v(lambda e, xap=xap, t=t, sl=sl: e.tensor_scalar(
                tm_out[:, t * D:(t + 1) * D], xap, rs[sl][:, :], nmr[sl][:, :],
                ALU.mult, ALU.add), wait=[t_nm])
            stat_free[sl] = [t_nm]
            free_tok[sl] = [t_xn] if src2 is None else [t_xn, cp_tok]
            for jg in range(2):
                last = None
                pslot = ptr[jg]
                for j4 in range(4):
                    j = jg * 4 + j4
                    last = p.t(lambda e, t=t, j=j, j4=j4, pslot=pslot: e.transpose(
                        pslot[:, j4 * 128:(j4 + 1) * 128],
                        tm_out[:, t * D + j * 128:t * D + (j + 1) * 128],
                        self.ident[:, :]),
                        wait=[t_xn] + ([tr_free[jg]] if tr_free[jg] is not None and
                                       j4 == 0 else []))
                tr_free[jg] = p.v(lambda e, t=t, jg=jg, pslot=pslot: e.tensor_copy(
                    T_out.rearrange("p (j tt) -> p j tt", j=ND)[:, jg * 4:(jg + 1) * 4,
                                                                t * 128:(t + 1) * 128],
                    pslot.rearrange("p (j tt) -> p j tt", j=4)[:, :, :]),
                    wait=[last])
        p.pop(mark)
        p.fence()

    def attention(self, layer):
        p = self.p
        L1 = (layer == 1)
        aoT = self.B2 if L1 else p.sbuf("aoT", [128, ND * T], BF16)
        mark = p.push()
        vt = p.sbuf("vaug", [128, NT * H * VA], BF16)
        vw = p.sbuf("vw", [128, ND * D], BF16)
        if not L1:
            vsrc = self.inp["v_w"]
        else:
            vsrc = self.inp["qkv_wT"][:, 2048:3072]
        tok_vw = self.dma_w(vw[:, :], vsrc, ND)
        pvv = p.psum("pvv", [128, 1024], F32)
        vev = None
        for t in range(NT):
            last = None
            for dc in range(2):
                for d in range(ND):
                    last = p.t(lambda e, t=t, dc=dc, d=d: e.matmul(
                        pvv[:, dc * 512:(dc + 1) * 512],
                        self.B1[:, d * T + t * 128:d * T + (t + 1) * 128],
                        vw[:, d * D + dc * 512:d * D + (dc + 1) * 512],
                        start=(d == 0), stop=(d == ND - 1)),
                        wait=[tok_vw] + ([vev] if vev is not None and dc == 0 and d == 0 else []))
            vev = p.v(lambda e, t=t: e.tensor_copy(
                vt.rearrange("p (tt h c) -> p tt h c", tt=NT, h=H)[:, t, :, 0:64],
                pvv.rearrange("p (h c) -> p h c", h=H)[:, :, :]), wait=[last])
        vready = p.v(lambda e: e.memset(
            vt.rearrange("p (tth c) -> p tth c", c=VA)[:, :, 64:65], 1.0), wait=[vev])
        p.fence()
        p.free(pvv, vw)
        ctx = {}
        if not L1:
            ctx["qkb"] = p.sbuf("qkb", [128, 16], F32)
            self.dma_f(ctx["qkb"][:, :], self.inp["qk_bT"])
            for nm in ("resid64", "skip64", "centre64"):
                ctx[nm] = p.sbuf(nm, [64, H], F32)
                tk = self.dma_f(ctx[nm][:, :], self.inp[nm])
            ctx["negc64"] = p.sbuf("negc64", [64, H], F32)
            p.v(lambda e: e.tensor_scalar(ctx["negc64"][:, :], ctx["centre64"][:, :],
                                          -1.0, None, ALU.mult), wait=[tk])
            ctx["invidx"] = p.sbuf("invidx", [64, T], F32)
            self.dma_f(ctx["invidx"][:, :], self.inp["invidx64"])
            ctx["vTh"] = [p.sbuf("vTh0", [64, T], BF16), p.sbuf("vTh1", [64, T], BF16)]
            ctx["csh"] = [p.sbuf("csh0", [64, T], F32), p.sbuf("csh1", [64, T], F32)]
            ctx["zb"] = [p.sbuf("zb0", [64, 512], F32), p.sbuf("zb1", [64, 512], F32)]
            ctx["zero64"] = p.sbuf("zero64", [64, 1], F32)
            p.v(lambda e: e.memset(ctx["zero64"][:, :], 0.0))
        else:
            ctx["cosb"] = p.sbuf("cosb", [128, T], F32)
            ctx["sinb"] = p.sbuf("sinb", [128, T], F32)
            self.dma_f(ctx["cosb"][:, :], self.inp["cos128"])
            self.dma_f(ctx["sinb"][:, :], self.inp["sin128"])
            ctx["sink_sb"] = p.sbuf("sink", [1, 16], F32)
            self.dma_f(ctx["sink_sb"][:, :], self.inp["sinks"])
            ctx["shufb"] = p.sbuf("shuf", [128, T], BF16)
            ctx["sq16"] = p.sbuf("sq16", [128, T], BF16)
            ctx["lntA"] = p.sbuf("lntA", [128, T], F32)
            ctx["lntB"] = p.sbuf("lntB", [128, T], F32)
            ctx["blk2b"] = p.sbuf("blk2b", [128, 2], BF16)
            self.dma_c(ctx["blk2b"][:, :], self.inp["blk2"])
            ctx["blk2T"] = p.sbuf("blk2Tf", [2, 128], F32)
            self.dma_f(ctx["blk2T"][:, :], self.inp["blk2T"])
            for wn, key in (("qnw", "qn_w128"), ("qnb", "qn_b128"),
                            ("knw", "kn_w128"), ("knb", "kn_b128")):
                ctx[wn] = p.sbuf(wn, [128, 1], F32)
                self.dma_f(ctx[wn][:, :], self.inp[key])
        ctx["neg1"] = p.sbuf("neg1L0", [1, 1], F32)
        p.v(lambda e: e.memset(ctx["neg1"][:, :], -1.0))
        ctx["qb"] = p.sbuf("qbuf", [128, T], BF16)
        ctx["kb"] = p.sbuf("kbuf", [128, T], BF16)
        ctx["pairw"] = p.sbuf("pairw", [128, ND * 256], BF16)
        ctx["sq"] = p.sbuf("sqb", [128, T], BF16)
        ctx["nrm"] = p.sbuf("nrm", [1, T], F32)
        ctx["mx"] = p.sbuf("mxb", [1, 8], F32)
        ctx["chb"] = p.sbuf("chb", [128, 2], F32)
        ctx["sinkx"] = p.sbuf("sinkx", [1, 2], F32)
        ctx["pT"] = [p.sbuf("pT0", [128, 2 * 512], BF16),
                     p.sbuf("pT1", [128, 2 * 512], BF16)]
        ctx["invr"] = [p.sbuf("invr0", [1, 512], F32), p.sbuf("invr1", [1, 512], F32)]
        ctx["ev64"] = [p.sbuf("ev640", [64, 512], F32), p.sbuf("ev641", [64, 512], F32)]
        ctx["t1b"] = [p.sbuf("t1b0", [64, 512], F32), p.sbuf("t1b1", [64, 512], F32)]
        ctx["vready"] = vready
        ctx["vt"] = vt
        ctx["aoT"] = aoT
        p.fence()
        stage = getattr(self, "attn_stage", None)
        tail = None
        if stage != "v":
            npair = 1 if stage in ("pairs", "heads1") else 8
            for jp in range(npair):
                tail = self.attn_pair(layer, jp, ctx, tail)
        p.fence()
        p.pop(mark)
        return aoT

    def attention_l1(self):
        p = self.p
        aoT = self.B2
        mark = p.push()
        p.fence()
        vt = p.sbuf("vaug", [128, NT * H * VA], BF16)
        vw = p.sbuf("vw", [128, ND * D], BF16)
        tok_vw = self.dma_w(vw[:, :], self.inp["qkv_wT"][:, 2048:3072], ND)
        pvv = p.psum("pvv", [128, 1024], F32)
        vev = None
        for t in range(NT):
            last = None
            for dc in range(2):
                for d in range(ND):
                    last = p.t(lambda e, t=t, dc=dc, d=d: e.matmul(
                        pvv[:, dc * 512:(dc + 1) * 512],
                        self.B1[:, d * T + t * 128:d * T + (t + 1) * 128],
                        vw[:, d * D + dc * 512:d * D + (dc + 1) * 512],
                        start=(d == 0), stop=(d == ND - 1)),
                        wait=[tok_vw] + ([vev] if vev is not None and dc == 0 and d == 0 else []))
            vev = p.v(lambda e, t=t: e.tensor_copy(
                vt.rearrange("p (tt h c) -> p tt h c", tt=NT, h=H)[:, t, :, 0:64],
                pvv.rearrange("p (h c) -> p h c", h=H)[:, :, :]), wait=[last])
        vready = p.v(lambda e: e.memset(
            vt.rearrange("p (tth c) -> p tth c", c=VA)[:, :, 64:65], 1.0), wait=[vev])
        p.fence()
        p.free(pvv, vw)
        ctx = {}
        ctx["cosb"] = p.sbuf("cosb", [128, T], F32)
        ctx["sinb"] = p.sbuf("sinb", [128, T], F32)
        self.dma_f(ctx["cosb"][:, :], self.inp["cos128"])
        self.dma_f(ctx["sinb"][:, :], self.inp["sin128"])
        ctx["sink_sb"] = p.sbuf("sink", [1, 16], F32)
        self.dma_f(ctx["sink_sb"][:, :], self.inp["sinks"])
        ctx["shufb"] = p.sbuf("shuf", [128, T], BF16)
        ctx["sq16"] = p.sbuf("sq16", [128, T], BF16)
        ctx["ropeA"] = p.sbuf("ropeA", [128, T], F32)
        ctx["ropeB"] = p.sbuf("ropeB", [128, T], F32)
        ctx["blk2b"] = p.sbuf("blk2b", [128, 2], BF16)
        self.dma_c(ctx["blk2b"][:, :], self.inp["blk2"])
        ctx["blk2T"] = p.sbuf("blk2Tf", [2, 128], F32)
        self.dma_f(ctx["blk2T"][:, :], self.inp["blk2T"])
        for wn, key in (("qnw", "qn_w128"), ("qnb", "qn_b128"),
                        ("knw", "kn_w128"), ("knb", "kn_b128")):
            ctx[wn] = p.sbuf(wn, [128, 1], F32)
            self.dma_f(ctx[wn][:, :], self.inp[key])
        ctx["qbs"] = [p.sbuf("qb0", [128, T], BF16), p.sbuf("qb1", [128, T], BF16)]
        ctx["kbs"] = [p.sbuf("kb0", [128, T], BF16), p.sbuf("kb1", [128, T], BF16)]
        ctx["pairw"] = p.sbuf("pairw", [128, ND * 256], BF16)
        ctx["sq"] = p.sbuf("sqb", [128, T], BF16)
        ctx["nrm"] = p.sbuf("nrm", [1, T], F32)
        ctx["mxs"] = [p.sbuf("mx0", [1, 8], F32), p.sbuf("mx1", [1, 8], F32)]
        ctx["chbs"] = [p.sbuf("chb0", [128, 2], F32), p.sbuf("chb1", [128, 2], F32)]
        ctx["sinkxs"] = [p.sbuf("skx0", [1, 2], F32), p.sbuf("skx1", [1, 2], F32)]
        ctx["lnmu"] = [p.sbuf("lnmu0", [2, 512], BF16), p.sbuf("lnmu1", [2, 512], BF16)]
        ctx["lnvr"] = [p.sbuf("lnvr0", [2, 512], F32), p.sbuf("lnvr1", [2, 512], F32)]
        ctx["lnm2"] = [p.sbuf("lnm20", [2, 512], F32), p.sbuf("lnm21", [2, 512], F32)]
        ctx["lnrs"] = [p.sbuf("lnrs0", [2, 512], BF16), p.sbuf("lnrs1", [2, 512], BF16)]
        ctx["blk2Tb"] = p.sbuf("blk2Tb", [2, 128], BF16)
        self.dma_c(ctx["blk2Tb"][:, :], self.inp["blk2T"])
        ctx["mxsc"] = p.sbuf("mxsc", [1, 16], F32)
        ctx["neg1"] = p.sbuf("neg1", [1, 1], F32)
        p.v(lambda e: e.memset(ctx["neg1"][:, :], -1.0))
        ctx["nhalf2"] = p.sbuf("nhalf2", [2, 1], F32)
        p.v(lambda e: e.memset(ctx["nhalf2"][:, :], -0.5))
        ctx["pT"] = [p.sbuf("pT0", [128, 2 * 512], BF16),
                     p.sbuf("pT1", [128, 2 * 512], BF16)]
        ctx["invr"] = [p.sbuf("invr0", [1, 512], F32), p.sbuf("invr1", [1, 512], F32)]
        ctx["ev64"] = [p.sbuf("ev640", [64, 512], F32), p.sbuf("ev641", [64, 512], F32)]
        ctx["vready"] = vready
        ctx["vt"] = vt
        ctx["aoT"] = aoT
        ctx["W2"] = [p.psum("WA", [128, 512], F32), p.psum("SA", [128, 512], F32),
                     p.psum("WB", [128, 512], F32), p.psum("SB", [128, 512], F32)]
        ctx["stp"] = [p.psum("stA", [128, 512], F32), p.psum("stB", [128, 512], F32)]
        ctx["pvP"] = p.psum("pvP", [VA, 512], F32)
        ctx["pabP"] = p.psum("pabP", [64, 512], F32)
        p.fence()
        state = {"Afree": [[None, None], [None, None]], "pairw_mm": None,
                 "nrm_free": None, "qb_free": [None, None], "rope_sc": None}
        hand = [None] * 8
        btail = None
        for jp in range(8):
            hand[jp] = self.attn_pA_l1(jp, ctx, state)
            if jp >= 1:
                btail = self.attn_pB_l1(jp - 1, ctx, hand[jp - 1], btail)
                state["qb_free"][(jp - 1) % 2] = btail
        btail = self.attn_pB_l1(7, ctx, hand[7], btail)
        self.dump("ao", aoT[:, :], [128, ND * T], BF16)
        self.attn_tail = btail
        p.pop(mark)
        return aoT

    def attn_pA_l1(self, jp, ctx, state):
        p = self.p
        par = jp % 2
        W2 = [ctx["W2"][2 * par], ctx["W2"][2 * par + 1]]
        bank_busy = list(state["Afree"][par])
        qb, kb = ctx["qbs"][par], ctx["kbs"][par]
        pairw = ctx["pairw"]
        qbf = state["qb_free"][par]
        wsrc = self.inp["qkv_wT"]
        wq = self.dma_c(pairw.rearrange("p (d two c) -> p d two c", d=ND, two=2)[:, :, 0, :],
                        wsrc[:, jp * 128:(jp + 1) * 128]
                        .rearrange("(dt p) c -> p dt c", p=128),
                        wait=[state["pairw_mm"]] if state["pairw_mm"] else (), q="dpw")
        wk = self.dma_c(pairw.rearrange("p (d two c) -> p d two c", d=ND, two=2)[:, :, 1, :],
                        wsrc[:, 1024 + jp * 128:1024 + (jp + 1) * 128]
                        .rearrange("(dt p) c -> p dt c", p=128),
                        wait=[state["pairw_mm"]] if state["pairw_mm"] else (), q="dpw")
        cps = [[None] * 4, [None] * 4]
        lastmm = None
        for ci in range(8):
            qk, cc = divmod(ci, 4)
            b = ci % 2
            pp = W2[b]
            wtok = wq if qk == 0 else wk
            for d in range(ND):
                lastmm = p.t(lambda e, qk=qk, cc=cc, d=d, pp=pp: e.matmul(
                    pp[:, :], pairw[:, d * 256 + qk * 128:d * 256 + qk * 128 + 128],
                    self.B1[:, d * T + cc * 512:d * T + (cc + 1) * 512],
                    start=(d == 0), stop=(d == ND - 1)),
                    wait=([wtok] if d == 0 else []) +
                    ([bank_busy[b]] if bank_busy[b] and d == 0 else []))
            buf = qb if qk == 0 else kb
            cp = p.a(lambda e, buf=buf, cc=cc, pp=pp: e.activation(
                buf[:, cc * 512:(cc + 1) * 512], pp[:, :], AF.Copy),
                wait=[lastmm] + ([qbf] if qbf and ci < 2 else []))
            bank_busy[b] = cp
            cps[qk][cc] = cp
        state["pairw_mm"] = lastmm
        if self.dbg and jp == 0:
            self.p.op("sync", None, wait=[cps[0][3], cps[1][3]])
            self.dump("qraw0", qb[:, :], [128, T], BF16)
            self.dump("kraw0", kb[:, :], [128, T], BF16)
        rtoks = []
        rope_prev = state["rope_sc"]
        for qk, (buf, wsb, bsb) in enumerate(
                ((qb, ctx["qnw"], ctx["qnb"]), (kb, ctx["knw"], ctx["knb"]))):
            if self.dbg and jp == 0 and qk == 0:
                self._dbg_ln = "qln0"
                self._dbg_rope = True
            r3 = self.ln64_rope_l1(buf, cps[qk], wsb, bsb, ctx, W2, bank_busy,
                                   rope_prev)
            rope_prev = r3
            rtoks.append(r3)
        state["rope_sc"] = rope_prev
        mx, chb, sinkx = ctx["mxs"][par], ctx["chbs"][par], ctx["sinkxs"][par]
        mxsc, sq = ctx["mxsc"], ctx["sq"]
        tmx = None
        for qk, (buf, tin) in enumerate(((qb, rtoks[0]), (kb, rtoks[1]))):
            tsq = p.a(lambda e, buf=buf: e.activation(sq[:, :], buf[:, :], AF.Square),
                      wait=[tin] + ([tmx] if tmx else []))
            for hh in range(2):
                g = 2 * qk + hh
                red = None
                for cc in range(4):
                    b = cc % 2
                    pnb = W2[b]
                    mmn = p.t(lambda e, hh=hh, cc=cc, pnb=pnb: e.matmul(
                        pnb[0:1, :], self.ones_col_bf[hh * 64:(hh + 1) * 64, :],
                        sq[hh * 64:(hh + 1) * 64, cc * 512:(cc + 1) * 512],
                        start=True, stop=True),
                        wait=[tsq] + ([bank_busy[b]] if bank_busy[b] else []))
                    red = p.v(lambda e, g=g, cc=cc, pnb=pnb: e.reduce_max(
                        mxsc[:, g * 4 + cc:g * 4 + cc + 1], pnb[0:1, :], axis=AXX),
                        wait=[mmn])
                    bank_busy[b] = red
                tmx = p.v(lambda e, g=g, mx=mx: e.reduce_max(
                    mx[:, g:g + 1], mxsc[:, g * 4:g * 4 + 4], axis=AXX),
                    wait=[red])
        pcb = W2[0]
        bc = []
        sk_toks = []
        for hh in range(2):
            n1 = p.v(lambda e, hh=hh, mx=mx: e.tensor_tensor(
                mx[:, 4 + hh:5 + hh], mx[:, hh:hh + 1], mx[:, 2 + hh:3 + hh],
                ALU.mult), wait=[tmx])
            n2 = p.a(lambda e, hh=hh, mx=mx: e.activation(
                mx[:, 4 + hh:5 + hh], mx[:, 4 + hh:5 + hh], AF.Sqrt), wait=[n1])
            ng = p.v(lambda e, hh=hh, mx=mx: e.tensor_scalar(
                mx[:, 4 + hh:5 + hh], mx[:, 4 + hh:5 + hh], -CSC, None,
                ALU.mult), wait=[n2])
            mmb = p.t(lambda e, hh=hh, mx=mx: e.matmul(
                pcb[:, hh:hh + 1], self.ones_row_f[:, :], mx[:, 4 + hh:5 + hh],
                start=True, stop=True),
                wait=[ng] + ([bank_busy[0]] if hh == 0 and bank_busy[0] else []))
            bc.append(p.v(lambda e, hh=hh, chb=chb: e.tensor_copy(
                chb[:, hh:hh + 1], pcb[:, hh:hh + 1]), wait=[mmb]))
            sk_toks.append(p.a(lambda e, hh=hh, jp=jp, sinkx=sinkx, mx=mx: e.activation(
                sinkx[:, hh:hh + 1],
                ctx["sink_sb"][:, 2 * jp + hh:2 * jp + hh + 1], AF.Exp,
                bias=mx[:, 4 + hh:5 + hh]), wait=[ng]))
        bank_busy[0] = bc[-1]
        state["Afree"][par] = bank_busy
        if self.dbg and jp == 0:
            self.dump("q0", qb[:, :], [128, T], BF16)
            self.dump("k0", kb[:, :], [128, T], BF16)
            self.dump("mx0", mx[:, :], [1, 8], F32)
            self.dump("chb0", chb[:, :], [128, 2], F32)
        return {"bc": bc[-1], "sk": sk_toks, "par": par}

    def ln64_rope_l1(self, buf, cp_toks, wsb, bsb, ctx, W2, bank_busy, rope_prev):
        p = self.p
        blk2b, blk2Tb = ctx["blk2b"], ctx["blk2Tb"]
        sq16, ropeA, ropeB = ctx["sq16"], ctx["ropeA"], ctx["ropeB"]
        cosb, sinb, shufb = ctx["cosb"], ctx["sinb"], ctx["shufb"]
        t1s = [None] * 4
        t2s = [None] * 4
        t5s = [None] * 4
        ua_toks = [None] * 4
        for cc in range(5):
            if cc < 4:
                b = cc % 2
                pp = W2[b]
                mu, vr, m2, rs = (ctx["lnmu"][b], ctx["lnvr"][b], ctx["lnm2"][b],
                                  ctx["lnrs"][b])
                mm1 = p.t(lambda e, cc=cc, pp=pp: e.matmul(
                    pp[0:2, :], blk2b[:, :], buf[:, cc * 512:(cc + 1) * 512],
                    start=True, stop=True),
                    wait=[cp_toks[cc]] + ([bank_busy[b]] if bank_busy[b] else []) +
                    ([rope_prev] if rope_prev and cc < 2 else []))
                tsq = p.a(lambda e, cc=cc: e.activation(
                    sq16[:, cc * 512:(cc + 1) * 512], buf[:, cc * 512:(cc + 1) * 512],
                    AF.Square), wait=[cp_toks[cc]] + ([rope_prev] if rope_prev and cc < 2 else []))
                mm2 = p.t(lambda e, cc=cc, pp=pp: e.matmul(
                    pp[32:34, :], blk2b[:, :], sq16[:, cc * 512:(cc + 1) * 512],
                    start=True, stop=True), wait=[tsq])
                t1 = p.v(lambda e, mu=mu, pp=pp: e.tensor_scalar(
                    mu[:, :], pp[0:2, :], 1.0 / 64.0, None, ALU.mult),
                    wait=[mm1])
                t2 = p.v(lambda e, vr=vr, pp=pp: e.tensor_scalar(
                    vr[:, :], pp[32:34, :], 1.0 / 64.0, None, ALU.mult),
                    wait=[mm2])
                t3 = p.v(lambda e, m2=m2, mu=mu: e.tensor_tensor(
                    m2[:, :], mu[:, :], mu[:, :], ALU.mult), wait=[t1])
                t4 = p.v(lambda e, vr=vr, m2=m2: e.tensor_tensor(
                    vr[:, :], vr[:, :], m2[:, :], ALU.subtract), wait=[t2, t3])
                t5a = p.a(lambda e, vr=vr: e.activation(
                    vr[:, :], vr[:, :], AF.Ln,
                    bias=self.epsb[0:2, 0:1]), wait=[t4])
                t5s[cc] = p.a(lambda e, vr=vr, rs=rs: e.activation(
                    rs[:, :], vr[:, :], AF.Exp,
                    scale=ctx["nhalf2"][:, :]), wait=[t5a])
                t1s[cc] = t1
                t2s[cc] = t2
            c2 = cc - 1
            if c2 >= 0:
                b = c2 % 2
                pp = W2[b]
                mu, rs = ctx["lnmu"][b], ctx["lnrs"][b]
                bmu = p.t(lambda e, mu=mu, pp=pp: e.matmul(
                    pp[:, :], blk2Tb[:, :], mu[:, :],
                    start=True, stop=True), wait=[t1s[c2], t2s[c2]])
                u1 = p.v(lambda e, c2=c2, pp=pp: e.tensor_tensor(
                    ropeA[:, c2 * 512:(c2 + 1) * 512], buf[:, c2 * 512:(c2 + 1) * 512],
                    pp[:, :], ALU.subtract), wait=[bmu])
                brs = p.t(lambda e, rs=rs, pp=pp: e.matmul(
                    pp[:, :], blk2Tb[:, :], rs[:, :],
                    start=True, stop=True), wait=[u1, t5s[c2]])
                u2 = p.v(lambda e, c2=c2, pp=pp: e.tensor_tensor(
                    ropeB[:, c2 * 512:(c2 + 1) * 512], ropeA[:, c2 * 512:(c2 + 1) * 512],
                    pp[:, :], ALU.mult), wait=[brs])
                ua_toks[c2] = p.v(lambda e, c2=c2: e.tensor_scalar(
                    buf[:, c2 * 512:(c2 + 1) * 512], ropeB[:, c2 * 512:(c2 + 1) * 512],
                    wsb[:, :], bsb[:, :], ALU.mult, ALU.add), wait=[u2])
                bank_busy[b] = u2
        if getattr(self, "_dbg_ln", None):
            self.p.op("sync", None, wait=ua_toks)
            self.dump(self._dbg_ln, buf[:, :], [128, T], BF16)
            self._dbg_ln = None
        sh = None
        for blk in range(4):
            src = blk + 1 if blk % 2 == 0 else blk - 1
            sh = self.p.s(lambda e, blk=blk, src=src: e.dma_start(
                out=shufb[blk * 32:(blk + 1) * 32, :],
                in_=buf[src * 32:(src + 1) * 32, :]), wait=ua_toks)
        r1 = p.v(lambda e: e.tensor_tensor(ropeA[:, :], buf[:, :], cosb[:, :],
                                           ALU.mult), wait=ua_toks)
        r2 = p.v(lambda e: e.tensor_tensor(ropeB[:, :], shufb[:, :], sinb[:, :],
                                           ALU.mult), wait=[sh])
        r3 = p.v(lambda e: e.tensor_tensor(buf[:, :], ropeA[:, :], ropeB[:, :],
                                           ALU.add), wait=[r1, r2])
        if getattr(self, "_dbg_rope", None):
            self.p.op("sync", None, wait=[r3])
            self.dump("shufd", shufb[:, :], [128, T], BF16)
            self.dump("cosd", cosb[:, :], [128, T], F32)
            self.dump("sind", sinb[:, :], [128, T], F32)
            self.dump("ropeAd", ropeA[:, :], [128, T], F32)
            self.dump("ropeBd", ropeB[:, :], [128, T], F32)
            self._dbg_rope = None
        return r3

    def attn_pB_l1(self, jp, ctx, hand, btail):
        par = hand["par"]
        ctx2 = dict(ctx)
        ctx2["qb"] = ctx["qbs"][par]
        ctx2["kb"] = ctx["kbs"][par]
        ctx2["chb"] = ctx["chbs"][par]
        ctx2["sinkx"] = ctx["sinkxs"][par]
        hp = (ctx["stp"], ctx["pvP"], ctx["pabP"])
        t0 = self.head_l1(2 * jp, 0, ctx2, hand["bc"], btail, hand["sk"][0], hp)
        t1 = self.head_l1(2 * jp + 1, 1, ctx2, hand["bc"], t0, hand["sk"][1], hp)
        return t1

    def attn_pair(self, layer, jp, ctx, prev_tail):
        p = self.p
        L1 = (layer == 1)
        qb, kb, pairw, sq = ctx["qb"], ctx["kb"], ctx["pairw"], ctx["sq"]
        nrm, mx, chb = ctx["nrm"], ctx["mx"], ctx["chb"]
        wsrc = self.inp["qk_w"] if not L1 else self.inp["qkv_wT"]
        wq = self.dma_c(pairw.rearrange("p (d two c) -> p d two c", d=ND, two=2)[:, :, 0, :],
                        wsrc[:, jp * 128:(jp + 1) * 128]
                        .rearrange("(dt p) c -> p dt c", p=128),
                        wait=[prev_tail] if prev_tail else (), q="dpw")
        wk = self.dma_c(pairw.rearrange("p (d two c) -> p d two c", d=ND, two=2)[:, :, 1, :],
                        wsrc[:, 1024 + jp * 128:1024 + (jp + 1) * 128]
                        .rearrange("(dt p) c -> p dt c", p=128),
                        wait=[prev_tail] if prev_tail else (), q="dpw")
        pq = p.psum("pq", [128, T], F32)
        pk = p.psum("pk", [128, T], F32)
        qmm = kmm = None
        for c in range(4):
            for d in range(ND):
                qmm = p.t(lambda e, c=c, d=d: e.matmul(
                    pq[:, c * 512:(c + 1) * 512], pairw[:, d * 256:d * 256 + 128],
                    self.B1[:, d * T + c * 512:d * T + (c + 1) * 512],
                    start=(d == 0), stop=(d == ND - 1)),
                    wait=[wq, ctx["vready"]] + ([prev_tail] if prev_tail else []))
        for c in range(4):
            for d in range(ND):
                kmm = p.t(lambda e, c=c, d=d: e.matmul(
                    pk[:, c * 512:(c + 1) * 512], pairw[:, d * 256 + 128:d * 256 + 256],
                    self.B1[:, d * T + c * 512:d * T + (c + 1) * 512],
                    start=(d == 0), stop=(d == ND - 1)), wait=[wk])
        if not L1:
            tq = p.v(lambda e, jp=jp: e.tensor_scalar(
                qb[:, :], pq[:, :], ctx["qkb"][:, jp:jp + 1], None, ALU.add),
                wait=[qmm])
            tk = p.v(lambda e, jp=jp: e.tensor_scalar(
                kb[:, :], pk[:, :], ctx["qkb"][:, 8 + jp:9 + jp], None, ALU.add),
                wait=[kmm])
        else:
            tq = p.a(lambda e: e.activation(qb[:, :], pq[:, :], AF.Copy), wait=[qmm])
            tk = p.a(lambda e: e.activation(kb[:, :], pk[:, :], AF.Copy), wait=[kmm])
        p.free(pq, pk)
        if L1:
            tq = self.qk_ln_rope(qb, tq, ctx["qnw"], ctx["qnb"], ctx, None)
            tk = self.qk_ln_rope(kb, tk, ctx["knw"], ctx["knb"], ctx, tq)
        pn = p.psum("pn", [1, T], F32)
        tmx = None
        free_pn = None
        for qk, (buf, tin) in enumerate(((qb, tq), (kb, tk))):
            tsq = p.a(lambda e, buf=buf: e.activation(sq[:, :], buf[:, :], AF.Square),
                      wait=[tin, tk if L1 else None] + ([tmx] if tmx else []))
            for hh in range(2):
                mmn = None
                for c in range(4):
                    mmn = p.t(lambda e, hh=hh, c=c: e.matmul(
                        pn[:, c * 512:(c + 1) * 512],
                        self.ones_col_bf[hh * 64:(hh + 1) * 64, :],
                        sq[hh * 64:(hh + 1) * 64, c * 512:(c + 1) * 512],
                        start=True, stop=True),
                        wait=[tsq] + ([free_pn] if free_pn and c == 0 else []))
                free_pn = p.v(lambda e, qk=qk, hh=hh: e.reduce_max(
                    mx[:, 2 * qk + hh:2 * qk + hh + 1], pn[:, :], axis=AXX),
                    wait=[mmn])
                tmx = free_pn
        p.free(pn)
        pcb = p.psum("pcb", [128, 2], F32)
        bc = []
        sk_toks = []
        for hh in range(2):
            n1 = p.v(lambda e, hh=hh: e.tensor_tensor(
                mx[:, 4 + hh:5 + hh], mx[:, hh:hh + 1], mx[:, 2 + hh:3 + hh],
                ALU.mult), wait=[tmx])
            n2 = p.a(lambda e, hh=hh: e.activation(
                mx[:, 4 + hh:5 + hh], mx[:, 4 + hh:5 + hh], AF.Sqrt), wait=[n1])
            ng = p.v(lambda e, hh=hh: e.tensor_scalar(
                mx[:, 4 + hh:5 + hh], mx[:, 4 + hh:5 + hh], -CSC, None,
                ALU.mult), wait=[n2])
            mmb = p.t(lambda e, hh=hh: e.matmul(
                pcb[:, hh:hh + 1], self.ones_row_f[:, :], mx[:, 4 + hh:5 + hh],
                start=True, stop=True), wait=[ng] + (bc[-1:] if bc else []))
            bc.append(p.v(lambda e, hh=hh: e.tensor_copy(chb[:, hh:hh + 1],
                                                         pcb[:, hh:hh + 1]), wait=[mmb]))
            if L1:
                sk_toks.append(p.a(lambda e, hh=hh, jp=jp: e.activation(
                    ctx["sinkx"][:, hh:hh + 1],
                    ctx["sink_sb"][:, 2 * jp + hh:2 * jp + hh + 1], AF.Exp,
                    bias=mx[:, 4 + hh:5 + hh]), wait=[ng]))
        p.free(pcb)
        stage = getattr(self, "attn_stage", None)
        nhh = 0 if stage == "pairs" else (1 if stage == "heads1" else 2)
        cs_toks = [None, None]
        if not L1 and nhh:
            ptv = p.psum("ptv", [64, 512], BF16)
            ev_prev = None
            for hh in range(nhh):
                h = 2 * jp + hh
                vTh = ctx["vTh"][hh]
                for g4 in range(4):
                    last = None
                    for i4 in range(4):
                        kt = g4 * 4 + i4
                        last = p.t(lambda e, kt=kt, i4=i4, h=h: e.transpose(
                            ptv[:, i4 * 128:(i4 + 1) * 128],
                            ctx["vt"][:, kt * H * VA + h * VA:kt * H * VA + h * VA + 64],
                            self.ident[:, :]),
                            wait=[ctx["vready"], bc[-1], prev_tail] +
                            ([ev_prev] if ev_prev and i4 == 0 else []))
                    ev_prev = p.v(lambda e, g4=g4, vTh=vTh: e.tensor_copy(
                        vTh[:, g4 * 512:(g4 + 1) * 512], ptv[:, :]), wait=[last])
                cs_toks[hh] = p.v(lambda e, hh=hh: e.tensor_tensor_scan(
                    ctx["csh"][hh][:, :], ctx["vTh"][hh][:, :], ctx["vTh"][hh][:, :],
                    ctx["zero64"][:, :], ALU.add, ALU.bypass), wait=[ev_prev])
            ctx["cs_gate"] = cs_toks[nhh - 1]
            p.free(ptv)
        hps = []
        for hh in range(nhh):
            hps.append(([p.psum(f"st{hh}A", [128, 512], F32),
                         p.psum(f"st{hh}B", [128, 512], F32)],
                        p.psum(f"pv{hh}", [VA, 512], F32),
                        p.psum(f"pab{hh}", [64, 512], F32)))
        tails = []
        if nhh == 1:
            h = 2 * jp
            if not L1:
                tails.append(self.head_l0(h, 0, ctx, bc[-1], prev_tail, cs_toks[0],
                                          hps[0]))
            else:
                tails.append(self.head_l1(h, 0, ctx, bc[-1], prev_tail, sk_toks[0],
                                          hps[0]))
        elif nhh == 2:
            if not L1:
                tails = self.twin_heads(0, jp, ctx, bc[-1], prev_tail, cs_toks,
                                        None, hps)
            else:
                tails = self.twin_heads(1, jp, ctx, bc[-1], prev_tail, None,
                                        sk_toks, hps)
        for (stp, pv, pab) in reversed(hps):
            p.free(pab, pv, stp[1], stp[0])
        if nhh == 0:
            return bc[-1]
        return tails[-1] if len(tails) == 1 else self._join(tails)

    def _join(self, toks):
        sems = {t[0] for t in toks}
        if len(sems) == 1:
            return max(toks, key=lambda t: t[1])
        return self.p.v(lambda e: e.memset(self._joinbuf[:, :], 0.0), wait=toks)

    def qk_ln_rope(self, buf, tin, wsb, bsb, ctx, prev_ln):
        p = self.p
        sq16, lntA, lntB = ctx["sq16"], ctx["lntA"], ctx["lntB"]
        blk2b, blk2T, cosb, sinb, shufb = (ctx["blk2b"], ctx["blk2T"], ctx["cosb"],
                                           ctx["sinb"], ctx["shufb"])
        mu = lntA[0:2, 0:T]
        vr = lntB[0:2, 0:T]
        m2 = shufb[0:2, 0:T]
        ps = p.psum("l1s", [2, T], F32)
        px = p.psum("l1x", [2, T], F32)
        mm1 = mm2 = None
        for c in range(4):
            mm1 = p.t(lambda e, c=c: e.matmul(ps[:, c * 512:(c + 1) * 512], blk2b[:, :],
                                              buf[:, c * 512:(c + 1) * 512],
                                              start=True, stop=True),
                      wait=[tin, prev_ln])
        tsq = p.a(lambda e: e.activation(sq16[:, :], buf[:, :], AF.Square),
                  wait=[tin, prev_ln])
        for c in range(4):
            mm2 = p.t(lambda e, c=c: e.matmul(px[:, c * 512:(c + 1) * 512], blk2b[:, :],
                                              sq16[:, c * 512:(c + 1) * 512],
                                              start=True, stop=True), wait=[tsq])
        t1 = p.v(lambda e: e.tensor_scalar(mu, ps[:, :], 1.0 / 64.0, None,
                                           ALU.mult), wait=[mm1])
        t2 = p.v(lambda e: e.tensor_scalar(vr, px[:, :], 1.0 / 64.0, None,
                                           ALU.mult), wait=[mm2])
        t3 = p.v(lambda e: e.tensor_tensor(m2, mu, mu, ALU.mult), wait=[t1])
        t4 = p.v(lambda e: e.tensor_tensor(vr, vr, m2, ALU.subtract), wait=[t2, t3])
        t5 = p.a(lambda e: e.activation(vr, vr, AF.Sqrt,
                                        bias=self.epsb[0:2, 0:1]), wait=[t4])
        t6 = p.v(lambda e: e.reciprocal(vr, vr), wait=[t5])
        p.free(px, ps)
        pmu = p.psum("pmu", [128, T], F32)
        prs = p.psum("prs", [128, T], F32)
        bmu = brs = None
        for c in range(4):
            bmu = p.t(lambda e, c=c: e.matmul(pmu[:, c * 512:(c + 1) * 512], blk2T[:, :],
                                              mu[:, c * 512:(c + 1) * 512],
                                              start=True, stop=True), wait=[t1, t6])
            brs = p.t(lambda e, c=c: e.matmul(prs[:, c * 512:(c + 1) * 512], blk2T[:, :],
                                              vr[:, c * 512:(c + 1) * 512],
                                              start=True, stop=True), wait=[t6])
        u1 = p.v(lambda e: e.tensor_tensor(lntA[:, :], buf[:, :], pmu[:, :],
                                           ALU.subtract), wait=[bmu, tin])
        u2 = p.v(lambda e: e.tensor_tensor(lntB[:, :], lntA[:, :], prs[:, :], ALU.mult),
                 wait=[u1, brs])
        ua = p.v(lambda e: e.tensor_scalar(buf[:, :], lntB[:, :], wsb[:, :],
                                           bsb[:, :], ALU.mult, ALU.add), wait=[u2])
        p.free(prs, pmu)
        sh = None
        for blk in range(4):
            src = blk + 1 if blk % 2 == 0 else blk - 1
            sh = self.p.s(lambda e, blk=blk, src=src: e.dma_start(
                out=shufb[blk * 32:(blk + 1) * 32, :],
                in_=buf[src * 32:(src + 1) * 32, :]), wait=[ua])
        r1 = p.v(lambda e: e.tensor_tensor(lntA[:, :], buf[:, :], cosb[:, :], ALU.mult),
                 wait=[ua])
        r2 = p.v(lambda e: e.tensor_tensor(lntB[:, :], shufb[:, :], sinb[:, :], ALU.mult),
                 wait=[sh])
        r3 = p.v(lambda e: e.tensor_tensor(buf[:, :], lntA[:, :], lntB[:, :], ALU.add),
                 wait=[r1, r2])
        return r3

    def head_l0(self, h, hh, ctx, tok_ch, head_prev, t_cs_in, hp):
        p = self.p
        base = 64 * hh
        qb, kb, vt = ctx["qb"], ctx["kb"], ctx["vt"]
        pT, invr, ev64, t1b = (ctx["pT"][hh], ctx["invr"][hh], ctx["ev64"][hh],
                               ctx["t1b"][hh])
        chb = ctx["chb"]
        vTh, csh, zb = ctx["vTh"][hh], ctx["csh"][hh], ctx["zb"][hh]
        invidx = ctx["invidx"]
        hstage = getattr(self, "head_stage", None)
        if hstage == "a":
            return t_cs_in
        nqc = 1 if hstage in ("b", "c") else 4
        (stp, pv, pab) = hp
        exp_tok = [None, None]
        pv_slot = [None, None]
        pv_free = head_prev
        pab_free = head_prev
        comb_tok = None
        tail = None
        kglob = 0
        for qc in range(nqc):
            pv_last = None
            nkt = 4 * qc + 4
            for kt in range(nkt):
                r = kt - 4 * qc
                lo = max(0, r * 128)
                sl = kglob % 2
                kglob += 1
                st = stp[sl]
                mm = p.t(lambda e, kt=kt, qc=qc, lo=lo, st=st: e.matmul(
                    st[:, lo:512], kb[base:base + 64, kt * 128:(kt + 1) * 128],
                    qb[base:base + 64, qc * 512 + lo:(qc + 1) * 512],
                    start=True, stop=True),
                    wait=[tok_ch] + ([exp_tok[sl]] if exp_tok[sl] else
                                     [head_prev, t_cs_in, ctx.get("cs_gate")]))
                if r >= 0:
                    mm = p.v(lambda e, st=st, lo=lo: e.tensor_tensor(
                        st[:, lo:lo + 128], st[:, lo:lo + 128], self.m_kgtq[:, :],
                        ALU.add), wait=[mm])
                exp_tok[sl] = p.a(lambda e, st=st, lo=lo, sl=sl: e.activation(
                    pT[:, sl * 512 + lo:(sl + 1) * 512], st[:, lo:512], AF.Exp,
                    bias=chb[:, hh:hh + 1], scale=self.c0125[:, :]),
                    wait=[mm] + ([pv_slot[sl]] if pv_slot[sl] else []))
                pv_last = p.t(lambda e, kt=kt, h=h, lo=lo, sl=sl, nkt=nkt: e.matmul(
                    pv[:, lo:512],
                    vt[:, kt * H * VA + h * VA:kt * H * VA + (h + 1) * VA],
                    pT[:, sl * 512 + lo:(sl + 1) * 512],
                    start=(kt == 0), stop=(kt == nkt - 1)),
                    wait=[exp_tok[sl], ctx["vready"]] +
                    ([pv_free] if pv_free and kt == 0 else []))
                pv_slot[sl] = pv_last
            if hstage == "b":
                tail = exp_tok[0]
                break
            q0 = qc * 512
            ti = p.v(lambda e: e.reciprocal(invr[:, :], pv[64:65, :]),
                     wait=[pv_last] + ([comb_tok] if comb_tok else []))
            ta = p.v(lambda e, h=h: e.tensor_scalar(
                invr[:, :], invr[:, :], ctx["resid64"][0:1, h:h + 1], None, ALU.mult),
                wait=[ti])
            tb = p.t(lambda e: e.matmul(pab[:, :], self.ones_row_f[:, 0:64], invr[:, :],
                                        start=True, stop=True),
                     wait=[ta] + ([pab_free] if pab_free else []))
            te = p.v(lambda e: e.tensor_copy(ev64[:, :], pv[0:64, :]), wait=[ti])
            pv_free = te
            t1 = p.v(lambda e: e.tensor_tensor(t1b[:, :], ev64[:, :], pab[:, :],
                                               ALU.mult), wait=[te, tb])
            pab_free = t1
            tz = p.v(lambda e, q0=q0: e.tensor_tensor(
                zb[:, :], csh[:, q0:q0 + 512], invidx[:, q0:q0 + 512],
                ALU.mult), wait=[t_cs_in] + ([comb_tok] if comb_tok else []))
            tu = p.v(lambda e, h=h: e.scalar_tensor_tensor(
                zb[:, :], zb[:, :], ctx["negc64"][:, h:h + 1],
                t1b[:, :], ALU.mult, ALU.add), wait=[tz, t1])
            comb_tok = p.v(lambda e, q0=q0, h=h: e.scalar_tensor_tensor(
                ctx["aoT"][base:base + 64, (h // 2) * T + q0:(h // 2) * T + q0 + 512],
                vTh[:, q0:q0 + 512], ctx["skip64"][:, h:h + 1], zb[:, :],
                ALU.mult, ALU.add), wait=[tu])
            tail = comb_tok
        return tail

    def head_l1(self, h, hh, ctx, tok_ch, head_prev, tok_sink, hp):
        p = self.p
        base = 64 * hh
        qb, kb, vt = ctx["qb"], ctx["kb"], ctx["vt"]
        pT, invr, ev64 = ctx["pT"][hh], ctx["invr"][hh], ctx["ev64"][hh]
        chb, sinkx = ctx["chb"], ctx["sinkx"]
        (stp, pv, pab) = hp
        exp_tok = [head_prev, head_prev]
        pv_slot = [None, None]
        pv_free = head_prev
        pab_free = head_prev
        comb_tok = None
        tail = None
        kglob = 0
        for qc in range(4):
            pv_last = None
            kts = list(range(max(0, 4 * qc - 2), 4 * qc + 4))
            for kt in kts:
                r = kt - 4 * qc
                lo = max(0, kt * 128 - qc * 512)
                hi = min(512, kt * 128 + 384 - qc * 512)
                sl = kglob % 2
                kglob += 1
                st = stp[sl]
                mm = p.t(lambda e, kt=kt, qc=qc, lo=lo, hi=hi, st=st: e.matmul(
                    st[:, lo:hi], kb[base:base + 64, kt * 128:(kt + 1) * 128],
                    qb[base:base + 64, qc * 512 + lo:qc * 512 + hi],
                    start=True, stop=True),
                    wait=[tok_ch] + ([exp_tok[sl]] if exp_tok[sl] else []))
                if r >= 0:
                    mm = p.v(lambda e, st=st, r=r: e.tensor_tensor(
                        st[:, r * 128:(r + 1) * 128], st[:, r * 128:(r + 1) * 128],
                        self.m_kgtq[:, :], ALU.add), wait=[mm])
                kcol = (r + 2) * 128
                if kcol < hi:
                    mm = p.v(lambda e, st=st, kcol=kcol: e.tensor_tensor(
                        st[:, kcol:kcol + 128], st[:, kcol:kcol + 128],
                        self.m_kleq[:, :], ALU.add), wait=[mm])
                first = (kt == kts[0])
                if first:
                    tz0 = p.v(lambda e, sl=sl: e.memset(
                        pT[:, sl * 512:(sl + 1) * 512], 0.0),
                        wait=[pv_slot[sl]] if pv_slot[sl] else [head_prev])
                else:
                    tz0 = None
                exp_tok[sl] = p.a(lambda e, st=st, lo=lo, hi=hi, sl=sl: e.activation(
                    pT[:, sl * 512 + lo:sl * 512 + hi], st[:, lo:hi], AF.Exp,
                    bias=chb[:, hh:hh + 1], scale=self.c0125[:, :]),
                    wait=[mm, tz0] + ([pv_slot[sl]] if pv_slot[sl] else []))
                mlo, mhi = (0, 512) if first else (lo, hi)
                pv_last = p.t(lambda e, kt=kt, h=h, mlo=mlo, mhi=mhi, sl=sl,
                              first=first, last_=(kt == kts[-1]): e.matmul(
                    pv[:, mlo:mhi],
                    vt[:, kt * H * VA + h * VA:kt * H * VA + (h + 1) * VA],
                    pT[:, sl * 512 + mlo:sl * 512 + mhi],
                    start=first, stop=last_),
                    wait=[exp_tok[sl], ctx["vready"]] +
                    ([pv_free] if pv_free and kt == kts[0] else []))
                pv_slot[sl] = pv_last
            q0 = qc * 512
            tl_ = p.a(lambda e, hh=hh: e.activation(
                invr[:, :], pv[64:65, :], AF.Ln, bias=sinkx[:, hh:hh + 1]),
                wait=[pv_last, tok_sink] + ([comb_tok] if comb_tok else []))
            ti = p.a(lambda e: e.activation(
                invr[:, :], invr[:, :], AF.Exp, scale=ctx["neg1"][:, :]),
                wait=[tl_])
            tb = p.t(lambda e: e.matmul(pab[:, :], self.ones_row_f[:, 0:64], invr[:, :],
                                        start=True, stop=True),
                     wait=[ti] + ([pab_free] if pab_free else []))
            te = p.v(lambda e: e.tensor_copy(ev64[:, :], pv[0:64, :]), wait=[ti])
            pv_free = te
            comb_tok = p.v(lambda e, q0=q0, h=h: e.tensor_tensor(
                ctx["aoT"][base:base + 64, (h // 2) * T + q0:(h // 2) * T + q0 + 512],
                ev64[:, :], pab[:, :], ALU.mult), wait=[te, tb])
            pab_free = comb_tok
            tail = comb_tok
        return tail


    def twin_heads(self, layer, jp, ctx, tok_ch, head_prev, cs_toks, sk_toks, hps):
        p = self.p
        L1 = (layer == 1)
        qb, kb, vt = ctx["qb"], ctx["kb"], ctx["vt"]
        chb = ctx["chb"]
        st_of = [hps[0][0], hps[1][0]]
        pv_of = [hps[0][1], hps[1][1]]
        pab_of = [hps[0][2], hps[1][2]]
        exp_tok = [[None, None], [None, None]]
        pv_slot = [[None, None], [None, None]]
        pv_free = [head_prev, head_prev]
        pab_free = [head_prev, head_prev]
        comb_tok = [None, None]
        tails = [None, None]
        kglob = [0, 0]
        if L1:
            kt_lists = [list(range(max(0, 4 * qc - 2), 4 * qc + 4)) for qc in range(4)]
        else:
            kt_lists = [list(range(4 * qc + 4)) for qc in range(4)]
        for qc in range(4):
            kts = kt_lists[qc]
            pv_last = [None, None]
            for kt in kts:
                for hh in range(2):
                    h = 2 * jp + hh
                    base = 64 * hh
                    stp = st_of[hh]
                    pv = pv_of[hh]
                    pT = ctx["pT"][hh]
                    r = kt - 4 * qc
                    sl = kglob[hh] % 2
                    kglob[hh] += 1
                    st = stp[sl]
                    if L1:
                        lo = max(0, kt * 128 - qc * 512)
                        hi = min(512, kt * 128 + 384 - qc * 512)
                    else:
                        lo = max(0, r * 128)
                        hi = 512
                    mm = p.t(lambda e, kt=kt, qc=qc, lo=lo, hi=hi, st=st, base=base:
                             e.matmul(
                        st[:, lo:hi], kb[base:base + 64, kt * 128:(kt + 1) * 128],
                        qb[base:base + 64, qc * 512 + lo:qc * 512 + hi],
                        start=True, stop=True),
                        wait=[tok_ch] + ([exp_tok[hh][sl]] if exp_tok[hh][sl] else
                                         [head_prev, ctx.get("cs_gate")]))
                    if r >= 0:
                        mm = p.v(lambda e, st=st, r=r: e.tensor_tensor(
                            st[:, r * 128:(r + 1) * 128], st[:, r * 128:(r + 1) * 128],
                            self.m_kgtq[:, :], ALU.add), wait=[mm])
                    if L1:
                        kcol = (r + 2) * 128
                        if kcol < hi:
                            mm = p.v(lambda e, st=st, kcol=kcol: e.tensor_tensor(
                                st[:, kcol:kcol + 128], st[:, kcol:kcol + 128],
                                self.m_kleq[:, :], ALU.add), wait=[mm])
                    first = (kt == kts[0])
                    tz0 = None
                    if L1 and first:
                        tz0 = p.v(lambda e, sl=sl, pT=pT: e.memset(
                            pT[:, sl * 512:(sl + 1) * 512], 0.0),
                            wait=[pv_slot[hh][sl]] if pv_slot[hh][sl] else [head_prev])
                    exp_tok[hh][sl] = p.a(lambda e, st=st, lo=lo, hi=hi, sl=sl, pT=pT,
                                          hh=hh: e.activation(
                        pT[:, sl * 512 + lo:sl * 512 + hi], st[:, lo:hi], AF.Exp,
                        bias=chb[:, hh:hh + 1], scale=self.c0125[:, :]),
                        wait=[mm, tz0] + ([pv_slot[hh][sl]] if pv_slot[hh][sl] else []))
                    if L1 and first:
                        mlo, mhi = 0, 512
                    else:
                        mlo, mhi = lo, hi
                    pv_last[hh] = p.t(lambda e, kt=kt, h=h, mlo=mlo, mhi=mhi, sl=sl,
                                      pv=pv, pT=pT, first=first,
                                      last_=(kt == kts[-1]): e.matmul(
                        pv[:, mlo:mhi],
                        vt[:, kt * H * VA + h * VA:kt * H * VA + (h + 1) * VA],
                        pT[:, sl * 512 + mlo:sl * 512 + mhi],
                        start=first, stop=last_),
                        wait=[exp_tok[hh][sl], ctx["vready"]] +
                        ([pv_free[hh]] if pv_free[hh] and first else []))
                    pv_slot[hh][sl] = pv_last[hh]
            q0 = qc * 512
            for hh in range(2):
                h = 2 * jp + hh
                base = 64 * hh
                pv = pv_of[hh]
                pab = pab_of[hh]
                invr, ev64, t1b = (ctx["invr"][hh], ctx["ev64"][hh],
                                   ctx["t1b"][hh] if not L1 else None)
                if not L1:
                    vTh, csh, zb = ctx["vTh"][hh], ctx["csh"][hh], ctx["zb"][hh]
                    tl_ = p.a(lambda e, pv=pv, invr=invr: e.activation(
                        invr[:, :], pv[64:65, :], AF.Ln),
                        wait=[pv_last[hh]] + ([comb_tok[hh]] if comb_tok[hh] else []))
                    ti = p.a(lambda e, invr=invr: e.activation(
                        invr[:, :], invr[:, :], AF.Exp, scale=ctx["neg1"][:, :]),
                        wait=[tl_])
                    ta = p.v(lambda e, h=h, invr=invr: e.tensor_scalar(
                        invr[:, :], invr[:, :], ctx["resid64"][0:1, h:h + 1], None,
                        ALU.mult), wait=[ti])
                    tb = p.t(lambda e, pab=pab, invr=invr: e.matmul(
                        pab[:, :], self.ones_row_f[:, 0:64], invr[:, :],
                        start=True, stop=True),
                        wait=[ta] + ([pab_free[hh]] if pab_free[hh] else []))
                    te = p.v(lambda e, pv=pv, ev64=ev64: e.tensor_copy(
                        ev64[:, :], pv[0:64, :]), wait=[ti])
                    pv_free[hh] = te
                    t1 = p.v(lambda e, t1b=t1b, ev64=ev64, pab=pab: e.tensor_tensor(
                        t1b[:, :], ev64[:, :], pab[:, :], ALU.mult), wait=[te, tb])
                    pab_free[hh] = t1
                    tz = p.v(lambda e, q0=q0, zb=zb, csh=csh: e.tensor_tensor(
                        zb[:, :], csh[:, q0:q0 + 512], ctx["invidx"][:, q0:q0 + 512],
                        ALU.mult),
                        wait=[cs_toks[hh]] + ([comb_tok[hh]] if comb_tok[hh] else []))
                    tu = p.v(lambda e, h=h, zb=zb, t1b=t1b: e.scalar_tensor_tensor(
                        zb[:, :], zb[:, :], ctx["negc64"][:, h:h + 1], t1b[:, :],
                        ALU.mult, ALU.add), wait=[tz, t1])
                    comb_tok[hh] = p.v(lambda e, q0=q0, h=h, base=base, zb=zb,
                                       vTh=vTh: e.scalar_tensor_tensor(
                        ctx["aoT"][base:base + 64,
                                   (h // 2) * T + q0:(h // 2) * T + q0 + 512],
                        vTh[:, q0:q0 + 512], ctx["skip64"][:, h:h + 1], zb[:, :],
                        ALU.mult, ALU.add), wait=[tu])
                else:
                    tr = p.v(lambda e, pv=pv, hh=hh, invr=invr: e.tensor_scalar(
                        invr[:, :], pv[64:65, :], ctx["sinkx"][:, hh:hh + 1], None,
                        ALU.add),
                        wait=[pv_last[hh], sk_toks[hh]] +
                        ([comb_tok[hh]] if comb_tok[hh] else []))
                    ti = p.v(lambda e, invr=invr: e.reciprocal(invr[:, :], invr[:, :]),
                             wait=[tr])
                    tb = p.t(lambda e, pab=pab, invr=invr: e.matmul(
                        pab[:, :], self.ones_row_f[:, 0:64], invr[:, :],
                        start=True, stop=True),
                        wait=[ti] + ([pab_free[hh]] if pab_free[hh] else []))
                    te = p.v(lambda e, pv=pv, ev64=ev64: e.tensor_copy(
                        ev64[:, :], pv[0:64, :]), wait=[tr])
                    pv_free[hh] = te
                    comb_tok[hh] = p.v(lambda e, q0=q0, h=h, base=base, ev64=ev64,
                                       pab=pab: e.tensor_tensor(
                        ctx["aoT"][base:base + 64,
                                   (h // 2) * T + q0:(h // 2) * T + q0 + 512],
                        ev64[:, :], pab[:, :], ALU.mult), wait=[te, tb])
                    pab_free[hh] = comb_tok[hh]
                tails[hh] = comb_tok[hh]
        return tails

    def proj_pass(self, layer, aoT, acc_ap=None, gate=None):
        p = self.p
        L1 = (layer == 1)
        mark = p.push()
        cw = p.sbuf("cw", [128, ND * D], BF16)
        wsrc = (self.inp["cproj_w"] if not L1 else self.inp["proj_wT"])
        tok_w = self.dma_w(cw[:, :], wsrc, ND, wait=[gate] if gate else ())
        pbr = None
        if L1:
            pbr = p.sbuf("pbr", [128, D], F32)
            self.dma_f(pbr[:, :], self.inp["projb_rep"],
                       wait=[gate] if gate else ())
        else:
            xin = p.sbuf("xin", [128, 2 * D], F32)
            oacc = p.sbuf("oacc", [128, 2 * D], F32)
        pc = [p.psum("pc0", [128, D], F32), p.psum("pc1", [128, D], F32)]
        ev_tok = [None, None]
        out_tok = [None, None]
        if gate is None:
            p.fence()
        mmt = None
        for t in range(NT):
            sl = t % 2
            if not L1:
                tok_x = self.dma_f(xin[:, sl * D:(sl + 1) * D],
                                   self.inp["x"][t * 128:(t + 1) * 128, :],
                                   wait=[out_tok[sl]] if out_tok[sl] else ())
            for dc in range(2):
                for j in range(ND):
                    mmt = p.t(lambda e, t=t, dc=dc, j=j, sl=sl: e.matmul(
                        pc[sl][:, dc * 512:(dc + 1) * 512],
                        aoT[:, j * T + t * 128:j * T + (t + 1) * 128],
                        cw[:, j * D + dc * 512:j * D + (dc + 1) * 512],
                        start=(j == 0), stop=(j == ND - 1)),
                        wait=[tok_w] + ([ev_tok[sl]] if ev_tok[sl] and dc == 0 and j == 0 else []) +
                        ([gate] if gate and t == 0 and dc == 0 and j == 0 else []))
            if not L1:
                tev = p.v(lambda e, sl=sl: e.tensor_tensor(
                    oacc[:, sl * D:(sl + 1) * D], xin[:, sl * D:(sl + 1) * D],
                    pc[sl][:, :], ALU.add), wait=[mmt, tok_x])
                ev_tok[sl] = tev
                out_tok[sl] = self.dma_f(self.acc0_hbm[t * 128:(t + 1) * 128, :],
                                         oacc[:, sl * D:(sl + 1) * D], wait=[tev])
            else:
                ev_tok[sl] = p.v(lambda e, t=t, sl=sl: e.tensor_tensor(
                    acc_ap(t), pc[sl][:, :], pbr[:, :], ALU.add),
                    wait=[mmt])
        if gate is None:
            p.fence()
        last_ev = ev_tok[(NT - 1) % 2]
        p.pop(mark)
        return mmt, last_ev


    def moe_chain(self):
        self.w1_cons = [None, None]
        self.w2_cons = None
        self.cn_cons = None
        self.pb_free = None
        """4 sequential MoE layers on the hidden stream in B2 (bf16 tm).
        Final fp32 result written to accf_hbm. B1 half region [*, 0:8192) is
        scratch for xhatT/xgT in a 1024-token-half layout (d-tile j at cols
        j*1024 + tok_local)."""
        p = self.p
        acch = p.sbuf("acch", [128, 8 * D], F32)
        hslab = p.sbuf("hslab", [128, 16 * 1024], BF16)
        w1b = p.sbuf("w1b", [128, 2 * ND * 512], BF16)
        w2b = p.sbuf("w2b", [128, 16 * 1024], BF16)
        lgb = p.sbuf("lgb", [128, 8], F32)
        lbb = p.sbuf("lbb", [128, 8], BF16)
        b1s = p.sbuf("b1s", [128, 32], F32)
        b1pp = p.sbuf("b1pp", [128, 32], F32)
        b2s = p.sbuf("b2s", [1, 1024], BF16)
        rwb = p.sbuf("rwb", [128, ND * 4], BF16)
        rbs = p.sbuf("rbs", [128, 4], F32)
        srep = p.sbuf("srep", [128, 4], F32)
        lgt = p.sbuf("lgt", [128, 4], F32)
        p4 = p.sbuf("p4b", [128, 4], F32)
        sc1 = p.sbuf("sc1", [128, 8], F32)
        tmask = p.sbuf("tmask", [128, 4], F32)
        comb = p.sbuf("comb", [128, 8 * 4], F32)
        scb = p.sbuf("scb", [128, 8], F32)
        aa = p.sbuf("aab", [128, 1], F32)
        bb = p.sbuf("bbb", [128, 1], F32)
        bufs = (acch, hslab, w1b, w2b, lgb, lbb, b1s, b1pp, b2s, rwb, rbs, srep,
                lgt, p4, sc1, tmask, comb, scb, aa, bb)
        for l in range(4):
            tok_rw = self.dma_w(rwb[:, :], self.inp["rw"][l], ND)
            tok_rb = self.dma_f(rbs[:, :], self.inp["rb_rep"][l])
            sS = p.sbuf("sS", [1, 4], F32)
            pS = p.psum("pS", [1, 4], F32)
            mmS = None
            for d in range(ND):
                mmS = p.t(lambda e, d=d: e.matmul(
                    pS[:, :], self.ones_col_bf[:, :], rwb[:, d * 4:(d + 1) * 4],
                    start=(d == 0), stop=(d == ND - 1)), wait=[tok_rw])
            tS = p.v(lambda e: e.tensor_copy(sS[:, :], pS[:, :]), wait=[mmS])
            p.free(pS)
            pSr = p.psum("pSr", [128, 4], F32)
            mmSr = p.t(lambda e: e.matmul(pSr[:, :], self.ones_row_f[:, :], sS[:, :],
                                          start=True, stop=True), wait=[tS])
            tSr = p.v(lambda e: e.tensor_copy(srep[:, :], pSr[:, :]), wait=[mmSr])
            p.free(pSr, sS)
            for th in range(2):
                self.moe_layer_half(l, th, bufs, tok_rb, tSr)
        p.free(*bufs)
        p.fence()

    def moe_layer_half(self, l, th, bufs, tok_rb, tok_srep, spill=None):
        (acch, hslab, w1b, w2b, lgb, lbb, b1s, b1pp, b2s, rwb, rbs, srep,
         lgt, p4, sc1, tmask, comb, scb, aa, bb) = bufs
        p = self.p
        t0 = th * 8
        B1h = self.B1.rearrange("p (j tt) -> p j tt", tt=1024)
        stat = p.sbuf("mst", [128, 12], F32)
        mv = p.sbuf("mmv", [128, 2], F32)
        mus = p.sbuf("mmu", [128, 8], F32)
        stds = p.sbuf("msd", [128, 8], F32)
        rs = p.sbuf("mrs", [128, 1], F32)
        nmr = p.sbuf("mnm", [128, 1], F32)
        xh_tok = []
        stat_free = None
        for tl in range(8):
            t = t0 + tl
            p.v(lambda e, t=t: e.bn_stats(stat[:, 0:6],
                                          self.B2[:, t * D:t * D + 512]),
                wait=[stat_free] if stat_free else ())
            t_st = p.v(lambda e, t=t: e.bn_stats(stat[:, 6:12],
                                                 self.B2[:, t * D + 512:(t + 1) * D]),
                       wait=[stat_free] if stat_free else ())
            t_mv = p.v(lambda e: e.bn_aggr(mv[:, :], stat[:, :]), wait=[t_st])
            t_sd = p.a(lambda e, tl=tl: e.activation(stds[:, tl:tl + 1], mv[:, 1:2],
                                                     AF.Sqrt, bias=self.epsb[:, :]),
                       wait=[t_mv])
            t_rs = p.v(lambda e, tl=tl: e.reciprocal(rs[:, :], stds[:, tl:tl + 1]),
                       wait=[t_sd])
            p.v(lambda e, tl=tl: e.tensor_copy(mus[:, tl:tl + 1], mv[:, 0:1]),
                wait=[t_mv])
            t_nm = p.v(lambda e: e.tensor_scalar(nmr[:, :], mv[:, 0:1], rs[:, :], -1.0,
                                                 ALU.mult, ALU.mult), wait=[t_rs])
            stat_free = t_nm
            xh_tok.append(p.v(lambda e, t=t: e.tensor_scalar(
                self.B2[:, t * D:(t + 1) * D], self.B2[:, t * D:(t + 1) * D],
                rs[:, :], nmr[:, :], ALU.mult, ALU.add), wait=[t_nm]))
        ptr2 = [p.psum("mtr0", [128, 512], BF16), p.psum("mtr1", [128, 512], BF16)]
        trf2 = [None, None]
        for tl in range(8):
            for jg in range(2):
                last = None
                pslot = ptr2[jg]
                for j4 in range(4):
                    j = jg * 4 + j4
                    last = p.t(lambda e, tl=tl, j=j, j4=j4, pslot=pslot: e.transpose(
                        pslot[:, j4 * 128:(j4 + 1) * 128],
                        self.B2[:, (t0 + tl) * D + j * 128:(t0 + tl) * D + (j + 1) * 128],
                        self.ident[:, :]),
                        wait=[xh_tok[tl]] +
                        ([trf2[jg]] if trf2[jg] is not None and j4 == 0 else []))
                trf2[jg] = p.v(lambda e, tl=tl, jg=jg, pslot=pslot: e.tensor_copy(
                    B1h[:, jg * 4:(jg + 1) * 4, tl * 128:(tl + 1) * 128],
                    pslot.rearrange("p (j tt) -> p j tt", j=4)[:, :, :]), wait=[last])
        trf = self._join([trf2[0], trf2[1]])
        p.free(ptr2[1], ptr2[0])
        pA = p.psum("pA", [128, 4], F32)
        comb_toks = []
        for tl in range(8):
            mmA = None
            for d in range(ND):
                mmA = p.t(lambda e, tl=tl, d=d: e.matmul(
                    pA[:, :], self.B1[:, d * 1024 + tl * 128:d * 1024 + (tl + 1) * 128],
                    rwb[:, d * 4:(d + 1) * 4], start=(d == 0), stop=(d == ND - 1)),
                    wait=[trf] + ([comb_toks[-1]] if comb_toks else []))
            tg = p.v(lambda e, tl=tl: e.tensor_scalar(
                lgt[:, :], pA[:, :], stds[:, tl:tl + 1], None, ALU.mult), wait=[mmA])
            tg = p.v(lambda e, tl=tl: e.scalar_tensor_tensor(
                lgt[:, :], srep[:, :], mus[:, tl:tl + 1], lgt[:, :], ALU.mult, ALU.add),
                wait=[tg, tok_srep])
            tg = p.v(lambda e: e.tensor_tensor(lgt[:, :], lgt[:, :], rbs[:, :], ALU.add),
                     wait=[tg, tok_rb])
            tm1 = p.v(lambda e: e.reduce_max(sc1[:, 0:1], lgt[:, :], axis=AXX), wait=[tg])
            tnm = p.v(lambda e: e.tensor_scalar(sc1[:, 1:2], sc1[:, 0:1], -1.0, None,
                                                ALU.mult), wait=[tm1])
            tex = p.a(lambda e: e.activation(p4[:, :], lgt[:, :], AF.Exp,
                                             bias=sc1[:, 1:2]), wait=[tnm])
            tm1p = p.v(lambda e: e.reduce_max(sc1[:, 2:3], p4[:, :], axis=AXX), wait=[tex])
            tt_ = p.v(lambda e: e.tensor_scalar(tmask[:, :], p4[:, :], sc1[:, 2:3], None,
                                                ALU.is_lt), wait=[tm1p])
            tpm = p.v(lambda e: e.tensor_tensor(tmask[:, :], p4[:, :], tmask[:, :],
                                                ALU.mult), wait=[tt_])
            tm2p = p.v(lambda e: e.reduce_max(sc1[:, 3:4], tmask[:, :], axis=AXX),
                       wait=[tpm])
            tind = p.v(lambda e: e.tensor_scalar(tmask[:, :], p4[:, :], sc1[:, 3:4], None,
                                                 ALU.is_ge), wait=[tm2p])
            tden = p.v(lambda e: e.tensor_tensor(sc1[:, 4:5], sc1[:, 2:3], sc1[:, 3:4],
                                                 ALU.add), wait=[tm2p])
            tdin = p.v(lambda e: e.reciprocal(sc1[:, 5:6], sc1[:, 4:5]), wait=[tden])
            tx1 = p.v(lambda e: e.tensor_tensor(tmask[:, :], p4[:, :], tmask[:, :],
                                                ALU.mult), wait=[tind])
            tcb = p.v(lambda e, tl=tl: e.tensor_scalar(
                comb[:, tl * 4:(tl + 1) * 4], tmask[:, :], sc1[:, 5:6], None, ALU.mult),
                wait=[tx1, tdin])
            tsc = p.v(lambda e, tl=tl: e.tensor_reduce(
                scb[:, tl:tl + 1], comb[:, tl * 4:(tl + 1) * 4], axis=AXX, op=ALU.add),
                wait=[tcb])
            comb_toks.append(tsc)
        p.free(pA)
        acc_toks = []
        prev = None
        for tl in range(8):
            ta_ = p.v(lambda e, tl=tl: e.tensor_tensor(aa[:, :], scb[:, tl:tl + 1],
                                                       stds[:, tl:tl + 1], ALU.mult),
                      wait=[comb_toks[tl]] + ([prev] if prev else []))
            tb_ = p.v(lambda e, tl=tl: e.tensor_tensor(bb[:, :], scb[:, tl:tl + 1],
                                                       mus[:, tl:tl + 1], ALU.mult),
                      wait=[ta_])
            prev = p.v(lambda e, tl=tl: e.tensor_scalar(
                acch[:, tl * D:(tl + 1) * D], self.B2[:, (t0 + tl) * D:(t0 + tl + 1) * D],
                aa[:, :], bb[:, :], ALU.mult, ALU.add), wait=[tb_])
            acc_toks.append(prev)
        y_tail = None
        for e_ in range(4):
            acc_toks, y_tail = self.moe_expert(l, th, e_, bufs, xh_tok, acc_toks,
                                               y_tail, B1h)
        if spill is None:
            spill = (l == 3)
        for tl in range(8):
            if not spill:
                p.v(lambda e, tl=tl: e.tensor_copy(
                    self.B2[:, (t0 + tl) * D:(t0 + tl + 1) * D],
                    acch[:, tl * D:(tl + 1) * D]), wait=[acc_toks[tl]])
            else:
                self.dma_f(self.accf_hbm[(t0 + tl) * 128:(t0 + tl + 1) * 128, :],
                           acch[:, tl * D:(tl + 1) * D], wait=[acc_toks[tl]])
        p.free(nmr, rs, stds, mus, mv, stat)
        p.fence()

    def moe_expert(self, l, th, e_, bufs, xh_tok, acc_toks, y_tail, B1h):
        (acch, hslab, w1b, w2b, lgb, lbb, b1s, b1pp, b2s, rwb, rbs, srep,
         lgt, p4, sc1, tmask, comb, scb, aa, bb) = bufs
        p = self.p
        t0 = th * 8
        yw = [self.cn_cons] if self.cn_cons else []
        tok_lg = self.dma_f(lgb[:, :], self.inp["ln_gT"][l, e_], wait=yw, q="dcn")
        tok_lb = self.dma_c(lbb[:, :], self.inp["ln_bT"][l, e_], wait=yw, q="dcg")
        tok_b1 = self.dma_f(b1s[:, :], self.inp["b1T"][l, e_], wait=yw, q="dcn")
        tok_b2 = self.dma_c(b2s[:, :], self.inp["b2r"][l, e_], wait=yw, q="dcg")
        ptr2 = [p.psum("mtr0", [128, 512], BF16), p.psum("mtr1", [128, 512], BF16)]
        trf2 = [None, None]
        for tl in range(8):
            for jg in range(2):
                last = None
                pslot = ptr2[jg]
                for j4 in range(4):
                    j = jg * 4 + j4
                    last = p.t(lambda e, tl=tl, j=j, j4=j4, pslot=pslot: e.transpose(
                        pslot[:, j4 * 128:(j4 + 1) * 128],
                        self.B2[:, (t0 + tl) * D + j * 128:(t0 + tl) * D + (j + 1) * 128],
                        self.ident[:, :]),
                        wait=[xh_tok[tl], tok_lg] +
                        ([trf2[jg]] if trf2[jg] is not None and j4 == 0 else
                         ([acc_toks[-1], y_tail] if tl == 0 and j4 == 0 else [])))
                for j4 in range(4):
                    j = jg * 4 + j4
                    trf2[jg] = p.v(lambda e, tl=tl, j=j, j4=j4, pslot=pslot:
                                   e.tensor_scalar(
                        self.B1[:, j * 1024 + tl * 128:j * 1024 + (tl + 1) * 128],
                        pslot[:, j4 * 128:(j4 + 1) * 128], lgb[:, j:j + 1], None,
                        ALU.mult), wait=[last])
        xg_tok = self._join([trf2[0], trf2[1]])
        p.free(ptr2[1], ptr2[0])
        for fh in range(2):
            hs_toks = []
            pb = p.psum("pb", [128, 4], F32)
            php = [p.psum("phA", [128, 1024], F32), p.psum("phB", [128, 1024], F32)]
            gel_slot = [None, None]
            gel_free = None
            w1_toks = [None, None]
            w1_cons = self.w1_cons
            for fs in range(4):
                fcol = fh * 2048 + fs * 512
                sl = fs % 2
                w1_toks[sl] = self.dma_w(
                    w1b[:, sl * ND * 512:(sl + 1) * ND * 512],
                    self.inp["w1"][l, e_][:, fcol:fcol + 512], ND,
                    wait=[w1_cons[sl]] if w1_cons[sl] else (),
                    q=("dwa", "dwb")[sl])
                bias_tok = None
                if th == 0:
                    mmb = None
                    for f4 in range(4):
                        for d in range(ND):
                            mmb = p.t(lambda e, sl=sl, f4=f4, d=d: e.matmul(
                                pb[:, f4:f4 + 1],
                                w1b[:, sl * ND * 512 + d * 512 + f4 * 128:
                                    sl * ND * 512 + d * 512 + (f4 + 1) * 128],
                                lbb[:, d:d + 1], start=(d == 0), stop=(d == ND - 1)),
                                wait=[w1_toks[sl], tok_lb] +
                                ([self.pb_free] if self.pb_free and f4 == 0 and d == 0
                                 else []) +
                                ([y_tail, xg_tok] if fs == 0 and f4 == 0 and d == 0
                                 else []))
                    fg0 = fh * 16 + fs * 4
                    bias_tok = p.v(lambda e, fg0=fg0: e.tensor_tensor(
                        b1pp[:, fg0:fg0 + 4], pb[:, :], b1s[:, fg0:fg0 + 4], ALU.add),
                        wait=[mmb, tok_b1])
                    self.pb_free = bias_tok
                for f4 in range(4):
                    fglob = fh * 16 + fs * 4 + f4
                    floc = fs * 4 + f4
                    psl = floc % 2
                    ph = php[psl]
                    mmh = None
                    for c in range(2):
                        for d in range(ND):
                            mmh = p.t(lambda e, ph=ph, sl=sl, f4=f4, c=c, d=d: e.matmul(
                                ph[:, c * 512:(c + 1) * 512],
                                w1b[:, sl * ND * 512 + d * 512 + f4 * 128:
                                    sl * ND * 512 + d * 512 + (f4 + 1) * 128],
                                self.B1[:, d * 1024 + c * 512:d * 1024 + (c + 1) * 512],
                                start=(d == 0), stop=(d == ND - 1)),
                                wait=[w1_toks[sl], xg_tok] +
                                ([gel_slot[psl]] if gel_slot[psl] and c == 0 and d == 0
                                 else []) +
                                ([y_tail] if y_tail and fs == 0 and f4 == 0 and c == 0
                                  and d == 0 else []))
                    gel_free = p.a(lambda e, ph=ph, floc=floc, fglob=fglob: e.activation(
                        hslab[:, floc * 1024:(floc + 1) * 1024], ph[:, :], AF.Gelu,
                        bias=b1pp[:, fglob:fglob + 1]),
                        wait=[mmh] + ([bias_tok] if bias_tok else []) +
                        ([y_tail] if y_tail and fs == 0 and f4 == 0 else []))
                    gel_slot[psl] = gel_free
                    hs_toks.append(gel_free)
                    w1_cons[sl] = mmh
            p.free(php[1], php[0], pb)
            tok_w2 = self.dma_w(w2b[:, :],
                                self.inp["w2"][l, e_][fh * 2048:(fh + 1) * 2048, :],
                                16, wait=[self.w2_cons] if self.w2_cons else (),
                                q="dw2")
            for tw in range(2):
                pys = [p.psum(f"py{i}", [128, 1024], F32) for i in range(4)]
                for tt2 in range(4):
                    tl = tw * 4 + tt2
                    for dc in range(2):
                        mmy = None
                        if fh == 0:
                            mmy = p.t(lambda e, tt2=tt2, dc=dc: e.matmul(
                                pys[tt2][:, dc * 512:(dc + 1) * 512],
                                self.ones_row_bf[:, :], b2s[:, dc * 512:(dc + 1) * 512],
                                start=True, stop=False),
                                wait=[tok_b2, hs_toks[-1]] +
                                ([y_tail] if y_tail else []))
                        for floc in range(16):
                            first = (fh != 0) and (floc == 0)
                            mmy = p.t(lambda e, tt2=tt2, dc=dc, floc=floc, tl=tl,
                                      first=first: e.matmul(
                                pys[tt2][:, dc * 512:(dc + 1) * 512],
                                hslab[:, floc * 1024 + tl * 128:
                                      floc * 1024 + (tl + 1) * 128],
                                w2b[:, floc * 1024 + dc * 512:
                                    floc * 1024 + (dc + 1) * 512],
                                start=first, stop=(floc == 15)),
                                wait=[tok_w2, hs_toks[-1]] +
                                ([y_tail] if y_tail and first else []))
                        last_y = mmy
                        self.w2_cons = mmy
                    tl_tok = p.v(lambda e, tt2=tt2, tl=tl, e_=e_: e.scalar_tensor_tensor(
                        acch[:, tl * D:(tl + 1) * D], pys[tt2][:, :],
                        comb[:, tl * 4 + e_:tl * 4 + e_ + 1],
                        acch[:, tl * D:(tl + 1) * D], ALU.mult, ALU.add),
                        wait=[last_y, acc_toks[tl]])
                    acc_toks[tl] = tl_tok
                    y_tail = tl_tok
                p.free(*pys)
        self.cn_cons = y_tail
        return acc_toks, y_tail

    def routing(self):
        p = self.p
        oh0 = p.sbuf("oh0", [128, NT * 8], BF16)
        oh1 = p.sbuf("oh1", [128, NT * 8], BF16)
        w01 = p.sbuf("w01", [128, 2 * NT], F32)
        mark = p.push()
        gate_bf = p.sbuf("gateb", [128, ND * 8], BF16)
        tok_gw = self.dma_w(gate_bf[:, :], self.inp["gate_wT"], ND)
        p8 = p.sbuf("p8b", [128, 8], F32)
        sc = p.sbuf("rsc", [128, 8], F32)
        zacc = p.sbuf("zacc", [128, 1], F32)
        lgt8 = p.sbuf("lgt8", [128, 8], F32)
        pm8 = p.sbuf("pm8", [128, 8], F32)
        cms = p.sbuf("cms", [1, 16], F32)
        pL = p.psum("pL", [128, 8], F32)
        oh_toks = []
        prevA = None
        for tl in range(NT):
            mmL = None
            for d in range(ND):
                mmL = p.t(lambda e, tl=tl, d=d: e.matmul(
                    pL[:, :], self.B1[:, d * T + tl * 128:d * T + (tl + 1) * 128],
                    gate_bf[:, d * 8:(d + 1) * 8], start=(d == 0), stop=(d == ND - 1)),
                    wait=[tok_gw] + ([prevA] if prevA else []))
            tm1 = p.v(lambda e: e.reduce_max(sc[:, 0:1], pL[:, :], axis=AXX), wait=[mmL])
            tg = p.v(lambda e: e.tensor_copy(lgt8[:, :], pL[:, :]), wait=[mmL])
            prevA = tg
            tnm = p.v(lambda e: e.tensor_scalar(sc[:, 1:2], sc[:, 0:1], -1.0, None,
                                                ALU.mult), wait=[tm1])
            tex = p.a(lambda e: e.activation(p8[:, :], lgt8[:, :], AF.Exp,
                                             bias=sc[:, 1:2], accum_out=zacc[:, :]),
                      wait=[tnm, tg])
            tzi = p.v(lambda e: e.reciprocal(sc[:, 2:3], zacc[:, :]), wait=[tex])
            tpn = p.v(lambda e: e.tensor_scalar(p8[:, :], p8[:, :], sc[:, 2:3], None,
                                                ALU.mult), wait=[tzi])
            tm1p = p.v(lambda e: e.reduce_max(sc[:, 3:4], p8[:, :], axis=AXX), wait=[tpn])
            tlt = p.v(lambda e: e.tensor_scalar(
                pm8[:, :], p8[:, :], sc[:, 3:4], None, ALU.is_lt), wait=[tm1p])
            tmm = p.v(lambda e: e.tensor_tensor(
                pm8[:, :], p8[:, :], pm8[:, :], ALU.mult), wait=[tlt])
            tm2p = p.v(lambda e: e.reduce_max(
                sc[:, 4:5], pm8[:, :], axis=AXX), wait=[tmm])
            t_i0 = p.v(lambda e, tl=tl: e.tensor_scalar(
                oh0[:, tl * 8:(tl + 1) * 8], p8[:, :], sc[:, 3:4], None, ALU.is_ge),
                wait=[tm1p])
            t_ig = p.v(lambda e, tl=tl: e.tensor_scalar(
                oh1[:, tl * 8:(tl + 1) * 8], p8[:, :], sc[:, 4:5], None, ALU.is_ge),
                wait=[tm2p])
            t_i1 = p.v(lambda e, tl=tl: e.tensor_tensor(
                oh1[:, tl * 8:(tl + 1) * 8], oh1[:, tl * 8:(tl + 1) * 8],
                oh0[:, tl * 8:(tl + 1) * 8], ALU.subtract), wait=[t_ig, t_i0])
            tdn = p.v(lambda e: e.tensor_tensor(sc[:, 5:6], sc[:, 3:4], sc[:, 4:5],
                                                ALU.add), wait=[tm2p])
            tdn = p.v(lambda e: e.tensor_scalar(sc[:, 5:6], sc[:, 5:6], 1e-9, None,
                                                ALU.add), wait=[tdn])
            tdi = p.v(lambda e: e.reciprocal(sc[:, 6:7], sc[:, 5:6]), wait=[tdn])
            p.v(lambda e, tl=tl: e.tensor_tensor(w01[:, tl:tl + 1], sc[:, 3:4],
                                                 sc[:, 6:7], ALU.mult), wait=[tdi])
            p.v(lambda e, tl=tl: e.tensor_tensor(w01[:, NT + tl:NT + tl + 1], sc[:, 4:5],
                                                 sc[:, 6:7], ALU.mult), wait=[tdi])
            oh_toks.append(t_i1)
        pcm = p.psum("pcm", [1, 16], F32)
        mcm = None
        for k, ohsb in ((0, oh0), (1, oh1)):
            for tl in range(NT):
                mcm = p.t(lambda e, k=k, tl=tl, ohsb=ohsb: e.matmul(
                    pcm[:, k * 8:(k + 1) * 8], self.ones_col_bf[:, :],
                    ohsb[:, tl * 8:(tl + 1) * 8],
                    start=(tl == 0), stop=(tl == NT - 1)),
                    wait=[oh_toks[tl]] + ([mcm] if mcm else []))
        tcm = p.v(lambda e: e.tensor_copy(cms[:, :], pcm[:, :]), wait=[mcm])
        p.free(pcm)
        if self.mode == "fused":
            cm_tok = self.dma_f(self.cc_in, cms[:, :], wait=[tcm])
        elif self.mode == "counts":
            cm_tok = self.dma_f(self.out_counts, cms[:, :], wait=[tcm])
        else:
            cm_tok = tcm
        p.fence()
        p.pop(mark)
        return oh0, oh1, w01, cm_tok

    def offsets_and_comb(self, comb, oh0, oh1, w01, cm_tok):
        p = self.p
        mark = p.push()
        ag = p.sbuf("agb", [8, 16], F32)
        if self.mode == "fused":
            p.g(lambda e: e.collective_compute(
                "AllGather", ALU.bypass, replica_groups=[list(range(8))],
                ins=[self.cc_in], outs=[self.cc_out]), wait=[cm_tok], inc="cc", n=1)
            tag = self.dma_f(ag[:, :], self.cc_out, wait=[("cc", 1)])
        else:
            tag = self.dma_f(ag[:, :], self.inp["allcounts"], wait=[cm_tok])
        sel = p.sbuf("selb", [8, 1], F32)
        tsel = self.dma_f(sel[:, :], self.inp["selmask"])
        po = p.psum("po", [1, 16], F32)
        mo = p.t(lambda e: e.matmul(po[:, :], sel[:, :], ag[:, :], start=True, stop=True),
                 wait=[tag, tsel])
        offs = p.sbuf("offs", [1, 16], F32)
        to = p.v(lambda e: e.tensor_scalar(offs[:, :], po[:, :], -1.0, None, ALU.add),
                 wait=[mo])
        p.free(po)
        por = p.psum("por", [128, 16], F32)
        mor = p.t(lambda e: e.matmul(por[:, :], self.ones_row_f[:, :], offs[:, :],
                                     start=True, stop=True), wait=[to])
        offr = p.sbuf("offr", [128, 16], F32)
        tor = p.v(lambda e: e.tensor_copy(offr[:, :], por[:, :]), wait=[mor])
        p.free(por)
        cnt0 = p.sbuf("cnt0", [8, T], F32)
        cnt1 = p.sbuf("cnt1", [8, T], F32)
        ohT = p.sbuf("ohT", [8, 2 * T], F32)
        zero8 = p.sbuf("zero8", [8, 1], F32)
        ohb = p.sbuf("ohb", [128, 8], BF16)
        p.v(lambda e: e.memset(zero8[:, :], 0.0))
        ptb = p.psum("ptb", [8, 128], BF16)
        tr_free = None
        for tl in range(NT):
            for k, ohsb in ((0, oh0), (1, oh1)):
                tc = p.v(lambda e, tl=tl, ohsb=ohsb: e.tensor_copy(
                    ohb[:, :], ohsb[:, tl * 8:(tl + 1) * 8]),
                    wait=[tr_free] if tr_free else ())
                tt_ = p.t(lambda e: e.transpose(ptb[:, :], ohb[:, :], self.ident[:, :]),
                          wait=[tc])
                tr_free = p.v(lambda e, k=k, tl=tl: e.tensor_copy(
                    ohT[:, k * T + tl * 128:k * T + (tl + 1) * 128], ptb[:, :]),
                    wait=[tt_])
        ts0 = p.v(lambda e: e.tensor_tensor_scan(cnt0[:, :], ohT[:, 0:T], ohT[:, 0:T],
                                                 zero8[:, :], ALU.add, ALU.bypass),
                  wait=[tr_free])
        ts1 = p.v(lambda e: e.tensor_tensor_scan(cnt1[:, :], ohT[:, T:2 * T],
                                                 ohT[:, T:2 * T], zero8[:, :],
                                                 ALU.add, ALU.bypass), wait=[ts0])
        pct = p.psum("pct", [128, 16], F32)
        cadj = p.sbuf("cadj", [128, 16], F32)
        r8 = p.sbuf("r8b", [128, 8], F32)
        rk = p.sbuf("rkb", [128, 4], F32)
        prev = ts1
        for tl in range(NT):
            mt = []
            for k, cnt in ((0, cnt0), (1, cnt1)):
                mt.append(p.t(lambda e, tl=tl, k=k, cnt=cnt: e.matmul(
                    pct[:, k * 8:(k + 1) * 8], cnt[:, tl * 128:(tl + 1) * 128],
                    self.ident_f[0:8, 0:8], start=True, stop=True),
                    wait=[prev] if prev else ()))
            tadj = p.v(lambda e: e.tensor_tensor(cadj[:, :], pct[:, :], offr[:, :],
                                                 ALU.add), wait=[mt[0], mt[1], tor])
            prev = tadj
            rk_t = []
            for k, ohsb in ((0, oh0), (1, oh1)):
                tm_ = p.v(lambda e, tl=tl, k=k, ohsb=ohsb: e.tensor_tensor(
                    r8[:, :], ohsb[:, tl * 8:(tl + 1) * 8], cadj[:, k * 8:(k + 1) * 8],
                    ALU.mult), wait=[tadj])
                trr = p.v(lambda e, k=k: e.reduce_sum(rk[:, k:k + 1], r8[:, :],
                                                      axis=AXX), wait=[tm_])
                rk_t.append(p.v(lambda e, k=k: e.tensor_scalar(
                    rk[:, 2 + k:3 + k], rk[:, k:k + 1], CAP, None, ALU.is_lt),
                    wait=[trr]))
            g0 = p.v(lambda e, tl=tl: e.tensor_tensor(rk[:, 0:1], rk[:, 2:3],
                                                      w01[:, tl:tl + 1], ALU.mult),
                     wait=[rk_t[0]])
            g1 = p.v(lambda e, tl=tl: e.tensor_tensor(rk[:, 1:2], rk[:, 3:4],
                                                      w01[:, NT + tl:NT + tl + 1],
                                                      ALU.mult), wait=[rk_t[1]])
            tc1 = p.v(lambda e, tl=tl: e.tensor_scalar(
                comb[:, tl * 8:(tl + 1) * 8], oh1[:, tl * 8:(tl + 1) * 8], rk[:, 1:2],
                None, ALU.mult), wait=[g1])
            prev = p.v(lambda e, tl=tl: e.scalar_tensor_tensor(
                comb[:, tl * 8:(tl + 1) * 8], oh0[:, tl * 8:(tl + 1) * 8], rk[:, 0:1],
                comb[:, tl * 8:(tl + 1) * 8], ALU.mult, ALU.add), wait=[g0, tc1])
        p.pop(mark)
        return prev

    def l1_ffn(self, comb, acc_ap, gates=(None, None)):
        p = self.p
        self.g_cons = [gates[0], gates[0]]
        self.u_cons = [gates[0], gates[0]]
        self.dn_cons = gates[0]
        self.pg_gate = gates[1]
        hu = p.sbuf("hub", [128, 8 * T], BF16)
        gs = p.sbuf("gsl", [128, 2 * ND * 256], BF16)
        us = p.sbuf("usl", [128, 2 * ND * 256], BF16)
        dn = p.sbuf("dnb", [128, 8 * D], BF16)
        sg = p.sbuf("sgb", [128, T], BF16)
        acc_toks = [None] * NT
        y_tail = None
        for e_ in range(8):
            for fh in range(2):
                hu_toks = []
                pgh = [p.psum("pgA", [128, 1024], F32), p.psum("pgB", [128, 1024], F32)]
                puh = [p.psum("puA", [128, 1024], F32), p.psum("puB", [128, 1024], F32)]
                g_toks = [None, None]
                u_toks = [None, None]
                g_cons = self.g_cons
                u_cons = self.u_cons
                hu_half = [None, None]
                sil_half = [None, None]
                for h2s in range(4):
                    sl = h2s % 2
                    col0 = fh * 1024 + h2s * 256
                    g_toks[sl] = self.dma_w(
                        gs[:, sl * ND * 256:(sl + 1) * ND * 256],
                        self.inp["gpwT"][e_][:, col0:col0 + 256], ND,
                        wait=[g_cons[sl]] if g_cons[sl] else (),
                        q=("dga", "dgb")[sl])
                    u_toks[sl] = self.dma_w(
                        us[:, sl * ND * 256:(sl + 1) * ND * 256],
                        self.inp["upwT"][e_][:, col0:col0 + 256], ND,
                        wait=[u_cons[sl]] if u_cons[sl] else (),
                        q=("dua", "dub")[sl])
                    for h2 in range(2):
                        f2loc = h2s * 2 + h2
                        for ch in range(2):
                            mmg = mmu = None
                            for c in range(2):
                                for d in range(ND):
                                    mmg = p.t(lambda e, sl=sl, h2=h2, ch=ch, c=c, d=d:
                                              e.matmul(
                                        pgh[ch][:, c * 512:(c + 1) * 512],
                                        gs[:, sl * ND * 256 + d * 256 + h2 * 128:
                                            sl * ND * 256 + d * 256 + (h2 + 1) * 128],
                                        self.B1[:, d * T + (2 * ch + c) * 512:
                                                d * T + (2 * ch + c + 1) * 512],
                                        start=(d == 0), stop=(d == ND - 1)),
                                        wait=[g_toks[sl]] +
                                        ([hu_half[ch]] if hu_half[ch] and c == 0 and
                                         d == 0 else []) +
                                        ([y_tail] if y_tail and h2s == 0 and h2 == 0 and
                                         ch == 0 and c == 0 and d == 0 else []) +
                                        ([self.pg_gate] if self.pg_gate and h2s == 0 and
                                         h2 == 0 and ch == 0 and c == 0 and d == 0 else []))
                            sil_half[ch] = p.a(lambda e, ch=ch: e.activation(
                                sg[:, ch * 1024:(ch + 1) * 1024], pgh[ch][:, :],
                                AF.Silu), wait=[mmg])
                            for c in range(2):
                                for d in range(ND):
                                    mmu = p.t(lambda e, sl=sl, h2=h2, ch=ch, c=c, d=d:
                                              e.matmul(
                                        puh[ch][:, c * 512:(c + 1) * 512],
                                        us[:, sl * ND * 256 + d * 256 + h2 * 128:
                                            sl * ND * 256 + d * 256 + (h2 + 1) * 128],
                                        self.B1[:, d * T + (2 * ch + c) * 512:
                                                d * T + (2 * ch + c + 1) * 512],
                                        start=(d == 0), stop=(d == ND - 1)),
                                        wait=[u_toks[sl]] +
                                        ([hu_half[ch]] if hu_half[ch] and c == 0 and
                                         d == 0 else []))
                            hu_half[ch] = p.v(lambda e, f2loc=f2loc, ch=ch: e.tensor_tensor(
                                hu[:, f2loc * T + ch * 1024:f2loc * T + (ch + 1) * 1024],
                                sg[:, ch * 1024:(ch + 1) * 1024], puh[ch][:, :],
                                ALU.mult), wait=[sil_half[ch], mmu])
                            hu_toks.append(hu_half[ch])
                        g_cons[sl] = mmg
                        u_cons[sl] = mmu
                p.free(puh[1], puh[0], pgh[1], pgh[0])
                tok_dn = self.dma_w(dn[:, :],
                                    self.inp["downT"][e_][fh * 1024:(fh + 1) * 1024, :],
                                    8, wait=[self.dn_cons] if self.dn_cons else (),
                                    q="ddn")
                for tw in range(4):
                    pys = [p.psum(f"qy{i}", [128, D], F32) for i in range(4)]
                    for tt2 in range(4):
                        tl = tw * 4 + tt2
                        for dc in range(2):
                            mmy = None
                            for f2loc in range(8):
                                mmy = p.t(lambda e, tt2=tt2, dc=dc, f2loc=f2loc,
                                          tl=tl: e.matmul(
                                    pys[tt2][:, dc * 512:(dc + 1) * 512],
                                    hu[:, f2loc * T + tl * 128:f2loc * T + (tl + 1) * 128],
                                    dn[:, f2loc * D + dc * 512:f2loc * D + (dc + 1) * 512],
                                    start=(f2loc == 0), stop=(f2loc == 7)),
                                    wait=[tok_dn, hu_toks[-1]] +
                                    ([y_tail] if y_tail and f2loc == 0 else []))
                            last_y = mmy
                            self.dn_cons = mmy
                        tl_tok = p.v(lambda e, tt2=tt2, tl=tl, e_=e_: e.scalar_tensor_tensor(
                            acc_ap(tl), pys[tt2][:, :],
                            comb[:, tl * 8 + e_:tl * 8 + e_ + 1],
                            acc_ap(tl), ALU.mult, ALU.add),
                            wait=[last_y] + ([acc_toks[tl]] if acc_toks[tl] else []))
                        acc_toks[tl] = tl_tok
                        y_tail = tl_tok
                    p.free(*pys)
        p.free(sg, dn, us, gs, hu)
        p.fence()
        return acc_toks

    def emit_out(self, acc_ap, acc_toks, x1_src):
        p = self.p
        mark = p.push()
        NS = 4
        xo = p.sbuf("xob", [128, NS * D], F32)
        oo = p.sbuf("oob", [128, NS * D], F32)
        out_tok = [None] * NS
        lasts = []
        for tl in range(NT):
            sl = tl % NS
            tx = self.dma_f(xo[:, sl * D:(sl + 1) * D],
                            x1_src[tl * 128:(tl + 1) * 128, :],
                            wait=[out_tok[sl]] if out_tok[sl] else (),
                            q=("dsa", "dsb", "dsa", "dsb")[sl])
            tv = p.v(lambda e, tl=tl, sl=sl: e.tensor_tensor(
                oo[:, sl * D:(sl + 1) * D], xo[:, sl * D:(sl + 1) * D],
                acc_ap(tl), ALU.add),
                wait=[tx, acc_toks[tl]])
            out_tok[sl] = self.dma_f(self.out[tl * 128:(tl + 1) * 128, :],
                                     oo[:, sl * D:(sl + 1) * D], wait=[tv],
                                     q=("dsc", "dsd", "dse", "ds")[sl])
            lasts.append(out_tok[sl])
        p.op("sync", None, wait=lasts[-NS:])
        p.pop(mark)

    def build(self):
        p = self.p
        m = self.mode
        if m in ("l0", "fused"):
            self.ln_pass(self.inp["x"])
            aoT = self.attention(0)
            self.proj_pass(0, aoT)
            self.p.free(aoT)
            p.fence()
            self.moe_chain()
            if m == "l0":
                mark = p.push()
                xo = p.sbuf("xo1", [128, 2 * D], F32)
                oo = p.sbuf("oo1", [128, 2 * D], F32)
                ot = [None, None]
                last = None
                for tl in range(NT):
                    sl = tl % 2
                    t1 = self.dma_f(xo[:, sl * D:(sl + 1) * D],
                                    self.acc0_hbm[tl * 128:(tl + 1) * 128, :],
                                    wait=[ot[sl]] if ot[sl] else ())
                    t2 = self.dma_f(oo[:, sl * D:(sl + 1) * D],
                                    self.accf_hbm[tl * 128:(tl + 1) * 128, :],
                                    wait=[ot[sl]] if ot[sl] else ())
                    tv = p.v(lambda e, sl=sl: e.tensor_tensor(
                        oo[:, sl * D:(sl + 1) * D], oo[:, sl * D:(sl + 1) * D],
                        xo[:, sl * D:(sl + 1) * D], ALU.add), wait=[t1, t2])
                    ot[sl] = self.dma_f(self.out_x1[tl * 128:(tl + 1) * 128, :],
                                        oo[:, sl * D:(sl + 1) * D], wait=[tv])
                    last = ot[sl]
                p.op("sync", None, wait=[last])
                p.pop(mark)
            else:
                self.ln_pass(self.acc0_hbm, src2=self.accf_hbm,
                             copy_out=self.x1_hbm)
                oh0, oh1, w01, cm_tok = self.routing()
                comb = p.sbuf("combL1", [128, NT * 8], F32)
                self.offsets_and_comb(comb, oh0, oh1, w01, cm_tok)
                aoT = self.attention_l1()
                acc1 = p.sbuf("acc1", [128, NT * D], F32)

                def acc_ap(tl):
                    return acc1[:, tl * D:(tl + 1) * D]
                pj_mm, pj_ev = self.proj_pass(1, aoT, acc_ap, gate=self.attn_tail)
                acc_toks = self.l1_ffn(comb, acc_ap, gates=(pj_mm, pj_ev))
                self.emit_out(acc_ap, acc_toks, self.x1_hbm)
                p.free(acc1, comb)
                p.free(w01, oh1, oh0)
        elif m == "counts":
            self.ln_pass(self.inp["x1_in"])
            oh0, oh1, w01, cm_tok = self.routing()
            p.op("sync", None, wait=[cm_tok])
            p.free(w01, oh1, oh0)
        elif m == "l1":
            self.ln_pass(self.inp["x1_in"])
            oh0, oh1, w01, cm_tok = self.routing()
            comb = p.sbuf("combL1", [128, NT * 8], F32)
            self.offsets_and_comb(comb, oh0, oh1, w01, cm_tok)
            aoT = self.attention_l1()
            acc1 = p.sbuf("acc1", [128, NT * D], F32)

            def acc_ap(tl):
                return acc1[:, tl * D:(tl + 1) * D]
            pj_mm, pj_ev = self.proj_pass(1, aoT, acc_ap, gate=self.attn_tail)
            acc_toks = self.l1_ffn(comb, acc_ap, gates=(pj_mm, pj_ev))
            self.emit_out(acc_ap, acc_toks, self.inp["x1_in"])
            p.free(acc1, comb)
            p.free(w01, oh1, oh0)
        p.finish()
        return self.nc


_CACHE = {}


def _build(mode, shapes_key, shapes):
    key = (mode, shapes_key)
    if key not in _CACHE:
        c = Core(mode, shapes)
        _CACHE[key] = c.build()
    return _CACHE[key]


def kernel(**inputs):
    ins = host_layouts(inputs)
    x = np.ascontiguousarray(np.asarray(inputs["L0_attn_x"], dtype=np.float32))
    shapes = {k: v.shape for k, v in ins.items()}
    skey = tuple(sorted((k, tuple(v)) for k, v in shapes.items()))
    base = {k: ins[k] for k in CONST_KEYS}
    sel = np.zeros((8, 8, 1), np.float32)
    for c in range(8):
        sel[c, :c, 0] = 1.0

    if FUSED:
        nc = _build("fused", skey, shapes)
        maps = []
        for c in range(8):
            m = dict(base)
            for k in L0_KEYS + L1_KEYS:
                m[k] = ins[k]
            m["x"] = x[c]
            m["selmask"] = sel[c]
            maps.append(m)
        res = run_bass_kernel_spmd(nc, maps, core_ids=list(range(8)))
        return np.stack([r["out"] for r in res.results], 0)

    nc0 = _build("l0", skey, shapes)
    maps = []
    for c in range(8):
        m = {k: ins[k] for k in CONST_KEYS + L0_KEYS}
        m["x"] = x[c]
        maps.append(m)
    r0 = run_bass_kernel_spmd(nc0, maps, core_ids=list(range(8)))
    x1 = [np.ascontiguousarray(r["x1"]) for r in r0.results]

    ncc = _build("counts", skey, shapes)
    maps = [dict({k: ins[k] for k in CONST_KEYS}, gate_wT=ins["gate_wT"],
                 x1_in=x1[c]) for c in range(8)]
    rc = run_bass_kernel_spmd(ncc, maps, core_ids=list(range(8)))
    allc = np.ascontiguousarray(
        np.concatenate([r["counts"] for r in rc.results], 0))

    nc1 = _build("l1", skey, shapes)
    maps = []
    for c in range(8):
        m = {k: ins[k] for k in CONST_KEYS + L1_KEYS}
        m["x1_in"] = x1[c]
        m["allcounts"] = allc
        m["selmask"] = sel[c]
        maps.append(m)
    r1 = run_bass_kernel_spmd(nc1, maps, core_ids=list(range(8)))
    return np.stack([r["out"] for r in r1.results], 0)


FUSED = True


# version 14
# speedup vs baseline: 1.0357x; 1.0007x over previous
import numpy as np
import ml_dtypes
import concourse.bass as bass
import concourse.mybir as mybir
from concourse.bass_utils import run_bass_kernel_spmd

F32 = mybir.dt.float32
BF16 = mybir.dt.bfloat16
AF = mybir.ActivationFunctionType
ALU = mybir.AluOpType
AXX = mybir.AxisListType.X
BF = ml_dtypes.bfloat16

T = 2048
D = 1024
NT = 16
ND = 8
H = 16
VA = 65
NEG = -1.0e38
EPS = 1e-5
CAP = 2560.0
CSC = 0.1250625


class Prog:

    def __init__(self, nc):
        self.nc = nc
        self.recs = []
        self.cnt = {}
        self.sem_names = []
        self.uid = 0
        self._mgr_map = {}
        self._stacks = {"s": [], "p": []}
        self._kind = {}
        self._fence = ()

    def sem(self, name):
        self.sem_names.append(name)
        self.cnt[name] = 0

    def fence(self):
        self._fence = tuple((s, self.cnt[s]) for s in self.sem_names if self.cnt[s] > 0)

    def op(self, eng, fn, wait=(), inc=None, n=1):
        w = tuple(t for t in wait if t is not None) + self._fence
        tok = None
        if inc is not None:
            self.cnt[inc] += n
            tok = (inc, self.cnt[inc])
        self.recs.append((eng, fn, w, inc, n))
        return tok

    def t(self, fn, wait=(), inc="pe", n=1):
        return self.op("tensor", fn, wait, inc, n)

    def v(self, fn, wait=(), inc="dve", n=1):
        return self.op("vector", fn, wait, inc, n)

    def a(self, fn, wait=(), inc="act", n=1):
        return self.op("scalar", fn, wait, inc, n)

    def g(self, fn, wait=(), inc="dg", n=16):
        w = tuple(wait) + (((inc, self.cnt[inc]),) if self.cnt.get(inc) else ())
        return self.op("gpsimd", fn, w, inc, n)

    def s(self, fn, wait=(), inc="ds", n=16):
        w = tuple(wait) + (((inc, self.cnt[inc]),) if self.cnt.get(inc) else ())
        return self.op("sync", fn, w, inc, n)

    def name(self, base):
        self.uid += 1
        return f"{base}_{self.uid}"

    def sbuf(self, base, shape, dtype):
        mgr = self.nc.sbuf_tensor(self.name(base), shape, dtype)
        h = mgr.__enter__()
        self._mgr_map[id(h)] = mgr
        self._kind[id(h)] = "s"
        self._stacks["s"].append(id(h))
        return h

    def psum(self, base, shape, dtype=F32):
        mgr = self.nc.psum_tensor(self.name(base), shape, dtype)
        h = mgr.__enter__()
        self._mgr_map[id(h)] = mgr
        self._kind[id(h)] = "p"
        self._stacks["p"].append(id(h))
        return h

    def free(self, *handles):
        for kind in ("s", "p"):
            ids = {id(h) for h in handles if self._kind.get(id(h)) == kind}
            if not ids:
                continue
            st = self._stacks[kind]
            assert ids == set(st[-len(ids):]), f"non-stack free ({kind})"
            for _ in range(len(ids)):
                hid = st.pop()
                self._mgr_map.pop(hid).__exit__(None, None, None)
                del self._kind[hid]

    def push(self):
        return (len(self._stacks["s"]), len(self._stacks["p"]))

    def pop(self, marker):
        ms, mp = marker
        for kind, m in (("p", mp), ("s", ms)):
            st = self._stacks[kind]
            while len(st) > m:
                hid = st.pop()
                self._mgr_map.pop(hid).__exit__(None, None, None)
                del self._kind[hid]

    def finish(self):
        sems = {}
        sem_mgrs = []
        for nm in self.sem_names:
            mgr = self.nc.semaphore(nm)
            sems[nm] = mgr.__enter__()
            sem_mgrs.append(mgr)
        blk_mgr = self.nc.Block()
        block = blk_mgr.__enter__()
        waited = {}

        def emitter(eng_name):
            def _(engine):
                for (e, fn, waits, inc, n) in self.recs:
                    if e != eng_name:
                        continue
                    for (snm, c) in waits:
                        if waited.get((eng_name, snm), 0) < c:
                            engine.wait_ge(sems[snm], c)
                            waited[(eng_name, snm)] = c
                    if fn is None:
                        continue
                    inst = fn(engine)
                    if inc is not None:
                        inst.then_inc(sems[inc], n)
            _.__name__ = "_"
            return _

        block.tensor(emitter("tensor"))
        block.vector(emitter("vector"))
        block.scalar(emitter("scalar"))
        block.gpsimd(emitter("gpsimd"))
        block.sync(emitter("sync"))
        blk_mgr.__exit__(None, None, None)
        for mgr in reversed(sem_mgrs):
            mgr.__exit__(None, None, None)


def host_layouts(inputs):
    f = np.float32

    def C(x):
        return np.ascontiguousarray(np.asarray(x, dtype=f))

    ins = {}
    ins["qk_w"] = C(inputs["L0_attn_qk_w"])
    ins["qk_bT"] = C(np.asarray(inputs["L0_attn_qk_b"]).reshape(16, 128).T)
    ins["v_w"] = C(inputs["L0_attn_v_w"])
    ins["cproj_w"] = C(inputs["L0_attn_cproj_w"])
    for nm, key in (("resid64", "L0_attn_resid_gain"), ("skip64", "L0_attn_skip_gain"),
                    ("centre64", "L0_attn_centre_gain")):
        g = np.asarray(inputs[key]).reshape(H)
        ins[nm] = C(np.broadcast_to(g[None, :], (64, H)))
    ins["rw"] = C(inputs["L0_ffn_router_w"])
    rb = np.asarray(inputs["L0_ffn_router_b"], dtype=f)
    ins["rb_rep"] = C(np.broadcast_to(rb[:, None, :], (4, 128, 4)))
    ins["ln_gT"] = C(np.asarray(inputs["L0_ffn_ln_g"]).reshape(4, 4, 8, 128).transpose(0, 1, 3, 2))
    ins["ln_bT"] = C(np.asarray(inputs["L0_ffn_ln_b"]).reshape(4, 4, 8, 128).transpose(0, 1, 3, 2))
    ins["w1"] = C(inputs["L0_ffn_w1"])
    ins["b1T"] = C(np.asarray(inputs["L0_ffn_b1"]).reshape(4, 4, 32, 128).transpose(0, 1, 3, 2))
    ins["w2"] = C(inputs["L0_ffn_w2"])
    ins["b2r"] = C(np.asarray(inputs["L0_ffn_b2"]).reshape(4, 4, 1, 1024))
    ins["qkv_wT"] = C(np.asarray(inputs["L1_attn_qkv_w"]).T)
    for nm, key in (("qn_w128", "L1_attn_qn_w"), ("qn_b128", "L1_attn_qn_b"),
                    ("kn_w128", "L1_attn_kn_w"), ("kn_b128", "L1_attn_kn_b")):
        v = np.asarray(inputs[key], dtype=f)
        ins[nm] = C(np.concatenate([v, v])[:, None])
    ins["sinks"] = C(np.asarray(inputs["L1_attn_sinks"])[None, :])
    ins["proj_wT"] = C(np.asarray(inputs["L1_attn_proj_w"]).T)
    ins["projb_rep"] = C(np.broadcast_to(
        np.asarray(inputs["L1_attn_proj_b"], dtype=f)[None, :], (128, D)))
    ins["gate_wT"] = C(np.asarray(inputs["L1_ffn_gate_w"]).T)
    ins["gpwT"] = C(np.asarray(inputs["L1_ffn_gate_proj_w"]).transpose(0, 2, 1))
    ins["upwT"] = C(np.asarray(inputs["L1_ffn_up_proj_w"]).transpose(0, 2, 1))
    ins["downT"] = C(np.asarray(inputs["L1_ffn_down_proj_w"]).transpose(0, 2, 1))
    ins["identity"] = np.eye(128, dtype=f)
    b = np.arange(128)[:, None]
    a = np.arange(128)[None, :]
    ins["mask_kgtq"] = np.where(b > a, NEG, 0.0).astype(f)
    ins["mask_kleq"] = np.where(b <= a, NEG, 0.0).astype(f)
    inv_idx = (1.0 / (np.arange(T, dtype=np.float64) + 1.0)).astype(f)
    ins["invidx64"] = C(np.broadcast_to(inv_idx[None, :], (64, T)))
    invf = (1.0 / (10000.0 ** (np.arange(0, 64, 2).astype(np.float64) / 64.0))).astype(f)
    emb = np.arange(T, dtype=f)[:, None] * invf[None, :]
    cos = np.cos(np.concatenate([emb, emb], 1)).astype(f).T
    sin = np.sin(np.concatenate([emb, emb], 1)).astype(f).T
    sgn = np.concatenate([-np.ones(32, f), np.ones(32, f)])[:, None]
    ins["cos128"] = C(np.concatenate([cos, cos], 0))
    ins["sin128"] = C(np.concatenate([sin * sgn, sin * sgn], 0))
    ins["ones_row"] = np.ones((1, 128), dtype=f)
    ins["ones_col"] = np.ones((128, 1), dtype=f)
    blk2 = np.zeros((128, 2), dtype=f)
    blk2[0:64, 0] = 1.0
    blk2[64:128, 1] = 1.0
    ins["blk2"] = blk2
    ins["blk2T"] = C(blk2.T)
    return ins


L0_KEYS = ["qk_w", "qk_bT", "v_w", "cproj_w", "resid64", "skip64", "centre64",
           "rw", "rb_rep", "ln_gT", "ln_bT", "w1", "b1T", "w2", "b2r"]
L1_KEYS = ["qkv_wT", "qn_w128", "qn_b128", "kn_w128", "kn_b128", "sinks",
           "proj_wT", "projb_rep", "gate_wT", "gpwT", "upwT", "downT"]
CONST_KEYS = ["identity", "mask_kgtq", "mask_kleq", "invidx64", "cos128", "sin128",
              "ones_row", "ones_col", "blk2", "blk2T"]


class Core:
    def __init__(self, mode, shapes, dbg=()):
        self.mode = mode
        self.dbg = set(dbg)
        self.dbg_names = []
        nc = bass.Bass("TRN2", target_bir_lowering=False)
        self.nc = nc
        p = Prog(nc)
        self.p = p
        for s in ("pe", "act", "dve", "dg", "ds", "cc", "dwa", "dwb", "dw2",
                  "dua", "dub", "dga", "dgb", "ddn", "dpw", "dcn", "dcg",
                  "dsa", "dsb", "dsc", "dsd", "dse"):
            p.sem(s)
        need = list(CONST_KEYS)
        if mode in ("l0", "fused"):
            need += L0_KEYS
        if mode == "counts":
            need += ["gate_wT"]
        if mode in ("l1", "fused"):
            need += L1_KEYS
        self.inp = {}
        for k in need:
            self.inp[k] = nc.dram_tensor(k, list(shapes[k]), F32, kind="ExternalInput").ap()
        if mode in ("l0", "fused"):
            self.inp["x"] = nc.dram_tensor("x", [T, D], F32, kind="ExternalInput").ap()
        if mode in ("counts", "l1"):
            self.inp["x1_in"] = nc.dram_tensor("x1_in", [T, D], F32, kind="ExternalInput").ap()
        if mode == "l1":
            self.inp["allcounts"] = nc.dram_tensor("allcounts", [8, 16], F32,
                                                   kind="ExternalInput").ap()
        if mode in ("l1", "fused"):
            self.inp["selmask"] = nc.dram_tensor("selmask", [8, 1], F32,
                                                 kind="ExternalInput").ap()
        if mode == "l0":
            self.out_x1 = nc.dram_tensor("x1", [T, D], F32, kind="ExternalOutput").ap()
        if mode == "counts":
            self.out_counts = nc.dram_tensor("counts", [1, 16], F32, kind="ExternalOutput").ap()
        if mode in ("l1", "fused"):
            self.out = nc.dram_tensor("out", [T, D], F32, kind="ExternalOutput").ap()
        if mode in ("l0", "fused"):
            self.acc0_hbm = nc.dram_tensor("acc0_hbm", [T, D], F32).ap()
            self.accf_hbm = nc.dram_tensor("accf_hbm", [T, D], F32).ap()
        if mode == "fused":
            self.x1_hbm = nc.dram_tensor("x1_hbm", [T, D], F32).ap()
            self.cc_in = nc.dram_tensor("cc_in", [1, 16], F32).ap()
            self.cc_out = nc.dram_tensor("cc_out", [8, 16], F32).ap()
        self.B1 = p.sbuf("B1", [128, ND * T], BF16)
        self.B2 = p.sbuf("B2", [128, NT * D], BF16)
        self.ident = p.sbuf("ident", [128, 128], BF16)
        self.ident_f = p.sbuf("identf", [128, 128], F32)
        self.m_kgtq = p.sbuf("mkgtq", [128, 128], F32)
        self.m_kleq = p.sbuf("mkleq", [128, 128], F32)
        self.ones_row_f = p.sbuf("onesrf", [1, 128], F32)
        self.ones_row_bf = p.sbuf("onesrb", [1, 128], BF16)
        self.ones_col_bf = p.sbuf("onescb", [128, 1], BF16)
        self.ones_col_f = p.sbuf("onescf", [128, 1], F32)
        p.v(lambda e: e.memset(self.ones_col_f[:, :], 1.0))
        self._joinbuf = p.sbuf("joinb", [1, 1], F32)
        self.epsb = p.sbuf("epsb", [128, 1], F32)
        self.c0125 = p.sbuf("c0125", [128, 1], F32)
        p.v(lambda e: e.memset(self.epsb[:, :], float(EPS)))
        p.v(lambda e: e.memset(self.c0125[:, :], 0.125))
        self.dma_c(self.ident[:, :], self.inp["identity"])
        self.dma_f(self.ident_f[:, :], self.inp["identity"])
        self.dma_f(self.m_kgtq[:, :], self.inp["mask_kgtq"])
        self.dma_f(self.m_kleq[:, :], self.inp["mask_kleq"])
        self.dma_f(self.ones_row_f[:, :], self.inp["ones_row"])
        self.dma_c(self.ones_row_bf[:, :], self.inp["ones_row"])
        self.dma_c(self.ones_col_bf[:, :], self.inp["ones_col"])
        p.fence()

    def dma_f(self, dst, src, wait=(), q="ds"):
        return self.p.s(lambda e: e.dma_start(out=dst, in_=src), wait=wait, inc=q)

    def dma_c(self, dst, src, wait=(), q="dg"):
        return self.p.g(lambda e: e.dma_start(out=dst, in_=src), wait=wait, inc=q)

    def dma_w(self, dst2d, src2d, nt_, wait=(), q="dg"):
        src3 = src2d.rearrange("(dt p) c -> p dt c", p=128)
        dst3 = dst2d.rearrange("p (dt c) -> p dt c", dt=nt_)
        return self.p.g(lambda e: e.dma_start(out=dst3, in_=src3), wait=wait, inc=q)

    def dump(self, name, src_ap, shape, dtype=F32):
        if name not in self.dbg:
            return
        t = self.nc.dram_tensor("dbg_" + name, list(shape), dtype, kind="ExternalOutput").ap()
        self.dbg_names.append(name)
        self.p.fence()
        self.dma_f(t, src_ap)
        self.p.fence()

    def ln_pass(self, src, src2=None, copy_out=None,
                tm_out=None, T_out=None, tiles=range(NT)):
        p = self.p
        tm_out = tm_out if tm_out is not None else self.B2
        T_out = T_out if T_out is not None else self.B1
        mark = p.push()
        NS = 4
        xa = p.sbuf("lnx", [128, NS * D], F32)
        xb = p.sbuf("lnx2", [128, NS * D], F32) if src2 is not None else None
        stat = [p.sbuf(f"lnst{i}", [128, 12], F32) for i in range(NS)]
        mv = [p.sbuf(f"lnmv{i}", [128, 2], F32) for i in range(NS)]
        rs = [p.sbuf(f"lnrs{i}", [128, 1], F32) for i in range(NS)]
        sd = [p.sbuf(f"lnsd{i}", [128, 1], F32) for i in range(NS)]
        nmr = [p.sbuf(f"lnnm{i}", [128, 1], F32) for i in range(NS)]
        ptr = [p.psum("lntr0", [128, 512], BF16), p.psum("lntr1", [128, 512], BF16)]
        free_tok = [None] * NS
        tr_free = [None, None]
        stat_free = [None] * NS
        for t in tiles:
            sl = t % NS
            xap = xa[:, sl * D:(sl + 1) * D]
            tok = self.dma_f(xap, src[t * 128:(t + 1) * 128, :],
                             wait=free_tok[sl] or (),
                             q=("dsa", "dsb", "dsa", "dsb")[sl])
            if src2 is not None:
                bap = xb[:, sl * D:(sl + 1) * D]
                tok2 = self.dma_f(bap, src2[t * 128:(t + 1) * 128, :],
                                  wait=free_tok[sl] or (),
                                  q=("dsc", "dsd", "dsc", "dsd")[sl])
                tok = p.v(lambda e, xap=xap, bap=bap: e.tensor_tensor(
                    xap, xap, bap, ALU.add), wait=[tok, tok2])
                cp_tok = None
                if copy_out is not None:
                    cp_tok = self.dma_f(copy_out[t * 128:(t + 1) * 128, :], xap,
                                        wait=[tok])
            p.v(lambda e, xap=xap, sl=sl: e.bn_stats(stat[sl][:, 0:6], xap[:, 0:512]),
                wait=[tok] + (stat_free[sl] or []))
            t_st = p.v(lambda e, xap=xap, sl=sl: e.bn_stats(stat[sl][:, 6:12],
                                                            xap[:, 512:1024]),
                       wait=[tok] + (stat_free[sl] or []))
            t_mv = p.v(lambda e, sl=sl: e.bn_aggr(mv[sl][:, :], stat[sl][:, :]),
                       wait=[t_st])
            t_sd = p.a(lambda e, sl=sl: e.activation(sd[sl][:, :], mv[sl][:, 1:2],
                                                     AF.Sqrt, bias=self.epsb[:, :]),
                       wait=[t_mv])
            t_rs = p.v(lambda e, sl=sl: e.reciprocal(rs[sl][:, :], sd[sl][:, :]),
                       wait=[t_sd])
            t_nm = p.v(lambda e, sl=sl: e.tensor_scalar(nmr[sl][:, :], mv[sl][:, 0:1],
                                                        rs[sl][:, :], -1.0,
                                                        ALU.mult, ALU.mult),
                       wait=[t_rs])
            t_xn = p.v(lambda e, xap=xap, t=t, sl=sl: e.tensor_scalar(
                tm_out[:, t * D:(t + 1) * D], xap, rs[sl][:, :], nmr[sl][:, :],
                ALU.mult, ALU.add), wait=[t_nm])
            stat_free[sl] = [t_nm]
            free_tok[sl] = [t_xn] if src2 is None else [t_xn, cp_tok]
            for jg in range(2):
                last = None
                pslot = ptr[jg]
                for j4 in range(4):
                    j = jg * 4 + j4
                    last = p.t(lambda e, t=t, j=j, j4=j4, pslot=pslot: e.transpose(
                        pslot[:, j4 * 128:(j4 + 1) * 128],
                        tm_out[:, t * D + j * 128:t * D + (j + 1) * 128],
                        self.ident[:, :]),
                        wait=[t_xn] + ([tr_free[jg]] if tr_free[jg] is not None and
                                       j4 == 0 else []))
                tr_free[jg] = p.v(lambda e, t=t, jg=jg, pslot=pslot: e.tensor_copy(
                    T_out.rearrange("p (j tt) -> p j tt", j=ND)[:, jg * 4:(jg + 1) * 4,
                                                                t * 128:(t + 1) * 128],
                    pslot.rearrange("p (j tt) -> p j tt", j=4)[:, :, :]),
                    wait=[last])
        p.pop(mark)
        p.fence()

    def attention(self, layer):
        p = self.p
        L1 = (layer == 1)
        aoT = self.B2 if L1 else p.sbuf("aoT", [128, ND * T], BF16)
        mark = p.push()
        vt = p.sbuf("vaug", [128, NT * H * VA], BF16)
        vw = p.sbuf("vw", [128, ND * D], BF16)
        if not L1:
            vsrc = self.inp["v_w"]
        else:
            vsrc = self.inp["qkv_wT"][:, 2048:3072]
        tok_vw = self.dma_w(vw[:, :], vsrc, ND)
        pvv = p.psum("pvv", [128, 1024], F32)
        vev = None
        for t in range(NT):
            last = None
            for dc in range(2):
                for d in range(ND):
                    last = p.t(lambda e, t=t, dc=dc, d=d: e.matmul(
                        pvv[:, dc * 512:(dc + 1) * 512],
                        self.B1[:, d * T + t * 128:d * T + (t + 1) * 128],
                        vw[:, d * D + dc * 512:d * D + (dc + 1) * 512],
                        start=(d == 0), stop=(d == ND - 1)),
                        wait=[tok_vw] + ([vev] if vev is not None and dc == 0 and d == 0 else []))
            vev = p.v(lambda e, t=t: e.tensor_copy(
                vt.rearrange("p (tt h c) -> p tt h c", tt=NT, h=H)[:, t, :, 0:64],
                pvv.rearrange("p (h c) -> p h c", h=H)[:, :, :]), wait=[last])
        vready = p.v(lambda e: e.memset(
            vt.rearrange("p (tth c) -> p tth c", c=VA)[:, :, 64:65], 1.0), wait=[vev])
        p.fence()
        p.free(pvv, vw)
        ctx = {}
        if not L1:
            ctx["qkb"] = p.sbuf("qkb", [128, 16], F32)
            self.dma_f(ctx["qkb"][:, :], self.inp["qk_bT"])
            for nm in ("resid64", "skip64", "centre64"):
                ctx[nm] = p.sbuf(nm, [64, H], F32)
                tk = self.dma_f(ctx[nm][:, :], self.inp[nm])
            ctx["negc64"] = p.sbuf("negc64", [64, H], F32)
            p.v(lambda e: e.tensor_scalar(ctx["negc64"][:, :], ctx["centre64"][:, :],
                                          -1.0, None, ALU.mult), wait=[tk])
            ctx["invidx"] = p.sbuf("invidx", [64, T], F32)
            self.dma_f(ctx["invidx"][:, :], self.inp["invidx64"])
            ctx["vTh"] = [p.sbuf("vTh0", [64, T], BF16), p.sbuf("vTh1", [64, T], BF16)]
            ctx["csh"] = [p.sbuf("csh0", [64, T], F32), p.sbuf("csh1", [64, T], F32)]
            ctx["zb"] = [p.sbuf("zb0", [64, 512], F32), p.sbuf("zb1", [64, 512], F32)]
            ctx["zero64"] = p.sbuf("zero64", [64, 1], F32)
            p.v(lambda e: e.memset(ctx["zero64"][:, :], 0.0))
        else:
            ctx["cosb"] = p.sbuf("cosb", [128, T], F32)
            ctx["sinb"] = p.sbuf("sinb", [128, T], F32)
            self.dma_f(ctx["cosb"][:, :], self.inp["cos128"])
            self.dma_f(ctx["sinb"][:, :], self.inp["sin128"])
            ctx["sink_sb"] = p.sbuf("sink", [1, 16], F32)
            self.dma_f(ctx["sink_sb"][:, :], self.inp["sinks"])
            ctx["shufb"] = p.sbuf("shuf", [128, T], BF16)
            ctx["sq16"] = p.sbuf("sq16", [128, T], BF16)
            ctx["lntA"] = p.sbuf("lntA", [128, T], F32)
            ctx["lntB"] = p.sbuf("lntB", [128, T], F32)
            ctx["blk2b"] = p.sbuf("blk2b", [128, 2], BF16)
            self.dma_c(ctx["blk2b"][:, :], self.inp["blk2"])
            ctx["blk2T"] = p.sbuf("blk2Tf", [2, 128], F32)
            self.dma_f(ctx["blk2T"][:, :], self.inp["blk2T"])
            for wn, key in (("qnw", "qn_w128"), ("qnb", "qn_b128"),
                            ("knw", "kn_w128"), ("knb", "kn_b128")):
                ctx[wn] = p.sbuf(wn, [128, 1], F32)
                self.dma_f(ctx[wn][:, :], self.inp[key])
        ctx["neg1"] = p.sbuf("neg1L0", [1, 1], F32)
        p.v(lambda e: e.memset(ctx["neg1"][:, :], -1.0))
        ctx["qb"] = p.sbuf("qbuf", [128, T], BF16)
        ctx["kb"] = p.sbuf("kbuf", [128, T], BF16)
        ctx["pairw"] = p.sbuf("pairw", [128, ND * 256], BF16)
        ctx["sq"] = p.sbuf("sqb", [128, T], BF16)
        ctx["nrm"] = p.sbuf("nrm", [1, T], F32)
        ctx["mx"] = p.sbuf("mxb", [1, 8], F32)
        ctx["chb"] = p.sbuf("chb", [128, 2], F32)
        ctx["sinkx"] = p.sbuf("sinkx", [1, 2], F32)
        ctx["pT"] = [p.sbuf("pT0", [128, 2 * 512], BF16),
                     p.sbuf("pT1", [128, 2 * 512], BF16)]
        ctx["invr"] = [p.sbuf("invr0", [1, 512], F32), p.sbuf("invr1", [1, 512], F32)]
        ctx["ev64"] = [p.sbuf("ev640", [64, 512], F32), p.sbuf("ev641", [64, 512], F32)]
        ctx["t1b"] = [p.sbuf("t1b0", [64, 512], F32), p.sbuf("t1b1", [64, 512], F32)]
        ctx["vready"] = vready
        ctx["vt"] = vt
        ctx["aoT"] = aoT
        p.fence()
        stage = getattr(self, "attn_stage", None)
        tail = None
        if stage != "v":
            npair = 1 if stage in ("pairs", "heads1") else 8
            for jp in range(npair):
                tail = self.attn_pair(layer, jp, ctx, tail)
        p.fence()
        p.pop(mark)
        return aoT

    def attention_l1(self):
        p = self.p
        aoT = self.B2
        mark = p.push()
        p.fence()
        vt = p.sbuf("vaug", [128, NT * H * VA], BF16)
        vw = p.sbuf("vw", [128, ND * D], BF16)
        tok_vw = self.dma_w(vw[:, :], self.inp["qkv_wT"][:, 2048:3072], ND)
        pvv = p.psum("pvv", [128, 1024], F32)
        vev = None
        for t in range(NT):
            last = None
            for dc in range(2):
                for d in range(ND):
                    last = p.t(lambda e, t=t, dc=dc, d=d: e.matmul(
                        pvv[:, dc * 512:(dc + 1) * 512],
                        self.B1[:, d * T + t * 128:d * T + (t + 1) * 128],
                        vw[:, d * D + dc * 512:d * D + (dc + 1) * 512],
                        start=(d == 0), stop=(d == ND - 1)),
                        wait=[tok_vw] + ([vev] if vev is not None and dc == 0 and d == 0 else []))
            vev = p.v(lambda e, t=t: e.tensor_copy(
                vt.rearrange("p (tt h c) -> p tt h c", tt=NT, h=H)[:, t, :, 0:64],
                pvv.rearrange("p (h c) -> p h c", h=H)[:, :, :]), wait=[last])
        vready = p.v(lambda e: e.memset(
            vt.rearrange("p (tth c) -> p tth c", c=VA)[:, :, 64:65], 1.0), wait=[vev])
        p.fence()
        p.free(pvv, vw)
        ctx = {}
        ctx["cosb"] = p.sbuf("cosb", [128, T], F32)
        ctx["sinb"] = p.sbuf("sinb", [128, T], F32)
        self.dma_f(ctx["cosb"][:, :], self.inp["cos128"])
        self.dma_f(ctx["sinb"][:, :], self.inp["sin128"])
        ctx["sink_sb"] = p.sbuf("sink", [1, 16], F32)
        self.dma_f(ctx["sink_sb"][:, :], self.inp["sinks"])
        ctx["shufb"] = p.sbuf("shuf", [128, T], BF16)
        ctx["sq16"] = p.sbuf("sq16", [128, T], BF16)
        ctx["ropeA"] = p.sbuf("ropeA", [128, T], F32)
        ctx["ropeB"] = p.sbuf("ropeB", [128, T], F32)
        ctx["blk2b"] = p.sbuf("blk2b", [128, 2], BF16)
        self.dma_c(ctx["blk2b"][:, :], self.inp["blk2"])
        ctx["blk2T"] = p.sbuf("blk2Tf", [2, 128], F32)
        self.dma_f(ctx["blk2T"][:, :], self.inp["blk2T"])
        for wn, key in (("qnw", "qn_w128"), ("qnb", "qn_b128"),
                        ("knw", "kn_w128"), ("knb", "kn_b128")):
            ctx[wn] = p.sbuf(wn, [128, 1], F32)
            self.dma_f(ctx[wn][:, :], self.inp[key])
        ctx["qbs"] = [p.sbuf("qb0", [128, T], BF16), p.sbuf("qb1", [128, T], BF16)]
        ctx["kbs"] = [p.sbuf("kb0", [128, T], BF16), p.sbuf("kb1", [128, T], BF16)]
        ctx["pairw"] = p.sbuf("pairw", [128, ND * 256], BF16)
        ctx["sq"] = p.sbuf("sqb", [128, T], BF16)
        ctx["nrm"] = p.sbuf("nrm", [1, T], F32)
        ctx["mxs"] = [p.sbuf("mx0", [1, 8], F32), p.sbuf("mx1", [1, 8], F32)]
        ctx["chbs"] = [p.sbuf("chb0", [128, 2], F32), p.sbuf("chb1", [128, 2], F32)]
        ctx["sinkxs"] = [p.sbuf("skx0", [1, 2], F32), p.sbuf("skx1", [1, 2], F32)]
        ctx["lnmu"] = [p.sbuf("lnmu0", [2, 512], BF16), p.sbuf("lnmu1", [2, 512], BF16)]
        ctx["lnvr"] = [p.sbuf("lnvr0", [2, 512], F32), p.sbuf("lnvr1", [2, 512], F32)]
        ctx["lnm2"] = [p.sbuf("lnm20", [2, 512], F32), p.sbuf("lnm21", [2, 512], F32)]
        ctx["lnrs"] = [p.sbuf("lnrs0", [2, 512], BF16), p.sbuf("lnrs1", [2, 512], BF16)]
        ctx["blk2Tb"] = p.sbuf("blk2Tb", [2, 128], BF16)
        self.dma_c(ctx["blk2Tb"][:, :], self.inp["blk2T"])
        ctx["mxsc"] = p.sbuf("mxsc", [1, 16], F32)
        ctx["neg1"] = p.sbuf("neg1", [1, 1], F32)
        p.v(lambda e: e.memset(ctx["neg1"][:, :], -1.0))
        ctx["nhalf2"] = p.sbuf("nhalf2", [2, 1], F32)
        p.v(lambda e: e.memset(ctx["nhalf2"][:, :], -0.5))
        ctx["pT"] = [p.sbuf("pT0", [128, 2 * 512], BF16),
                     p.sbuf("pT1", [128, 2 * 512], BF16)]
        ctx["invr"] = [p.sbuf("invr0", [1, 512], F32), p.sbuf("invr1", [1, 512], F32)]
        ctx["ev64"] = [p.sbuf("ev640", [64, 512], F32), p.sbuf("ev641", [64, 512], F32)]
        ctx["vready"] = vready
        ctx["vt"] = vt
        ctx["aoT"] = aoT
        ctx["W2"] = [p.psum("WA", [128, 512], F32), p.psum("SA", [128, 512], F32),
                     p.psum("WB", [128, 512], F32), p.psum("SB", [128, 512], F32)]
        ctx["stp"] = [p.psum("stA", [128, 512], F32), p.psum("stB", [128, 512], F32)]
        ctx["pvP"] = p.psum("pvP", [VA, 512], F32)
        ctx["pabP"] = p.psum("pabP", [64, 512], F32)
        p.fence()
        state = {"Afree": [[None, None], [None, None]], "pairw_mm": None,
                 "nrm_free": None, "qb_free": [None, None], "rope_sc": None}
        hand = [None] * 8
        btail = None
        for jp in range(8):
            hand[jp] = self.attn_pA_l1(jp, ctx, state)
            if jp >= 1:
                btail = self.attn_pB_l1(jp - 1, ctx, hand[jp - 1], btail)
                state["qb_free"][(jp - 1) % 2] = btail
        btail = self.attn_pB_l1(7, ctx, hand[7], btail)
        self.dump("ao", aoT[:, :], [128, ND * T], BF16)
        self.attn_tail = btail
        p.pop(mark)
        return aoT

    def attn_pA_l1(self, jp, ctx, state):
        p = self.p
        par = jp % 2
        W2 = [ctx["W2"][2 * par], ctx["W2"][2 * par + 1]]
        bank_busy = list(state["Afree"][par])
        qb, kb = ctx["qbs"][par], ctx["kbs"][par]
        pairw = ctx["pairw"]
        qbf = state["qb_free"][par]
        wsrc = self.inp["qkv_wT"]
        wq = self.dma_c(pairw.rearrange("p (d two c) -> p d two c", d=ND, two=2)[:, :, 0, :],
                        wsrc[:, jp * 128:(jp + 1) * 128]
                        .rearrange("(dt p) c -> p dt c", p=128),
                        wait=[state["pairw_mm"]] if state["pairw_mm"] else (), q="dpw")
        wk = self.dma_c(pairw.rearrange("p (d two c) -> p d two c", d=ND, two=2)[:, :, 1, :],
                        wsrc[:, 1024 + jp * 128:1024 + (jp + 1) * 128]
                        .rearrange("(dt p) c -> p dt c", p=128),
                        wait=[state["pairw_mm"]] if state["pairw_mm"] else (), q="dpw")
        cps = [[None] * 4, [None] * 4]
        lastmm = None
        for ci in range(8):
            qk, cc = divmod(ci, 4)
            b = ci % 2
            pp = W2[b]
            wtok = wq if qk == 0 else wk
            for d in range(ND):
                lastmm = p.t(lambda e, qk=qk, cc=cc, d=d, pp=pp: e.matmul(
                    pp[:, :], pairw[:, d * 256 + qk * 128:d * 256 + qk * 128 + 128],
                    self.B1[:, d * T + cc * 512:d * T + (cc + 1) * 512],
                    start=(d == 0), stop=(d == ND - 1)),
                    wait=([wtok] if d == 0 else []) +
                    ([bank_busy[b]] if bank_busy[b] and d == 0 else []))
            buf = qb if qk == 0 else kb
            cp = p.a(lambda e, buf=buf, cc=cc, pp=pp: e.activation(
                buf[:, cc * 512:(cc + 1) * 512], pp[:, :], AF.Copy),
                wait=[lastmm] + ([qbf] if qbf and ci < 2 else []))
            bank_busy[b] = cp
            cps[qk][cc] = cp
        state["pairw_mm"] = lastmm
        if self.dbg and jp == 0:
            self.p.op("sync", None, wait=[cps[0][3], cps[1][3]])
            self.dump("qraw0", qb[:, :], [128, T], BF16)
            self.dump("kraw0", kb[:, :], [128, T], BF16)
        rtoks = []
        rope_prev = state["rope_sc"]
        for qk, (buf, wsb, bsb) in enumerate(
                ((qb, ctx["qnw"], ctx["qnb"]), (kb, ctx["knw"], ctx["knb"]))):
            if self.dbg and jp == 0 and qk == 0:
                self._dbg_ln = "qln0"
                self._dbg_rope = True
            r3 = self.ln64_rope_l1(buf, cps[qk], wsb, bsb, ctx, W2, bank_busy,
                                   rope_prev)
            rope_prev = r3
            rtoks.append(r3)
        state["rope_sc"] = rope_prev
        mx, chb, sinkx = ctx["mxs"][par], ctx["chbs"][par], ctx["sinkxs"][par]
        mxsc, sq = ctx["mxsc"], ctx["sq"]
        tmx = None
        for qk, (buf, tin) in enumerate(((qb, rtoks[0]), (kb, rtoks[1]))):
            tsq = p.a(lambda e, buf=buf: e.activation(sq[:, :], buf[:, :], AF.Square),
                      wait=[tin] + ([tmx] if tmx else []))
            for hh in range(2):
                g = 2 * qk + hh
                red = None
                for cc in range(4):
                    b = cc % 2
                    pnb = W2[b]
                    mmn = p.t(lambda e, hh=hh, cc=cc, pnb=pnb: e.matmul(
                        pnb[0:1, :], self.ones_col_bf[hh * 64:(hh + 1) * 64, :],
                        sq[hh * 64:(hh + 1) * 64, cc * 512:(cc + 1) * 512],
                        start=True, stop=True),
                        wait=[tsq] + ([bank_busy[b]] if bank_busy[b] else []))
                    red = p.v(lambda e, g=g, cc=cc, pnb=pnb: e.reduce_max(
                        mxsc[:, g * 4 + cc:g * 4 + cc + 1], pnb[0:1, :], axis=AXX),
                        wait=[mmn])
                    bank_busy[b] = red
                tmx = p.v(lambda e, g=g, mx=mx: e.reduce_max(
                    mx[:, g:g + 1], mxsc[:, g * 4:g * 4 + 4], axis=AXX),
                    wait=[red])
        pcb = W2[0]
        bc = []
        sk_toks = []
        for hh in range(2):
            n1 = p.v(lambda e, hh=hh, mx=mx: e.tensor_tensor(
                mx[:, 4 + hh:5 + hh], mx[:, hh:hh + 1], mx[:, 2 + hh:3 + hh],
                ALU.mult), wait=[tmx])
            n2 = p.a(lambda e, hh=hh, mx=mx: e.activation(
                mx[:, 4 + hh:5 + hh], mx[:, 4 + hh:5 + hh], AF.Sqrt), wait=[n1])
            ng = p.v(lambda e, hh=hh, mx=mx: e.tensor_scalar(
                mx[:, 4 + hh:5 + hh], mx[:, 4 + hh:5 + hh], -CSC, None,
                ALU.mult), wait=[n2])
            mmb = p.t(lambda e, hh=hh, mx=mx: e.matmul(
                pcb[:, hh:hh + 1], self.ones_row_f[:, :], mx[:, 4 + hh:5 + hh],
                start=True, stop=True),
                wait=[ng] + ([bank_busy[0]] if hh == 0 and bank_busy[0] else []))
            bc.append(p.v(lambda e, hh=hh, chb=chb: e.tensor_copy(
                chb[:, hh:hh + 1], pcb[:, hh:hh + 1]), wait=[mmb]))
            sk_toks.append(p.a(lambda e, hh=hh, jp=jp, sinkx=sinkx, mx=mx: e.activation(
                sinkx[:, hh:hh + 1],
                ctx["sink_sb"][:, 2 * jp + hh:2 * jp + hh + 1], AF.Exp,
                bias=mx[:, 4 + hh:5 + hh]), wait=[ng]))
        bank_busy[0] = bc[-1]
        state["Afree"][par] = bank_busy
        if self.dbg and jp == 0:
            self.dump("q0", qb[:, :], [128, T], BF16)
            self.dump("k0", kb[:, :], [128, T], BF16)
            self.dump("mx0", mx[:, :], [1, 8], F32)
            self.dump("chb0", chb[:, :], [128, 2], F32)
        return {"bc": bc[-1], "sk": sk_toks, "par": par}

    def ln64_rope_l1(self, buf, cp_toks, wsb, bsb, ctx, W2, bank_busy, rope_prev):
        p = self.p
        blk2b, blk2Tb = ctx["blk2b"], ctx["blk2Tb"]
        sq16, ropeA, ropeB = ctx["sq16"], ctx["ropeA"], ctx["ropeB"]
        cosb, sinb, shufb = ctx["cosb"], ctx["sinb"], ctx["shufb"]
        t1s = [None] * 4
        t2s = [None] * 4
        t5s = [None] * 4
        ua_toks = [None] * 4
        for cc in range(5):
            if cc < 4:
                b = cc % 2
                pp = W2[b]
                mu, vr, m2, rs = (ctx["lnmu"][b], ctx["lnvr"][b], ctx["lnm2"][b],
                                  ctx["lnrs"][b])
                mm1 = p.t(lambda e, cc=cc, pp=pp: e.matmul(
                    pp[0:2, :], blk2b[:, :], buf[:, cc * 512:(cc + 1) * 512],
                    start=True, stop=True),
                    wait=[cp_toks[cc]] + ([bank_busy[b]] if bank_busy[b] else []) +
                    ([rope_prev] if rope_prev and cc < 2 else []))
                tsq = p.a(lambda e, cc=cc: e.activation(
                    sq16[:, cc * 512:(cc + 1) * 512], buf[:, cc * 512:(cc + 1) * 512],
                    AF.Square), wait=[cp_toks[cc]] + ([rope_prev] if rope_prev and cc < 2 else []))
                mm2 = p.t(lambda e, cc=cc, pp=pp: e.matmul(
                    pp[32:34, :], blk2b[:, :], sq16[:, cc * 512:(cc + 1) * 512],
                    start=True, stop=True), wait=[tsq])
                t1 = p.v(lambda e, mu=mu, pp=pp: e.tensor_scalar(
                    mu[:, :], pp[0:2, :], 1.0 / 64.0, None, ALU.mult),
                    wait=[mm1])
                t2 = p.v(lambda e, vr=vr, pp=pp: e.tensor_scalar(
                    vr[:, :], pp[32:34, :], 1.0 / 64.0, None, ALU.mult),
                    wait=[mm2])
                t3 = p.v(lambda e, m2=m2, mu=mu: e.tensor_tensor(
                    m2[:, :], mu[:, :], mu[:, :], ALU.mult), wait=[t1])
                t4 = p.v(lambda e, vr=vr, m2=m2: e.tensor_tensor(
                    vr[:, :], vr[:, :], m2[:, :], ALU.subtract), wait=[t2, t3])
                t5a = p.a(lambda e, vr=vr: e.activation(
                    vr[:, :], vr[:, :], AF.Ln,
                    bias=self.epsb[0:2, 0:1]), wait=[t4])
                t5s[cc] = p.a(lambda e, vr=vr, rs=rs: e.activation(
                    rs[:, :], vr[:, :], AF.Exp,
                    scale=ctx["nhalf2"][:, :]), wait=[t5a])
                t1s[cc] = t1
                t2s[cc] = t2
            c2 = cc - 1
            if c2 >= 0:
                b = c2 % 2
                pp = W2[b]
                mu, rs = ctx["lnmu"][b], ctx["lnrs"][b]
                bmu = p.t(lambda e, mu=mu, pp=pp: e.matmul(
                    pp[:, :], blk2Tb[:, :], mu[:, :],
                    start=True, stop=True), wait=[t1s[c2], t2s[c2]])
                u1 = p.v(lambda e, c2=c2, pp=pp: e.tensor_tensor(
                    ropeA[:, c2 * 512:(c2 + 1) * 512], buf[:, c2 * 512:(c2 + 1) * 512],
                    pp[:, :], ALU.subtract), wait=[bmu])
                brs = p.t(lambda e, rs=rs, pp=pp: e.matmul(
                    pp[:, :], blk2Tb[:, :], rs[:, :],
                    start=True, stop=True), wait=[u1, t5s[c2]])
                u2 = p.v(lambda e, c2=c2, pp=pp: e.tensor_tensor(
                    ropeB[:, c2 * 512:(c2 + 1) * 512], ropeA[:, c2 * 512:(c2 + 1) * 512],
                    pp[:, :], ALU.mult), wait=[brs])
                ua_toks[c2] = p.v(lambda e, c2=c2: e.tensor_scalar(
                    buf[:, c2 * 512:(c2 + 1) * 512], ropeB[:, c2 * 512:(c2 + 1) * 512],
                    wsb[:, :], bsb[:, :], ALU.mult, ALU.add), wait=[u2])
                bank_busy[b] = u2
        if getattr(self, "_dbg_ln", None):
            self.p.op("sync", None, wait=ua_toks)
            self.dump(self._dbg_ln, buf[:, :], [128, T], BF16)
            self._dbg_ln = None
        sh = None
        for blk in range(4):
            src = blk + 1 if blk % 2 == 0 else blk - 1
            sh = self.p.s(lambda e, blk=blk, src=src: e.dma_start(
                out=shufb[blk * 32:(blk + 1) * 32, :],
                in_=buf[src * 32:(src + 1) * 32, :]), wait=ua_toks)
        r1 = p.v(lambda e: e.tensor_tensor(ropeA[:, :], buf[:, :], cosb[:, :],
                                           ALU.mult), wait=ua_toks)
        r2 = p.v(lambda e: e.tensor_tensor(ropeB[:, :], shufb[:, :], sinb[:, :],
                                           ALU.mult), wait=[sh])
        r3 = p.v(lambda e: e.tensor_tensor(buf[:, :], ropeA[:, :], ropeB[:, :],
                                           ALU.add), wait=[r1, r2])
        if getattr(self, "_dbg_rope", None):
            self.p.op("sync", None, wait=[r3])
            self.dump("shufd", shufb[:, :], [128, T], BF16)
            self.dump("cosd", cosb[:, :], [128, T], F32)
            self.dump("sind", sinb[:, :], [128, T], F32)
            self.dump("ropeAd", ropeA[:, :], [128, T], F32)
            self.dump("ropeBd", ropeB[:, :], [128, T], F32)
            self._dbg_rope = None
        return r3

    def attn_pB_l1(self, jp, ctx, hand, btail):
        par = hand["par"]
        ctx2 = dict(ctx)
        ctx2["qb"] = ctx["qbs"][par]
        ctx2["kb"] = ctx["kbs"][par]
        ctx2["chb"] = ctx["chbs"][par]
        ctx2["sinkx"] = ctx["sinkxs"][par]
        hp = (ctx["stp"], ctx["pvP"], ctx["pabP"])
        t0 = self.head_l1(2 * jp, 0, ctx2, hand["bc"], btail, hand["sk"][0], hp)
        t1 = self.head_l1(2 * jp + 1, 1, ctx2, hand["bc"], t0, hand["sk"][1], hp)
        return t1

    def attn_pair(self, layer, jp, ctx, prev_tail):
        p = self.p
        L1 = (layer == 1)
        qb, kb, pairw, sq = ctx["qb"], ctx["kb"], ctx["pairw"], ctx["sq"]
        nrm, mx, chb = ctx["nrm"], ctx["mx"], ctx["chb"]
        wsrc = self.inp["qk_w"] if not L1 else self.inp["qkv_wT"]
        wq = self.dma_c(pairw.rearrange("p (d two c) -> p d two c", d=ND, two=2)[:, :, 0, :],
                        wsrc[:, jp * 128:(jp + 1) * 128]
                        .rearrange("(dt p) c -> p dt c", p=128),
                        wait=[prev_tail] if prev_tail else (), q="dpw")
        wk = self.dma_c(pairw.rearrange("p (d two c) -> p d two c", d=ND, two=2)[:, :, 1, :],
                        wsrc[:, 1024 + jp * 128:1024 + (jp + 1) * 128]
                        .rearrange("(dt p) c -> p dt c", p=128),
                        wait=[prev_tail] if prev_tail else (), q="dpw")
        pq = p.psum("pq", [128, T], F32)
        pk = p.psum("pk", [128, T], F32)
        qmm = kmm = None
        for c in range(4):
            for d in range(ND):
                qmm = p.t(lambda e, c=c, d=d: e.matmul(
                    pq[:, c * 512:(c + 1) * 512], pairw[:, d * 256:d * 256 + 128],
                    self.B1[:, d * T + c * 512:d * T + (c + 1) * 512],
                    start=(d == 0), stop=(d == ND - 1)),
                    wait=[wq, ctx["vready"]] + ([prev_tail] if prev_tail else []))
        for c in range(4):
            for d in range(ND):
                kmm = p.t(lambda e, c=c, d=d: e.matmul(
                    pk[:, c * 512:(c + 1) * 512], pairw[:, d * 256 + 128:d * 256 + 256],
                    self.B1[:, d * T + c * 512:d * T + (c + 1) * 512],
                    start=(d == 0), stop=(d == ND - 1)), wait=[wk])
        if not L1:
            tq = p.v(lambda e, jp=jp: e.tensor_scalar(
                qb[:, :], pq[:, :], ctx["qkb"][:, jp:jp + 1], None, ALU.add),
                wait=[qmm])
            tk = p.v(lambda e, jp=jp: e.tensor_scalar(
                kb[:, :], pk[:, :], ctx["qkb"][:, 8 + jp:9 + jp], None, ALU.add),
                wait=[kmm])
        else:
            tq = p.a(lambda e: e.activation(qb[:, :], pq[:, :], AF.Copy), wait=[qmm])
            tk = p.a(lambda e: e.activation(kb[:, :], pk[:, :], AF.Copy), wait=[kmm])
        p.free(pq, pk)
        if L1:
            tq = self.qk_ln_rope(qb, tq, ctx["qnw"], ctx["qnb"], ctx, None)
            tk = self.qk_ln_rope(kb, tk, ctx["knw"], ctx["knb"], ctx, tq)
        pn = p.psum("pn", [1, T], F32)
        tmx = None
        free_pn = None
        for qk, (buf, tin) in enumerate(((qb, tq), (kb, tk))):
            tsq = p.a(lambda e, buf=buf: e.activation(sq[:, :], buf[:, :], AF.Square),
                      wait=[tin, tk if L1 else None] + ([tmx] if tmx else []))
            for hh in range(2):
                mmn = None
                for c in range(4):
                    mmn = p.t(lambda e, hh=hh, c=c: e.matmul(
                        pn[:, c * 512:(c + 1) * 512],
                        self.ones_col_bf[hh * 64:(hh + 1) * 64, :],
                        sq[hh * 64:(hh + 1) * 64, c * 512:(c + 1) * 512],
                        start=True, stop=True),
                        wait=[tsq] + ([free_pn] if free_pn and c == 0 else []))
                free_pn = p.v(lambda e, qk=qk, hh=hh: e.reduce_max(
                    mx[:, 2 * qk + hh:2 * qk + hh + 1], pn[:, :], axis=AXX),
                    wait=[mmn])
                tmx = free_pn
        p.free(pn)
        pcb = p.psum("pcb", [128, 2], F32)
        bc = []
        sk_toks = []
        for hh in range(2):
            n1 = p.v(lambda e, hh=hh: e.tensor_tensor(
                mx[:, 4 + hh:5 + hh], mx[:, hh:hh + 1], mx[:, 2 + hh:3 + hh],
                ALU.mult), wait=[tmx])
            n2 = p.a(lambda e, hh=hh: e.activation(
                mx[:, 4 + hh:5 + hh], mx[:, 4 + hh:5 + hh], AF.Sqrt), wait=[n1])
            ng = p.v(lambda e, hh=hh: e.tensor_scalar(
                mx[:, 4 + hh:5 + hh], mx[:, 4 + hh:5 + hh], -CSC, None,
                ALU.mult), wait=[n2])
            mmb = p.t(lambda e, hh=hh: e.matmul(
                pcb[:, hh:hh + 1], self.ones_row_f[:, :], mx[:, 4 + hh:5 + hh],
                start=True, stop=True), wait=[ng] + (bc[-1:] if bc else []))
            bc.append(p.v(lambda e, hh=hh: e.tensor_copy(chb[:, hh:hh + 1],
                                                         pcb[:, hh:hh + 1]), wait=[mmb]))
            if L1:
                sk_toks.append(p.a(lambda e, hh=hh, jp=jp: e.activation(
                    ctx["sinkx"][:, hh:hh + 1],
                    ctx["sink_sb"][:, 2 * jp + hh:2 * jp + hh + 1], AF.Exp,
                    bias=mx[:, 4 + hh:5 + hh]), wait=[ng]))
        p.free(pcb)
        stage = getattr(self, "attn_stage", None)
        nhh = 0 if stage == "pairs" else (1 if stage == "heads1" else 2)
        cs_toks = [None, None]
        if not L1 and nhh:
            ptv = p.psum("ptv", [64, 512], BF16)
            ev_prev = None
            for hh in range(nhh):
                h = 2 * jp + hh
                vTh = ctx["vTh"][hh]
                for g4 in range(4):
                    last = None
                    for i4 in range(4):
                        kt = g4 * 4 + i4
                        last = p.t(lambda e, kt=kt, i4=i4, h=h: e.transpose(
                            ptv[:, i4 * 128:(i4 + 1) * 128],
                            ctx["vt"][:, kt * H * VA + h * VA:kt * H * VA + h * VA + 64],
                            self.ident[:, :]),
                            wait=[ctx["vready"], bc[-1], prev_tail] +
                            ([ev_prev] if ev_prev and i4 == 0 else []))
                    ev_prev = p.v(lambda e, g4=g4, vTh=vTh: e.tensor_copy(
                        vTh[:, g4 * 512:(g4 + 1) * 512], ptv[:, :]), wait=[last])
                cs_toks[hh] = p.v(lambda e, hh=hh: e.tensor_tensor_scan(
                    ctx["csh"][hh][:, :], ctx["vTh"][hh][:, :], ctx["vTh"][hh][:, :],
                    ctx["zero64"][:, :], ALU.add, ALU.bypass), wait=[ev_prev])
            ctx["cs_gate"] = cs_toks[nhh - 1]
            p.free(ptv)
        hps = []
        for hh in range(nhh):
            hps.append(([p.psum(f"st{hh}A", [128, 512], F32),
                         p.psum(f"st{hh}B", [128, 512], F32)],
                        p.psum(f"pv{hh}", [VA, 512], F32),
                        p.psum(f"pab{hh}", [64, 512], F32)))
        tails = []
        if nhh == 1:
            h = 2 * jp
            if not L1:
                tails.append(self.head_l0(h, 0, ctx, bc[-1], prev_tail, cs_toks[0],
                                          hps[0]))
            else:
                tails.append(self.head_l1(h, 0, ctx, bc[-1], prev_tail, sk_toks[0],
                                          hps[0]))
        elif nhh == 2:
            if not L1:
                tails = self.twin_heads(0, jp, ctx, bc[-1], prev_tail, cs_toks,
                                        None, hps)
            else:
                tails = self.twin_heads(1, jp, ctx, bc[-1], prev_tail, None,
                                        sk_toks, hps)
        for (stp, pv, pab) in reversed(hps):
            p.free(pab, pv, stp[1], stp[0])
        if nhh == 0:
            return bc[-1]
        return tails[-1] if len(tails) == 1 else self._join(tails)

    def _join(self, toks):
        sems = {t[0] for t in toks}
        if len(sems) == 1:
            return max(toks, key=lambda t: t[1])
        return self.p.v(lambda e: e.memset(self._joinbuf[:, :], 0.0), wait=toks)

    def qk_ln_rope(self, buf, tin, wsb, bsb, ctx, prev_ln):
        p = self.p
        sq16, lntA, lntB = ctx["sq16"], ctx["lntA"], ctx["lntB"]
        blk2b, blk2T, cosb, sinb, shufb = (ctx["blk2b"], ctx["blk2T"], ctx["cosb"],
                                           ctx["sinb"], ctx["shufb"])
        mu = lntA[0:2, 0:T]
        vr = lntB[0:2, 0:T]
        m2 = shufb[0:2, 0:T]
        ps = p.psum("l1s", [2, T], F32)
        px = p.psum("l1x", [2, T], F32)
        mm1 = mm2 = None
        for c in range(4):
            mm1 = p.t(lambda e, c=c: e.matmul(ps[:, c * 512:(c + 1) * 512], blk2b[:, :],
                                              buf[:, c * 512:(c + 1) * 512],
                                              start=True, stop=True),
                      wait=[tin, prev_ln])
        tsq = p.a(lambda e: e.activation(sq16[:, :], buf[:, :], AF.Square),
                  wait=[tin, prev_ln])
        for c in range(4):
            mm2 = p.t(lambda e, c=c: e.matmul(px[:, c * 512:(c + 1) * 512], blk2b[:, :],
                                              sq16[:, c * 512:(c + 1) * 512],
                                              start=True, stop=True), wait=[tsq])
        t1 = p.v(lambda e: e.tensor_scalar(mu, ps[:, :], 1.0 / 64.0, None,
                                           ALU.mult), wait=[mm1])
        t2 = p.v(lambda e: e.tensor_scalar(vr, px[:, :], 1.0 / 64.0, None,
                                           ALU.mult), wait=[mm2])
        t3 = p.v(lambda e: e.tensor_tensor(m2, mu, mu, ALU.mult), wait=[t1])
        t4 = p.v(lambda e: e.tensor_tensor(vr, vr, m2, ALU.subtract), wait=[t2, t3])
        t5 = p.a(lambda e: e.activation(vr, vr, AF.Sqrt,
                                        bias=self.epsb[0:2, 0:1]), wait=[t4])
        t6 = p.v(lambda e: e.reciprocal(vr, vr), wait=[t5])
        p.free(px, ps)
        pmu = p.psum("pmu", [128, T], F32)
        prs = p.psum("prs", [128, T], F32)
        bmu = brs = None
        for c in range(4):
            bmu = p.t(lambda e, c=c: e.matmul(pmu[:, c * 512:(c + 1) * 512], blk2T[:, :],
                                              mu[:, c * 512:(c + 1) * 512],
                                              start=True, stop=True), wait=[t1, t6])
            brs = p.t(lambda e, c=c: e.matmul(prs[:, c * 512:(c + 1) * 512], blk2T[:, :],
                                              vr[:, c * 512:(c + 1) * 512],
                                              start=True, stop=True), wait=[t6])
        u1 = p.v(lambda e: e.tensor_tensor(lntA[:, :], buf[:, :], pmu[:, :],
                                           ALU.subtract), wait=[bmu, tin])
        u2 = p.v(lambda e: e.tensor_tensor(lntB[:, :], lntA[:, :], prs[:, :], ALU.mult),
                 wait=[u1, brs])
        ua = p.v(lambda e: e.tensor_scalar(buf[:, :], lntB[:, :], wsb[:, :],
                                           bsb[:, :], ALU.mult, ALU.add), wait=[u2])
        p.free(prs, pmu)
        sh = None
        for blk in range(4):
            src = blk + 1 if blk % 2 == 0 else blk - 1
            sh = self.p.s(lambda e, blk=blk, src=src: e.dma_start(
                out=shufb[blk * 32:(blk + 1) * 32, :],
                in_=buf[src * 32:(src + 1) * 32, :]), wait=[ua])
        r1 = p.v(lambda e: e.tensor_tensor(lntA[:, :], buf[:, :], cosb[:, :], ALU.mult),
                 wait=[ua])
        r2 = p.v(lambda e: e.tensor_tensor(lntB[:, :], shufb[:, :], sinb[:, :], ALU.mult),
                 wait=[sh])
        r3 = p.v(lambda e: e.tensor_tensor(buf[:, :], lntA[:, :], lntB[:, :], ALU.add),
                 wait=[r1, r2])
        return r3

    def head_l0(self, h, hh, ctx, tok_ch, head_prev, t_cs_in, hp):
        p = self.p
        base = 64 * hh
        qb, kb, vt = ctx["qb"], ctx["kb"], ctx["vt"]
        pT, invr, ev64, t1b = (ctx["pT"][hh], ctx["invr"][hh], ctx["ev64"][hh],
                               ctx["t1b"][hh])
        chb = ctx["chb"]
        vTh, csh, zb = ctx["vTh"][hh], ctx["csh"][hh], ctx["zb"][hh]
        invidx = ctx["invidx"]
        hstage = getattr(self, "head_stage", None)
        if hstage == "a":
            return t_cs_in
        nqc = 1 if hstage in ("b", "c") else 4
        (stp, pv, pab) = hp
        exp_tok = [None, None]
        pv_slot = [None, None]
        pv_free = head_prev
        pab_free = head_prev
        comb_tok = None
        tail = None
        kglob = 0
        for qc in range(nqc):
            pv_last = None
            nkt = 4 * qc + 4
            for kt in range(nkt):
                r = kt - 4 * qc
                lo = max(0, r * 128)
                sl = kglob % 2
                kglob += 1
                st = stp[sl]
                mm = p.t(lambda e, kt=kt, qc=qc, lo=lo, st=st: e.matmul(
                    st[:, lo:512], kb[base:base + 64, kt * 128:(kt + 1) * 128],
                    qb[base:base + 64, qc * 512 + lo:(qc + 1) * 512],
                    start=True, stop=True),
                    wait=[tok_ch] + ([exp_tok[sl]] if exp_tok[sl] else
                                     [head_prev, t_cs_in, ctx.get("cs_gate")]))
                if r >= 0:
                    mm = p.v(lambda e, st=st, lo=lo: e.tensor_tensor(
                        st[:, lo:lo + 128], st[:, lo:lo + 128], self.m_kgtq[:, :],
                        ALU.add), wait=[mm])
                exp_tok[sl] = p.a(lambda e, st=st, lo=lo, sl=sl: e.activation(
                    pT[:, sl * 512 + lo:(sl + 1) * 512], st[:, lo:512], AF.Exp,
                    bias=chb[:, hh:hh + 1], scale=self.c0125[:, :]),
                    wait=[mm] + ([pv_slot[sl]] if pv_slot[sl] else []))
                pv_last = p.t(lambda e, kt=kt, h=h, lo=lo, sl=sl, nkt=nkt: e.matmul(
                    pv[:, lo:512],
                    vt[:, kt * H * VA + h * VA:kt * H * VA + (h + 1) * VA],
                    pT[:, sl * 512 + lo:(sl + 1) * 512],
                    start=(kt == 0), stop=(kt == nkt - 1)),
                    wait=[exp_tok[sl], ctx["vready"]] +
                    ([pv_free] if pv_free and kt == 0 else []))
                pv_slot[sl] = pv_last
            if hstage == "b":
                tail = exp_tok[0]
                break
            q0 = qc * 512
            ti = p.v(lambda e: e.reciprocal(invr[:, :], pv[64:65, :]),
                     wait=[pv_last] + ([comb_tok] if comb_tok else []))
            ta = p.v(lambda e, h=h: e.tensor_scalar(
                invr[:, :], invr[:, :], ctx["resid64"][0:1, h:h + 1], None, ALU.mult),
                wait=[ti])
            tb = p.t(lambda e: e.matmul(pab[:, :], self.ones_row_f[:, 0:64], invr[:, :],
                                        start=True, stop=True),
                     wait=[ta] + ([pab_free] if pab_free else []))
            te = p.v(lambda e: e.tensor_copy(ev64[:, :], pv[0:64, :]), wait=[ti])
            pv_free = te
            t1 = p.v(lambda e: e.tensor_tensor(t1b[:, :], ev64[:, :], pab[:, :],
                                               ALU.mult), wait=[te, tb])
            pab_free = t1
            tz = p.v(lambda e, q0=q0: e.tensor_tensor(
                zb[:, :], csh[:, q0:q0 + 512], invidx[:, q0:q0 + 512],
                ALU.mult), wait=[t_cs_in] + ([comb_tok] if comb_tok else []))
            tu = p.v(lambda e, h=h: e.scalar_tensor_tensor(
                zb[:, :], zb[:, :], ctx["negc64"][:, h:h + 1],
                t1b[:, :], ALU.mult, ALU.add), wait=[tz, t1])
            comb_tok = p.v(lambda e, q0=q0, h=h: e.scalar_tensor_tensor(
                ctx["aoT"][base:base + 64, (h // 2) * T + q0:(h // 2) * T + q0 + 512],
                vTh[:, q0:q0 + 512], ctx["skip64"][:, h:h + 1], zb[:, :],
                ALU.mult, ALU.add), wait=[tu])
            tail = comb_tok
        return tail

    def head_l1(self, h, hh, ctx, tok_ch, head_prev, tok_sink, hp):
        p = self.p
        base = 64 * hh
        qb, kb, vt = ctx["qb"], ctx["kb"], ctx["vt"]
        pT, invr, ev64 = ctx["pT"][hh], ctx["invr"][hh], ctx["ev64"][hh]
        chb, sinkx = ctx["chb"], ctx["sinkx"]
        (stp, pv, pab) = hp
        exp_tok = [head_prev, head_prev]
        pv_slot = [None, None]
        pv_free = head_prev
        pab_free = head_prev
        comb_tok = None
        tail = None
        kglob = 0
        for qc in range(4):
            pv_last = None
            kts = list(range(max(0, 4 * qc - 2), 4 * qc + 4))
            for kt in kts:
                r = kt - 4 * qc
                lo = max(0, kt * 128 - qc * 512)
                hi = min(512, kt * 128 + 384 - qc * 512)
                sl = kglob % 2
                kglob += 1
                st = stp[sl]
                mm = p.t(lambda e, kt=kt, qc=qc, lo=lo, hi=hi, st=st: e.matmul(
                    st[:, lo:hi], kb[base:base + 64, kt * 128:(kt + 1) * 128],
                    qb[base:base + 64, qc * 512 + lo:qc * 512 + hi],
                    start=True, stop=True),
                    wait=[tok_ch] + ([exp_tok[sl]] if exp_tok[sl] else []))
                if r >= 0:
                    mm = p.v(lambda e, st=st, r=r: e.tensor_tensor(
                        st[:, r * 128:(r + 1) * 128], st[:, r * 128:(r + 1) * 128],
                        self.m_kgtq[:, :], ALU.add), wait=[mm])
                kcol = (r + 2) * 128
                if kcol < hi:
                    mm = p.v(lambda e, st=st, kcol=kcol: e.tensor_tensor(
                        st[:, kcol:kcol + 128], st[:, kcol:kcol + 128],
                        self.m_kleq[:, :], ALU.add), wait=[mm])
                first = (kt == kts[0])
                if first:
                    tz0 = p.v(lambda e, sl=sl: e.memset(
                        pT[:, sl * 512:(sl + 1) * 512], 0.0),
                        wait=[pv_slot[sl]] if pv_slot[sl] else [head_prev])
                else:
                    tz0 = None
                exp_tok[sl] = p.a(lambda e, st=st, lo=lo, hi=hi, sl=sl: e.activation(
                    pT[:, sl * 512 + lo:sl * 512 + hi], st[:, lo:hi], AF.Exp,
                    bias=chb[:, hh:hh + 1], scale=self.c0125[:, :]),
                    wait=[mm, tz0] + ([pv_slot[sl]] if pv_slot[sl] else []))
                mlo, mhi = (0, 512) if first else (lo, hi)
                pv_last = p.t(lambda e, kt=kt, h=h, mlo=mlo, mhi=mhi, sl=sl,
                              first=first, last_=(kt == kts[-1]): e.matmul(
                    pv[:, mlo:mhi],
                    vt[:, kt * H * VA + h * VA:kt * H * VA + (h + 1) * VA],
                    pT[:, sl * 512 + mlo:sl * 512 + mhi],
                    start=first, stop=last_),
                    wait=[exp_tok[sl], ctx["vready"]] +
                    ([pv_free] if pv_free and kt == kts[0] else []))
                pv_slot[sl] = pv_last
            q0 = qc * 512
            tl_ = p.a(lambda e, hh=hh: e.activation(
                invr[:, :], pv[64:65, :], AF.Ln, bias=sinkx[:, hh:hh + 1]),
                wait=[pv_last, tok_sink] + ([comb_tok] if comb_tok else []))
            ti = p.a(lambda e: e.activation(
                invr[:, :], invr[:, :], AF.Exp, scale=ctx["neg1"][:, :]),
                wait=[tl_])
            tb = p.t(lambda e: e.matmul(pab[:, :], self.ones_row_f[:, 0:64], invr[:, :],
                                        start=True, stop=True),
                     wait=[ti] + ([pab_free] if pab_free else []))
            te = p.v(lambda e: e.tensor_copy(ev64[:, :], pv[0:64, :]), wait=[ti])
            pv_free = te
            comb_tok = p.v(lambda e, q0=q0, h=h: e.tensor_tensor(
                ctx["aoT"][base:base + 64, (h // 2) * T + q0:(h // 2) * T + q0 + 512],
                ev64[:, :], pab[:, :], ALU.mult), wait=[te, tb])
            pab_free = comb_tok
            tail = comb_tok
        return tail


    def twin_heads(self, layer, jp, ctx, tok_ch, head_prev, cs_toks, sk_toks, hps):
        p = self.p
        L1 = (layer == 1)
        qb, kb, vt = ctx["qb"], ctx["kb"], ctx["vt"]
        chb = ctx["chb"]
        st_of = [hps[0][0], hps[1][0]]
        pv_of = [hps[0][1], hps[1][1]]
        pab_of = [hps[0][2], hps[1][2]]
        exp_tok = [[None, None], [None, None]]
        pv_slot = [[None, None], [None, None]]
        pv_free = [head_prev, head_prev]
        pab_free = [head_prev, head_prev]
        comb_tok = [None, None]
        tails = [None, None]
        kglob = [0, 0]
        if L1:
            kt_lists = [list(range(max(0, 4 * qc - 2), 4 * qc + 4)) for qc in range(4)]
        else:
            kt_lists = [list(range(4 * qc + 4)) for qc in range(4)]
        for qc in range(4):
            kts = kt_lists[qc]
            pv_last = [None, None]
            for kt in kts:
                for hh in range(2):
                    h = 2 * jp + hh
                    base = 64 * hh
                    stp = st_of[hh]
                    pv = pv_of[hh]
                    pT = ctx["pT"][hh]
                    r = kt - 4 * qc
                    sl = kglob[hh] % 2
                    kglob[hh] += 1
                    st = stp[sl]
                    if L1:
                        lo = max(0, kt * 128 - qc * 512)
                        hi = min(512, kt * 128 + 384 - qc * 512)
                    else:
                        lo = max(0, r * 128)
                        hi = 512
                    mm = p.t(lambda e, kt=kt, qc=qc, lo=lo, hi=hi, st=st, base=base:
                             e.matmul(
                        st[:, lo:hi], kb[base:base + 64, kt * 128:(kt + 1) * 128],
                        qb[base:base + 64, qc * 512 + lo:qc * 512 + hi],
                        start=True, stop=True),
                        wait=[tok_ch] + ([exp_tok[hh][sl]] if exp_tok[hh][sl] else
                                         [head_prev, ctx.get("cs_gate")]))
                    if r >= 0:
                        mm = p.v(lambda e, st=st, r=r: e.tensor_tensor(
                            st[:, r * 128:(r + 1) * 128], st[:, r * 128:(r + 1) * 128],
                            self.m_kgtq[:, :], ALU.add), wait=[mm])
                    if L1:
                        kcol = (r + 2) * 128
                        if kcol < hi:
                            mm = p.v(lambda e, st=st, kcol=kcol: e.tensor_tensor(
                                st[:, kcol:kcol + 128], st[:, kcol:kcol + 128],
                                self.m_kleq[:, :], ALU.add), wait=[mm])
                    first = (kt == kts[0])
                    tz0 = None
                    if L1 and first:
                        tz0 = p.v(lambda e, sl=sl, pT=pT: e.memset(
                            pT[:, sl * 512:(sl + 1) * 512], 0.0),
                            wait=[pv_slot[hh][sl]] if pv_slot[hh][sl] else [head_prev])
                    exp_tok[hh][sl] = p.a(lambda e, st=st, lo=lo, hi=hi, sl=sl, pT=pT,
                                          hh=hh: e.activation(
                        pT[:, sl * 512 + lo:sl * 512 + hi], st[:, lo:hi], AF.Exp,
                        bias=chb[:, hh:hh + 1], scale=self.c0125[:, :]),
                        wait=[mm, tz0] + ([pv_slot[hh][sl]] if pv_slot[hh][sl] else []))
                    if L1 and first:
                        mlo, mhi = 0, 512
                    else:
                        mlo, mhi = lo, hi
                    pv_last[hh] = p.t(lambda e, kt=kt, h=h, mlo=mlo, mhi=mhi, sl=sl,
                                      pv=pv, pT=pT, first=first,
                                      last_=(kt == kts[-1]): e.matmul(
                        pv[:, mlo:mhi],
                        vt[:, kt * H * VA + h * VA:kt * H * VA + (h + 1) * VA],
                        pT[:, sl * 512 + mlo:sl * 512 + mhi],
                        start=first, stop=last_),
                        wait=[exp_tok[hh][sl], ctx["vready"]] +
                        ([pv_free[hh]] if pv_free[hh] and first else []))
                    pv_slot[hh][sl] = pv_last[hh]
            q0 = qc * 512
            for hh in range(2):
                h = 2 * jp + hh
                base = 64 * hh
                pv = pv_of[hh]
                pab = pab_of[hh]
                invr, ev64, t1b = (ctx["invr"][hh], ctx["ev64"][hh],
                                   ctx["t1b"][hh] if not L1 else None)
                if not L1:
                    vTh, csh, zb = ctx["vTh"][hh], ctx["csh"][hh], ctx["zb"][hh]
                    tl_ = p.a(lambda e, pv=pv, invr=invr: e.activation(
                        invr[:, :], pv[64:65, :], AF.Ln),
                        wait=[pv_last[hh]] + ([comb_tok[hh]] if comb_tok[hh] else []))
                    ti = p.a(lambda e, invr=invr: e.activation(
                        invr[:, :], invr[:, :], AF.Exp, scale=ctx["neg1"][:, :]),
                        wait=[tl_])
                    ta = p.v(lambda e, h=h, invr=invr: e.tensor_scalar(
                        invr[:, :], invr[:, :], ctx["resid64"][0:1, h:h + 1], None,
                        ALU.mult), wait=[ti])
                    tb = p.t(lambda e, pab=pab, invr=invr: e.matmul(
                        pab[:, :], self.ones_row_f[:, 0:64], invr[:, :],
                        start=True, stop=True),
                        wait=[ta] + ([pab_free[hh]] if pab_free[hh] else []))
                    te = p.v(lambda e, pv=pv, ev64=ev64: e.tensor_copy(
                        ev64[:, :], pv[0:64, :]), wait=[ti])
                    pv_free[hh] = te
                    t1 = p.v(lambda e, t1b=t1b, ev64=ev64, pab=pab: e.tensor_tensor(
                        t1b[:, :], ev64[:, :], pab[:, :], ALU.mult), wait=[te, tb])
                    pab_free[hh] = t1
                    tz = p.v(lambda e, q0=q0, zb=zb, csh=csh: e.tensor_tensor(
                        zb[:, :], csh[:, q0:q0 + 512], ctx["invidx"][:, q0:q0 + 512],
                        ALU.mult),
                        wait=[cs_toks[hh]] + ([comb_tok[hh]] if comb_tok[hh] else []))
                    tu = p.v(lambda e, h=h, zb=zb, t1b=t1b: e.scalar_tensor_tensor(
                        zb[:, :], zb[:, :], ctx["negc64"][:, h:h + 1], t1b[:, :],
                        ALU.mult, ALU.add), wait=[tz, t1])
                    comb_tok[hh] = p.v(lambda e, q0=q0, h=h, base=base, zb=zb,
                                       vTh=vTh: e.scalar_tensor_tensor(
                        ctx["aoT"][base:base + 64,
                                   (h // 2) * T + q0:(h // 2) * T + q0 + 512],
                        vTh[:, q0:q0 + 512], ctx["skip64"][:, h:h + 1], zb[:, :],
                        ALU.mult, ALU.add), wait=[tu])
                else:
                    tr = p.v(lambda e, pv=pv, hh=hh, invr=invr: e.tensor_scalar(
                        invr[:, :], pv[64:65, :], ctx["sinkx"][:, hh:hh + 1], None,
                        ALU.add),
                        wait=[pv_last[hh], sk_toks[hh]] +
                        ([comb_tok[hh]] if comb_tok[hh] else []))
                    ti = p.v(lambda e, invr=invr: e.reciprocal(invr[:, :], invr[:, :]),
                             wait=[tr])
                    tb = p.t(lambda e, pab=pab, invr=invr: e.matmul(
                        pab[:, :], self.ones_row_f[:, 0:64], invr[:, :],
                        start=True, stop=True),
                        wait=[ti] + ([pab_free[hh]] if pab_free[hh] else []))
                    te = p.v(lambda e, pv=pv, ev64=ev64: e.tensor_copy(
                        ev64[:, :], pv[0:64, :]), wait=[tr])
                    pv_free[hh] = te
                    comb_tok[hh] = p.v(lambda e, q0=q0, h=h, base=base, ev64=ev64,
                                       pab=pab: e.tensor_tensor(
                        ctx["aoT"][base:base + 64,
                                   (h // 2) * T + q0:(h // 2) * T + q0 + 512],
                        ev64[:, :], pab[:, :], ALU.mult), wait=[te, tb])
                    pab_free[hh] = comb_tok[hh]
                tails[hh] = comb_tok[hh]
        return tails

    def proj_pass(self, layer, aoT, acc_ap=None, gate=None):
        p = self.p
        L1 = (layer == 1)
        mark = p.push()
        cw = p.sbuf("cw", [128, ND * D], BF16)
        wsrc = (self.inp["cproj_w"] if not L1 else self.inp["proj_wT"])
        tok_w = self.dma_w(cw[:, :], wsrc, ND, wait=[gate] if gate else ())
        pbr = None
        if L1:
            pbr = p.sbuf("pbr", [128, D], F32)
            self.dma_f(pbr[:, :], self.inp["projb_rep"],
                       wait=[gate] if gate else ())
        else:
            xin = p.sbuf("xin", [128, 2 * D], F32)
            oacc = p.sbuf("oacc", [128, 2 * D], F32)
        pc = [p.psum("pc0", [128, D], F32), p.psum("pc1", [128, D], F32)]
        ev_tok = [None, None]
        out_tok = [None, None]
        if gate is None:
            p.fence()
        mmt = None
        for t in range(NT):
            sl = t % 2
            if not L1:
                tok_x = self.dma_f(xin[:, sl * D:(sl + 1) * D],
                                   self.inp["x"][t * 128:(t + 1) * 128, :],
                                   wait=[out_tok[sl]] if out_tok[sl] else ())
            for dc in range(2):
                for j in range(ND):
                    mmt = p.t(lambda e, t=t, dc=dc, j=j, sl=sl: e.matmul(
                        pc[sl][:, dc * 512:(dc + 1) * 512],
                        aoT[:, j * T + t * 128:j * T + (t + 1) * 128],
                        cw[:, j * D + dc * 512:j * D + (dc + 1) * 512],
                        start=(j == 0), stop=(j == ND - 1)),
                        wait=[tok_w] + ([ev_tok[sl]] if ev_tok[sl] and dc == 0 and j == 0 else []) +
                        ([gate] if gate and t == 0 and dc == 0 and j == 0 else []))
            if not L1:
                tev = p.v(lambda e, sl=sl: e.tensor_tensor(
                    oacc[:, sl * D:(sl + 1) * D], xin[:, sl * D:(sl + 1) * D],
                    pc[sl][:, :], ALU.add), wait=[mmt, tok_x])
                ev_tok[sl] = tev
                out_tok[sl] = self.dma_f(self.acc0_hbm[t * 128:(t + 1) * 128, :],
                                         oacc[:, sl * D:(sl + 1) * D], wait=[tev])
            else:
                ev_tok[sl] = p.v(lambda e, t=t, sl=sl: e.tensor_tensor(
                    acc_ap(t), pc[sl][:, :], pbr[:, :], ALU.add),
                    wait=[mmt])
        if gate is None:
            p.fence()
        last_ev = ev_tok[(NT - 1) % 2]
        p.pop(mark)
        return mmt, last_ev


    def moe_chain(self):
        p = self.p
        self.w1_cons = [None, None]
        self.w2_cons = None
        self.cn_cons = None
        self.pb_free = None
        m = {}
        m["acch"] = p.sbuf("acch", [128, 8 * D], F32)
        m["hslab"] = p.sbuf("hslab", [128, 16 * 1024], BF16)
        m["w1b"] = p.sbuf("w1b", [128, 2 * ND * 512], BF16)
        m["w2b"] = p.sbuf("w2b", [128, 16 * 1024], BF16)
        m["lgb"] = p.sbuf("lgb", [128, 8], F32)
        m["lbb"] = p.sbuf("lbb", [128, 8], BF16)
        m["b1s"] = p.sbuf("b1s", [128, 32], F32)
        m["b1pp"] = p.sbuf("b1pp", [128, 32], F32)
        m["b2s"] = p.sbuf("b2s", [1, 1024], BF16)
        m["rwb"] = [p.sbuf("rwb0", [128, ND * 4], BF16),
                    p.sbuf("rwb1", [128, ND * 4], BF16)]
        m["rbs"] = [p.sbuf("rbs0", [128, 4], F32), p.sbuf("rbs1", [128, 4], F32)]
        m["srep"] = [p.sbuf("srep0", [128, 4], F32), p.sbuf("srep1", [128, 4], F32)]
        m["lgt"] = p.sbuf("lgt", [128, 4], F32)
        m["p4"] = p.sbuf("p4b", [128, 4], F32)
        m["sc1"] = p.sbuf("sc1", [128, 8], F32)
        m["tmask"] = p.sbuf("tmask", [128, 4], F32)
        m["comb"] = [p.sbuf("comb0", [128, 8 * 4], F32),
                     p.sbuf("comb1", [128, 8 * 4], F32)]
        m["scb"] = [p.sbuf("scb0", [128, 8], F32), p.sbuf("scb1", [128, 8], F32)]
        m["mus"] = [p.sbuf("mus0", [128, 8], F32), p.sbuf("mus1", [128, 8], F32)]
        m["stds"] = [p.sbuf("stds0", [128, 8], F32), p.sbuf("stds1", [128, 8], F32)]
        m["aa"] = p.sbuf("aab", [128, 1], F32)
        m["bb"] = p.sbuf("bbb", [128, 1], F32)
        m["stat"] = p.sbuf("mst", [128, 12], F32)
        m["mv"] = p.sbuf("mmv", [128, 2], F32)
        m["rs"] = p.sbuf("mrs", [128, 1], F32)
        m["nmr"] = p.sbuf("mnm", [128, 1], F32)
        self.mm = m
        self.rw_toks = [None, None]
        self.mmA_last = {}
        self.hbase_free = [None, None]
        self.ep_toks = {}
        self.xg_last = {}
        self.prep_out = {}
        self.pending_gate = None
        self.moe_stat_free = None
        self.moe_y_tail = None
        self.moe_prep(0)
        for hidx in range(8):
            self.moe_experts_half(hidx)
        flat = ([m["acch"], m["hslab"], m["w1b"], m["w2b"], m["lgb"], m["lbb"],
                 m["b1s"], m["b1pp"], m["b2s"]] + m["rwb"] + m["rbs"] + m["srep"]
                + [m["lgt"], m["p4"], m["sc1"], m["tmask"]] + m["comb"] + m["scb"]
                + m["mus"] + m["stds"] + [m["aa"], m["bb"], m["stat"], m["mv"],
                                          m["rs"], m["nmr"]])
        p.free(*flat)
        p.fence()

    def moe_prep(self, hidx):
        p = self.p
        m = self.mm
        l, th = divmod(hidx, 2)
        par = hidx % 2
        lp = l % 2
        t0 = th * 8
        b0 = par * 8192
        pgate = self.moe_y_tail
        if th == 0:
            gate = self.mmA_last.get(hidx - 3)
            tok_rw = self.dma_w(m["rwb"][lp][:, :], self.inp["rw"][l], ND,
                                wait=[gate] if gate else ())
            tok_rb = self.dma_f(m["rbs"][lp][:, :], self.inp["rb_rep"][l],
                                wait=[gate] if gate else ())
            sS = p.sbuf("sS", [1, 4], F32)
            pS = p.psum("pS", [1, 4], F32)
            mmS = None
            for d in range(ND):
                mmS = p.t(lambda e, d=d, lp=lp: e.matmul(
                    pS[:, :], self.ones_col_bf[:, :],
                    m["rwb"][lp][:, d * 4:(d + 1) * 4],
                    start=(d == 0), stop=(d == ND - 1)),
                    wait=[tok_rw] + ([pgate] if pgate and d == 0 else []))
            tS = p.v(lambda e: e.tensor_copy(sS[:, :], pS[:, :]), wait=[mmS])
            p.free(pS)
            pSr = p.psum("pSr", [128, 4], F32)
            mmSr = p.t(lambda e: e.matmul(pSr[:, :], self.ones_row_f[:, :], sS[:, :],
                                          start=True, stop=True), wait=[tS])
            tok_srep = p.v(lambda e, lp=lp: e.tensor_copy(m["srep"][lp][:, :],
                                                          pSr[:, :]), wait=[mmSr])
            p.free(pSr, sS)
            self.rw_toks[lp] = (tok_rb, tok_srep)
            pgate = None
        tok_rb, tok_srep = self.rw_toks[lp]
        stat, mv, rs, nmr = m["stat"], m["mv"], m["rs"], m["nmr"]
        mus, stds = m["mus"][par], m["stds"][par]
        epg = self.ep_toks.get(hidx - 2)
        xgg = self.xg_last.get(hidx - 2)
        xh_tok = []
        stat_free = self.moe_stat_free
        for tl in range(8):
            t = t0 + tl
            extra = ([epg[tl]] if epg else []) + ([xgg] if xgg and tl == 0 else [])
            p.v(lambda e, t=t: e.bn_stats(stat[:, 0:6],
                                          self.B2[:, t * D:t * D + 512]),
                wait=([stat_free] if stat_free else []) + extra)
            t_st = p.v(lambda e, t=t: e.bn_stats(stat[:, 6:12],
                                                 self.B2[:, t * D + 512:(t + 1) * D]),
                       wait=([stat_free] if stat_free else []) + extra)
            t_mv = p.v(lambda e: e.bn_aggr(mv[:, :], stat[:, :]), wait=[t_st])
            t_sd = p.a(lambda e, tl=tl: e.activation(stds[:, tl:tl + 1], mv[:, 1:2],
                                                     AF.Sqrt, bias=self.epsb[:, :]),
                       wait=[t_mv])
            t_rs = p.v(lambda e, tl=tl: e.reciprocal(rs[:, :], stds[:, tl:tl + 1]),
                       wait=[t_sd])
            p.v(lambda e, tl=tl: e.tensor_copy(mus[:, tl:tl + 1], mv[:, 0:1]),
                wait=[t_mv])
            t_nm = p.v(lambda e: e.tensor_scalar(nmr[:, :], mv[:, 0:1], rs[:, :], -1.0,
                                                 ALU.mult, ALU.mult), wait=[t_rs])
            stat_free = t_nm
            xh_tok.append(p.v(lambda e, t=t: e.tensor_scalar(
                self.B2[:, t * D:(t + 1) * D], self.B2[:, t * D:(t + 1) * D],
                rs[:, :], nmr[:, :], ALU.mult, ALU.add), wait=[t_nm]))
        self.moe_stat_free = stat_free
        ptr2 = [p.psum("mtr0", [128, 512], BF16), p.psum("mtr1", [128, 512], BF16)]
        trf2 = [None, None]
        hb = self.hbase_free[par]
        for tl in range(8):
            for jg in range(2):
                last = None
                pslot = ptr2[jg]
                for j4 in range(4):
                    j = jg * 4 + j4
                    last = p.t(lambda e, tl=tl, j=j, j4=j4, t0=t0, pslot=pslot:
                               e.transpose(
                        pslot[:, j4 * 128:(j4 + 1) * 128],
                        self.B2[:, (t0 + tl) * D + j * 128:(t0 + tl) * D + (j + 1) * 128],
                        self.ident[:, :]),
                        wait=[xh_tok[tl]] +
                        ([trf2[jg]] if trf2[jg] is not None and j4 == 0 else
                         ([pgate, hb] if tl == 0 and j4 == 0 else [])))
                trf2[jg] = p.v(lambda e, tl=tl, jg=jg, b0=b0, pslot=pslot:
                               e.tensor_copy(
                    self.B1.rearrange("p (j tt) -> p j tt", tt=1024)
                    [:, (b0 // 1024) + jg * 4:(b0 // 1024) + (jg + 1) * 4,
                     tl * 128:(tl + 1) * 128],
                    pslot.rearrange("p (j tt) -> p j tt", j=4)[:, :, :]), wait=[last])
        trf = self._join([trf2[0], trf2[1]])
        p.free(ptr2[1], ptr2[0])
        lgt, p4, sc1, tmask = m["lgt"], m["p4"], m["sc1"], m["tmask"]
        comb, scb = m["comb"][par], m["scb"][par]
        rbs, srep = m["rbs"][lp], m["srep"][lp]
        pA = p.psum("pA", [128, 4], F32)
        comb_toks = []
        for tl in range(8):
            mmA = None
            for d in range(ND):
                mmA = p.t(lambda e, tl=tl, d=d, b0=b0, lp=lp: e.matmul(
                    pA[:, :],
                    self.B1[:, b0 + d * 1024 + tl * 128:b0 + d * 1024 + (tl + 1) * 128],
                    m["rwb"][lp][:, d * 4:(d + 1) * 4],
                    start=(d == 0), stop=(d == ND - 1)),
                    wait=[trf] + ([comb_toks[-1]] if comb_toks else []))
            tg = p.v(lambda e, tl=tl: e.tensor_scalar(
                lgt[:, :], pA[:, :], stds[:, tl:tl + 1], None, ALU.mult), wait=[mmA])
            tg = p.v(lambda e, tl=tl: e.scalar_tensor_tensor(
                lgt[:, :], srep[:, :], mus[:, tl:tl + 1], lgt[:, :], ALU.mult, ALU.add),
                wait=[tg, tok_srep])
            tg = p.v(lambda e: e.tensor_tensor(lgt[:, :], lgt[:, :], rbs[:, :], ALU.add),
                     wait=[tg, tok_rb])
            tm1 = p.v(lambda e: e.reduce_max(sc1[:, 0:1], lgt[:, :], axis=AXX), wait=[tg])
            tnm = p.v(lambda e: e.tensor_scalar(sc1[:, 1:2], sc1[:, 0:1], -1.0, None,
                                                ALU.mult), wait=[tm1])
            tex = p.a(lambda e: e.activation(p4[:, :], lgt[:, :], AF.Exp,
                                             bias=sc1[:, 1:2]), wait=[tnm])
            tm1p = p.v(lambda e: e.reduce_max(sc1[:, 2:3], p4[:, :], axis=AXX), wait=[tex])
            tt_ = p.v(lambda e: e.tensor_scalar(tmask[:, :], p4[:, :], sc1[:, 2:3], None,
                                                ALU.is_lt), wait=[tm1p])
            tpm = p.v(lambda e: e.tensor_tensor(tmask[:, :], p4[:, :], tmask[:, :],
                                                ALU.mult), wait=[tt_])
            tm2p = p.v(lambda e: e.reduce_max(sc1[:, 3:4], tmask[:, :], axis=AXX),
                       wait=[tpm])
            tind = p.v(lambda e: e.tensor_scalar(tmask[:, :], p4[:, :], sc1[:, 3:4], None,
                                                 ALU.is_ge), wait=[tm2p])
            tden = p.v(lambda e: e.tensor_tensor(sc1[:, 4:5], sc1[:, 2:3], sc1[:, 3:4],
                                                 ALU.add), wait=[tm2p])
            tdin = p.v(lambda e: e.reciprocal(sc1[:, 5:6], sc1[:, 4:5]), wait=[tden])
            tx1 = p.v(lambda e: e.tensor_tensor(tmask[:, :], p4[:, :], tmask[:, :],
                                                ALU.mult), wait=[tind])
            tcb = p.v(lambda e, tl=tl: e.tensor_scalar(
                comb[:, tl * 4:(tl + 1) * 4], tmask[:, :], sc1[:, 5:6], None, ALU.mult),
                wait=[tx1, tdin])
            tsc = p.v(lambda e, tl=tl: e.tensor_reduce(
                scb[:, tl:tl + 1], comb[:, tl * 4:(tl + 1) * 4], axis=AXX, op=ALU.add),
                wait=[tcb])
            comb_toks.append(tsc)
        p.free(pA)
        self.mmA_last[hidx] = mmA
        self.prep_out[hidx] = (xh_tok, comb_toks)
        self.pending_gate = comb_toks[-1]

    def moe_experts_half(self, hidx):
        p = self.p
        m = self.mm
        l, th = divmod(hidx, 2)
        par = hidx % 2
        t0 = th * 8
        b0 = par * 8192
        acch = m["acch"]
        aa, bb = m["aa"], m["bb"]
        mus, stds = m["mus"][par], m["stds"][par]
        scb = m["scb"][par]
        xh_tok, comb_toks = self.prep_out.pop(hidx)
        epm1 = self.ep_toks.get(hidx - 1)
        acc_toks = []
        prev = None
        for tl in range(8):
            ta_ = p.v(lambda e, tl=tl: e.tensor_tensor(aa[:, :], scb[:, tl:tl + 1],
                                                       stds[:, tl:tl + 1], ALU.mult),
                      wait=[comb_toks[tl]] + ([prev] if prev else []) +
                      ([epm1[tl]] if epm1 else []))
            tb_ = p.v(lambda e, tl=tl: e.tensor_tensor(bb[:, :], scb[:, tl:tl + 1],
                                                       mus[:, tl:tl + 1], ALU.mult),
                      wait=[ta_])
            prev = p.v(lambda e, tl=tl, t0=t0: e.tensor_scalar(
                acch[:, tl * D:(tl + 1) * D], self.B2[:, (t0 + tl) * D:(t0 + tl + 1) * D],
                aa[:, :], bb[:, :], ALU.mult, ALU.add), wait=[tb_])
            acc_toks.append(prev)
        y_tail = self.cn_cons
        for e_ in range(4):
            acc_toks, y_tail = self.moe_expert(l, th, e_, m, xh_tok, acc_toks,
                                               y_tail, b0, m["comb"][par])
            if e_ == 1 and hidx < 7:
                self.moe_y_tail = y_tail
                self.moe_prep(hidx + 1)
        self.xg_last[hidx] = self.xg_join
        spill = (l == 3)
        eps = []
        for tl in range(8):
            if not spill:
                eps.append(p.v(lambda e, tl=tl, t0=t0: e.tensor_copy(
                    self.B2[:, (t0 + tl) * D:(t0 + tl + 1) * D],
                    acch[:, tl * D:(tl + 1) * D]), wait=[acc_toks[tl]]))
            else:
                eps.append(self.dma_f(
                    self.accf_hbm[(t0 + tl) * 128:(t0 + tl + 1) * 128, :],
                    acch[:, tl * D:(tl + 1) * D], wait=[acc_toks[tl]]))
        self.ep_toks[hidx] = eps
        self.hbase_free[par] = y_tail

    def moe_expert(self, l, th, e_, m, xh_tok, acc_toks, y_tail, b0, comb):
        (acch, hslab, w1b, w2b, lgb, lbb, b1s, b1pp, b2s) = (
            m["acch"], m["hslab"], m["w1b"], m["w2b"], m["lgb"], m["lbb"],
            m["b1s"], m["b1pp"], m["b2s"])
        p = self.p
        t0 = th * 8
        yw = [self.cn_cons] if self.cn_cons else []
        tok_lg = self.dma_f(lgb[:, :], self.inp["ln_gT"][l, e_], wait=yw, q="dcn")
        tok_lb = self.dma_c(lbb[:, :], self.inp["ln_bT"][l, e_], wait=yw, q="dcg")
        tok_b1 = self.dma_f(b1s[:, :], self.inp["b1T"][l, e_], wait=yw, q="dcn")
        tok_b2 = self.dma_c(b2s[:, :], self.inp["b2r"][l, e_], wait=yw, q="dcg")
        ptr2 = [p.psum("mtr0", [128, 512], BF16), p.psum("mtr1", [128, 512], BF16)]
        trf2 = [None, None]
        pg = self.pending_gate
        self.pending_gate = None
        for tl in range(8):
            for jg in range(2):
                last = None
                pslot = ptr2[jg]
                for j4 in range(4):
                    j = jg * 4 + j4
                    last = p.t(lambda e, tl=tl, j=j, j4=j4, pslot=pslot: e.transpose(
                        pslot[:, j4 * 128:(j4 + 1) * 128],
                        self.B2[:, (t0 + tl) * D + j * 128:(t0 + tl) * D + (j + 1) * 128],
                        self.ident[:, :]),
                        wait=[xh_tok[tl], tok_lg] +
                        ([trf2[jg]] if trf2[jg] is not None and j4 == 0 else
                         ([acc_toks[-1], y_tail, pg] if tl == 0 and j4 == 0 else [])))
                for j4 in range(4):
                    j = jg * 4 + j4
                    trf2[jg] = p.v(lambda e, tl=tl, j=j, j4=j4, b0=b0, pslot=pslot:
                                   e.tensor_scalar(
                        self.B1[:, b0 + j * 1024 + tl * 128:b0 + j * 1024 + (tl + 1) * 128],
                        pslot[:, j4 * 128:(j4 + 1) * 128], lgb[:, j:j + 1], None,
                        ALU.mult), wait=[last])
        xg_tok = self._join([trf2[0], trf2[1]])
        self.xg_join = xg_tok
        p.free(ptr2[1], ptr2[0])
        for fh in range(2):
            hs_toks = []
            pb = p.psum("pb", [128, 4], F32)
            php = [p.psum("phA", [128, 1024], F32), p.psum("phB", [128, 1024], F32)]
            gel_slot = [None, None]
            gel_free = None
            w1_toks = [None, None]
            w1_cons = self.w1_cons
            for fs in range(4):
                fcol = fh * 2048 + fs * 512
                sl = fs % 2
                w1_toks[sl] = self.dma_w(
                    w1b[:, sl * ND * 512:(sl + 1) * ND * 512],
                    self.inp["w1"][l, e_][:, fcol:fcol + 512], ND,
                    wait=[w1_cons[sl]] if w1_cons[sl] else (),
                    q=("dwa", "dwb")[sl])
                bias_tok = None
                if th == 0:
                    mmb = None
                    for f4 in range(4):
                        for d in range(ND):
                            mmb = p.t(lambda e, sl=sl, f4=f4, d=d: e.matmul(
                                pb[:, f4:f4 + 1],
                                w1b[:, sl * ND * 512 + d * 512 + f4 * 128:
                                    sl * ND * 512 + d * 512 + (f4 + 1) * 128],
                                lbb[:, d:d + 1], start=(d == 0), stop=(d == ND - 1)),
                                wait=[w1_toks[sl], tok_lb] +
                                ([self.pb_free] if self.pb_free and f4 == 0 and d == 0
                                 else []) +
                                ([y_tail, xg_tok] if fs == 0 and f4 == 0 and d == 0
                                 else []))
                    fg0 = fh * 16 + fs * 4
                    bias_tok = p.v(lambda e, fg0=fg0: e.tensor_tensor(
                        b1pp[:, fg0:fg0 + 4], pb[:, :], b1s[:, fg0:fg0 + 4], ALU.add),
                        wait=[mmb, tok_b1])
                    self.pb_free = bias_tok
                for f4 in range(4):
                    fglob = fh * 16 + fs * 4 + f4
                    floc = fs * 4 + f4
                    psl = floc % 2
                    ph = php[psl]
                    mmh = None
                    for c in range(2):
                        for d in range(ND):
                            mmh = p.t(lambda e, ph=ph, sl=sl, f4=f4, c=c, d=d: e.matmul(
                                ph[:, c * 512:(c + 1) * 512],
                                w1b[:, sl * ND * 512 + d * 512 + f4 * 128:
                                    sl * ND * 512 + d * 512 + (f4 + 1) * 128],
                                self.B1[:, b0 + d * 1024 + c * 512:
                                        b0 + d * 1024 + (c + 1) * 512],
                                start=(d == 0), stop=(d == ND - 1)),
                                wait=[w1_toks[sl], xg_tok] +
                                ([gel_slot[psl]] if gel_slot[psl] and c == 0 and d == 0
                                 else []) +
                                ([y_tail] if y_tail and fs == 0 and f4 == 0 and c == 0
                                  and d == 0 else []))
                    gel_free = p.a(lambda e, ph=ph, floc=floc, fglob=fglob: e.activation(
                        hslab[:, floc * 1024:(floc + 1) * 1024], ph[:, :], AF.Gelu,
                        bias=b1pp[:, fglob:fglob + 1]),
                        wait=[mmh] + ([bias_tok] if bias_tok else []) +
                        ([y_tail] if y_tail and fs == 0 and f4 == 0 else []))
                    gel_slot[psl] = gel_free
                    hs_toks.append(gel_free)
                    w1_cons[sl] = mmh
            p.free(php[1], php[0], pb)
            tok_w2 = self.dma_w(w2b[:, :],
                                self.inp["w2"][l, e_][fh * 2048:(fh + 1) * 2048, :],
                                16, wait=[self.w2_cons] if self.w2_cons else (),
                                q="dw2")
            for tw in range(2):
                pys = [p.psum(f"py{i}", [128, 1024], F32) for i in range(4)]
                for tt2 in range(4):
                    tl = tw * 4 + tt2
                    for dc in range(2):
                        mmy = None
                        if fh == 0:
                            mmy = p.t(lambda e, tt2=tt2, dc=dc: e.matmul(
                                pys[tt2][:, dc * 512:(dc + 1) * 512],
                                self.ones_row_bf[:, :], b2s[:, dc * 512:(dc + 1) * 512],
                                start=True, stop=False),
                                wait=[tok_b2, hs_toks[-1]] +
                                ([y_tail] if y_tail else []))
                        for floc in range(16):
                            first = (fh != 0) and (floc == 0)
                            mmy = p.t(lambda e, tt2=tt2, dc=dc, floc=floc, tl=tl,
                                      first=first: e.matmul(
                                pys[tt2][:, dc * 512:(dc + 1) * 512],
                                hslab[:, floc * 1024 + tl * 128:
                                      floc * 1024 + (tl + 1) * 128],
                                w2b[:, floc * 1024 + dc * 512:
                                    floc * 1024 + (dc + 1) * 512],
                                start=first, stop=(floc == 15)),
                                wait=[tok_w2, hs_toks[-1]] +
                                ([y_tail] if y_tail and first else []))
                        last_y = mmy
                        self.w2_cons = mmy
                    tl_tok = p.v(lambda e, tt2=tt2, tl=tl, e_=e_: e.scalar_tensor_tensor(
                        acch[:, tl * D:(tl + 1) * D], pys[tt2][:, :],
                        comb[:, tl * 4 + e_:tl * 4 + e_ + 1],
                        acch[:, tl * D:(tl + 1) * D], ALU.mult, ALU.add),
                        wait=[last_y, acc_toks[tl]])
                    acc_toks[tl] = tl_tok
                    y_tail = tl_tok
                p.free(*pys)
        self.cn_cons = y_tail
        return acc_toks, y_tail

    def routing(self):
        p = self.p
        oh0 = p.sbuf("oh0", [128, NT * 8], BF16)
        oh1 = p.sbuf("oh1", [128, NT * 8], BF16)
        w01 = p.sbuf("w01", [128, 2 * NT], F32)
        mark = p.push()
        gate_bf = p.sbuf("gateb", [128, ND * 8], BF16)
        tok_gw = self.dma_w(gate_bf[:, :], self.inp["gate_wT"], ND)
        p8 = p.sbuf("p8b", [128, 8], F32)
        sc = p.sbuf("rsc", [128, 8], F32)
        zacc = p.sbuf("zacc", [128, 1], F32)
        lgt8 = p.sbuf("lgt8", [128, 8], F32)
        pm8 = p.sbuf("pm8", [128, 8], F32)
        cms = p.sbuf("cms", [1, 16], F32)
        pL = p.psum("pL", [128, 8], F32)
        oh_toks = []
        prevA = None
        for tl in range(NT):
            mmL = None
            for d in range(ND):
                mmL = p.t(lambda e, tl=tl, d=d: e.matmul(
                    pL[:, :], self.B1[:, d * T + tl * 128:d * T + (tl + 1) * 128],
                    gate_bf[:, d * 8:(d + 1) * 8], start=(d == 0), stop=(d == ND - 1)),
                    wait=[tok_gw] + ([prevA] if prevA else []))
            tm1 = p.v(lambda e: e.reduce_max(sc[:, 0:1], pL[:, :], axis=AXX), wait=[mmL])
            tg = p.v(lambda e: e.tensor_copy(lgt8[:, :], pL[:, :]), wait=[mmL])
            prevA = tg
            tnm = p.v(lambda e: e.tensor_scalar(sc[:, 1:2], sc[:, 0:1], -1.0, None,
                                                ALU.mult), wait=[tm1])
            tex = p.a(lambda e: e.activation(p8[:, :], lgt8[:, :], AF.Exp,
                                             bias=sc[:, 1:2], accum_out=zacc[:, :]),
                      wait=[tnm, tg])
            tzi = p.v(lambda e: e.reciprocal(sc[:, 2:3], zacc[:, :]), wait=[tex])
            tpn = p.v(lambda e: e.tensor_scalar(p8[:, :], p8[:, :], sc[:, 2:3], None,
                                                ALU.mult), wait=[tzi])
            tm1p = p.v(lambda e: e.reduce_max(sc[:, 3:4], p8[:, :], axis=AXX), wait=[tpn])
            tlt = p.v(lambda e: e.tensor_scalar(
                pm8[:, :], p8[:, :], sc[:, 3:4], None, ALU.is_lt), wait=[tm1p])
            tmm = p.v(lambda e: e.tensor_tensor(
                pm8[:, :], p8[:, :], pm8[:, :], ALU.mult), wait=[tlt])
            tm2p = p.v(lambda e: e.reduce_max(
                sc[:, 4:5], pm8[:, :], axis=AXX), wait=[tmm])
            t_i0 = p.v(lambda e, tl=tl: e.tensor_scalar(
                oh0[:, tl * 8:(tl + 1) * 8], p8[:, :], sc[:, 3:4], None, ALU.is_ge),
                wait=[tm1p])
            t_ig = p.v(lambda e, tl=tl: e.tensor_scalar(
                oh1[:, tl * 8:(tl + 1) * 8], p8[:, :], sc[:, 4:5], None, ALU.is_ge),
                wait=[tm2p])
            t_i1 = p.v(lambda e, tl=tl: e.tensor_tensor(
                oh1[:, tl * 8:(tl + 1) * 8], oh1[:, tl * 8:(tl + 1) * 8],
                oh0[:, tl * 8:(tl + 1) * 8], ALU.subtract), wait=[t_ig, t_i0])
            tdn = p.v(lambda e: e.tensor_tensor(sc[:, 5:6], sc[:, 3:4], sc[:, 4:5],
                                                ALU.add), wait=[tm2p])
            tdn = p.v(lambda e: e.tensor_scalar(sc[:, 5:6], sc[:, 5:6], 1e-9, None,
                                                ALU.add), wait=[tdn])
            tdi = p.v(lambda e: e.reciprocal(sc[:, 6:7], sc[:, 5:6]), wait=[tdn])
            p.v(lambda e, tl=tl: e.tensor_tensor(w01[:, tl:tl + 1], sc[:, 3:4],
                                                 sc[:, 6:7], ALU.mult), wait=[tdi])
            p.v(lambda e, tl=tl: e.tensor_tensor(w01[:, NT + tl:NT + tl + 1], sc[:, 4:5],
                                                 sc[:, 6:7], ALU.mult), wait=[tdi])
            oh_toks.append(t_i1)
        pcm = p.psum("pcm", [1, 16], F32)
        mcm = None
        for k, ohsb in ((0, oh0), (1, oh1)):
            for tl in range(NT):
                mcm = p.t(lambda e, k=k, tl=tl, ohsb=ohsb: e.matmul(
                    pcm[:, k * 8:(k + 1) * 8], self.ones_col_bf[:, :],
                    ohsb[:, tl * 8:(tl + 1) * 8],
                    start=(tl == 0), stop=(tl == NT - 1)),
                    wait=[oh_toks[tl]] + ([mcm] if mcm else []))
        tcm = p.v(lambda e: e.tensor_copy(cms[:, :], pcm[:, :]), wait=[mcm])
        p.free(pcm)
        if self.mode == "fused":
            cm_tok = self.dma_f(self.cc_in, cms[:, :], wait=[tcm])
        elif self.mode == "counts":
            cm_tok = self.dma_f(self.out_counts, cms[:, :], wait=[tcm])
        else:
            cm_tok = tcm
        p.fence()
        p.pop(mark)
        return oh0, oh1, w01, cm_tok

    def offsets_and_comb(self, comb, oh0, oh1, w01, cm_tok):
        p = self.p
        mark = p.push()
        ag = p.sbuf("agb", [8, 16], F32)
        if self.mode == "fused":
            p.g(lambda e: e.collective_compute(
                "AllGather", ALU.bypass, replica_groups=[list(range(8))],
                ins=[self.cc_in], outs=[self.cc_out]), wait=[cm_tok], inc="cc", n=1)
            tag = self.dma_f(ag[:, :], self.cc_out, wait=[("cc", 1)])
        else:
            tag = self.dma_f(ag[:, :], self.inp["allcounts"], wait=[cm_tok])
        sel = p.sbuf("selb", [8, 1], F32)
        tsel = self.dma_f(sel[:, :], self.inp["selmask"])
        po = p.psum("po", [1, 16], F32)
        mo = p.t(lambda e: e.matmul(po[:, :], sel[:, :], ag[:, :], start=True, stop=True),
                 wait=[tag, tsel])
        offs = p.sbuf("offs", [1, 16], F32)
        to = p.v(lambda e: e.tensor_scalar(offs[:, :], po[:, :], -1.0, None, ALU.add),
                 wait=[mo])
        p.free(po)
        por = p.psum("por", [128, 16], F32)
        mor = p.t(lambda e: e.matmul(por[:, :], self.ones_row_f[:, :], offs[:, :],
                                     start=True, stop=True), wait=[to])
        offr = p.sbuf("offr", [128, 16], F32)
        tor = p.v(lambda e: e.tensor_copy(offr[:, :], por[:, :]), wait=[mor])
        p.free(por)
        cnt0 = p.sbuf("cnt0", [8, T], F32)
        cnt1 = p.sbuf("cnt1", [8, T], F32)
        ohT = p.sbuf("ohT", [8, 2 * T], F32)
        zero8 = p.sbuf("zero8", [8, 1], F32)
        ohb = p.sbuf("ohb", [128, 8], BF16)
        p.v(lambda e: e.memset(zero8[:, :], 0.0))
        ptb = p.psum("ptb", [8, 128], BF16)
        tr_free = None
        for tl in range(NT):
            for k, ohsb in ((0, oh0), (1, oh1)):
                tc = p.v(lambda e, tl=tl, ohsb=ohsb: e.tensor_copy(
                    ohb[:, :], ohsb[:, tl * 8:(tl + 1) * 8]),
                    wait=[tr_free] if tr_free else ())
                tt_ = p.t(lambda e: e.transpose(ptb[:, :], ohb[:, :], self.ident[:, :]),
                          wait=[tc])
                tr_free = p.v(lambda e, k=k, tl=tl: e.tensor_copy(
                    ohT[:, k * T + tl * 128:k * T + (tl + 1) * 128], ptb[:, :]),
                    wait=[tt_])
        ts0 = p.v(lambda e: e.tensor_tensor_scan(cnt0[:, :], ohT[:, 0:T], ohT[:, 0:T],
                                                 zero8[:, :], ALU.add, ALU.bypass),
                  wait=[tr_free])
        ts1 = p.v(lambda e: e.tensor_tensor_scan(cnt1[:, :], ohT[:, T:2 * T],
                                                 ohT[:, T:2 * T], zero8[:, :],
                                                 ALU.add, ALU.bypass), wait=[ts0])
        pct = p.psum("pct", [128, 16], F32)
        cadj = p.sbuf("cadj", [128, 16], F32)
        r8 = p.sbuf("r8b", [128, 8], F32)
        rk = p.sbuf("rkb", [128, 4], F32)
        prev = ts1
        for tl in range(NT):
            mt = []
            for k, cnt in ((0, cnt0), (1, cnt1)):
                mt.append(p.t(lambda e, tl=tl, k=k, cnt=cnt: e.matmul(
                    pct[:, k * 8:(k + 1) * 8], cnt[:, tl * 128:(tl + 1) * 128],
                    self.ident_f[0:8, 0:8], start=True, stop=True),
                    wait=[prev] if prev else ()))
            tadj = p.v(lambda e: e.tensor_tensor(cadj[:, :], pct[:, :], offr[:, :],
                                                 ALU.add), wait=[mt[0], mt[1], tor])
            prev = tadj
            rk_t = []
            for k, ohsb in ((0, oh0), (1, oh1)):
                tm_ = p.v(lambda e, tl=tl, k=k, ohsb=ohsb: e.tensor_tensor(
                    r8[:, :], ohsb[:, tl * 8:(tl + 1) * 8], cadj[:, k * 8:(k + 1) * 8],
                    ALU.mult), wait=[tadj])
                trr = p.v(lambda e, k=k: e.reduce_sum(rk[:, k:k + 1], r8[:, :],
                                                      axis=AXX), wait=[tm_])
                rk_t.append(p.v(lambda e, k=k: e.tensor_scalar(
                    rk[:, 2 + k:3 + k], rk[:, k:k + 1], CAP, None, ALU.is_lt),
                    wait=[trr]))
            g0 = p.v(lambda e, tl=tl: e.tensor_tensor(rk[:, 0:1], rk[:, 2:3],
                                                      w01[:, tl:tl + 1], ALU.mult),
                     wait=[rk_t[0]])
            g1 = p.v(lambda e, tl=tl: e.tensor_tensor(rk[:, 1:2], rk[:, 3:4],
                                                      w01[:, NT + tl:NT + tl + 1],
                                                      ALU.mult), wait=[rk_t[1]])
            tc1 = p.v(lambda e, tl=tl: e.tensor_scalar(
                comb[:, tl * 8:(tl + 1) * 8], oh1[:, tl * 8:(tl + 1) * 8], rk[:, 1:2],
                None, ALU.mult), wait=[g1])
            prev = p.v(lambda e, tl=tl: e.scalar_tensor_tensor(
                comb[:, tl * 8:(tl + 1) * 8], oh0[:, tl * 8:(tl + 1) * 8], rk[:, 0:1],
                comb[:, tl * 8:(tl + 1) * 8], ALU.mult, ALU.add), wait=[g0, tc1])
        p.pop(mark)
        return prev

    def l1_ffn(self, comb, acc_ap, gates=(None, None)):
        p = self.p
        self.g_cons = [gates[0], gates[0]]
        self.u_cons = [gates[0], gates[0]]
        self.dn_cons = gates[0]
        self.pg_gate = gates[1]
        hu = p.sbuf("hub", [128, 8 * T], BF16)
        gs = p.sbuf("gsl", [128, 2 * ND * 256], BF16)
        us = p.sbuf("usl", [128, 2 * ND * 256], BF16)
        dn = p.sbuf("dnb", [128, 8 * D], BF16)
        sg = p.sbuf("sgb", [128, T], BF16)
        acc_toks = [None] * NT
        y_tail = None
        for e_ in range(8):
            for fh in range(2):
                hu_toks = []
                pgh = [p.psum("pgA", [128, 1024], F32), p.psum("pgB", [128, 1024], F32)]
                puh = [p.psum("puA", [128, 1024], F32), p.psum("puB", [128, 1024], F32)]
                g_toks = [None, None]
                u_toks = [None, None]
                g_cons = self.g_cons
                u_cons = self.u_cons
                hu_half = [None, None]
                sil_half = [None, None]
                for h2s in range(4):
                    sl = h2s % 2
                    col0 = fh * 1024 + h2s * 256
                    g_toks[sl] = self.dma_w(
                        gs[:, sl * ND * 256:(sl + 1) * ND * 256],
                        self.inp["gpwT"][e_][:, col0:col0 + 256], ND,
                        wait=[g_cons[sl]] if g_cons[sl] else (),
                        q=("dga", "dgb")[sl])
                    u_toks[sl] = self.dma_w(
                        us[:, sl * ND * 256:(sl + 1) * ND * 256],
                        self.inp["upwT"][e_][:, col0:col0 + 256], ND,
                        wait=[u_cons[sl]] if u_cons[sl] else (),
                        q=("dua", "dub")[sl])
                    for h2 in range(2):
                        f2loc = h2s * 2 + h2
                        for ch in range(2):
                            mmg = mmu = None
                            for c in range(2):
                                for d in range(ND):
                                    mmg = p.t(lambda e, sl=sl, h2=h2, ch=ch, c=c, d=d:
                                              e.matmul(
                                        pgh[ch][:, c * 512:(c + 1) * 512],
                                        gs[:, sl * ND * 256 + d * 256 + h2 * 128:
                                            sl * ND * 256 + d * 256 + (h2 + 1) * 128],
                                        self.B1[:, d * T + (2 * ch + c) * 512:
                                                d * T + (2 * ch + c + 1) * 512],
                                        start=(d == 0), stop=(d == ND - 1)),
                                        wait=[g_toks[sl]] +
                                        ([hu_half[ch]] if hu_half[ch] and c == 0 and
                                         d == 0 else []) +
                                        ([y_tail] if y_tail and h2s == 0 and h2 == 0 and
                                         ch == 0 and c == 0 and d == 0 else []) +
                                        ([self.pg_gate] if self.pg_gate and h2s == 0 and
                                         h2 == 0 and ch == 0 and c == 0 and d == 0 else []))
                            sil_half[ch] = p.a(lambda e, ch=ch: e.activation(
                                sg[:, ch * 1024:(ch + 1) * 1024], pgh[ch][:, :],
                                AF.Silu), wait=[mmg])
                            for c in range(2):
                                for d in range(ND):
                                    mmu = p.t(lambda e, sl=sl, h2=h2, ch=ch, c=c, d=d:
                                              e.matmul(
                                        puh[ch][:, c * 512:(c + 1) * 512],
                                        us[:, sl * ND * 256 + d * 256 + h2 * 128:
                                            sl * ND * 256 + d * 256 + (h2 + 1) * 128],
                                        self.B1[:, d * T + (2 * ch + c) * 512:
                                                d * T + (2 * ch + c + 1) * 512],
                                        start=(d == 0), stop=(d == ND - 1)),
                                        wait=[u_toks[sl]] +
                                        ([hu_half[ch]] if hu_half[ch] and c == 0 and
                                         d == 0 else []))
                            hu_half[ch] = p.v(lambda e, f2loc=f2loc, ch=ch: e.tensor_tensor(
                                hu[:, f2loc * T + ch * 1024:f2loc * T + (ch + 1) * 1024],
                                sg[:, ch * 1024:(ch + 1) * 1024], puh[ch][:, :],
                                ALU.mult), wait=[sil_half[ch], mmu])
                            hu_toks.append(hu_half[ch])
                        g_cons[sl] = mmg
                        u_cons[sl] = mmu
                p.free(puh[1], puh[0], pgh[1], pgh[0])
                tok_dn = self.dma_w(dn[:, :],
                                    self.inp["downT"][e_][fh * 1024:(fh + 1) * 1024, :],
                                    8, wait=[self.dn_cons] if self.dn_cons else (),
                                    q="ddn")
                for tw in range(4):
                    pys = [p.psum(f"qy{i}", [128, D], F32) for i in range(4)]
                    for tt2 in range(4):
                        tl = tw * 4 + tt2
                        for dc in range(2):
                            mmy = None
                            for f2loc in range(8):
                                mmy = p.t(lambda e, tt2=tt2, dc=dc, f2loc=f2loc,
                                          tl=tl: e.matmul(
                                    pys[tt2][:, dc * 512:(dc + 1) * 512],
                                    hu[:, f2loc * T + tl * 128:f2loc * T + (tl + 1) * 128],
                                    dn[:, f2loc * D + dc * 512:f2loc * D + (dc + 1) * 512],
                                    start=(f2loc == 0), stop=(f2loc == 7)),
                                    wait=[tok_dn, hu_toks[-1]] +
                                    ([y_tail] if y_tail and f2loc == 0 else []))
                            last_y = mmy
                            self.dn_cons = mmy
                        tl_tok = p.v(lambda e, tt2=tt2, tl=tl, e_=e_: e.scalar_tensor_tensor(
                            acc_ap(tl), pys[tt2][:, :],
                            comb[:, tl * 8 + e_:tl * 8 + e_ + 1],
                            acc_ap(tl), ALU.mult, ALU.add),
                            wait=[last_y] + ([acc_toks[tl]] if acc_toks[tl] else []))
                        acc_toks[tl] = tl_tok
                        y_tail = tl_tok
                    p.free(*pys)
        p.free(sg, dn, us, gs, hu)
        p.fence()
        return acc_toks

    def emit_out(self, acc_ap, acc_toks, x1_src):
        p = self.p
        mark = p.push()
        NS = 4
        xo = p.sbuf("xob", [128, NS * D], F32)
        oo = p.sbuf("oob", [128, NS * D], F32)
        out_tok = [None] * NS
        lasts = []
        for tl in range(NT):
            sl = tl % NS
            tx = self.dma_f(xo[:, sl * D:(sl + 1) * D],
                            x1_src[tl * 128:(tl + 1) * 128, :],
                            wait=[out_tok[sl]] if out_tok[sl] else (),
                            q=("dsa", "dsb", "dsa", "dsb")[sl])
            tv = p.v(lambda e, tl=tl, sl=sl: e.tensor_tensor(
                oo[:, sl * D:(sl + 1) * D], xo[:, sl * D:(sl + 1) * D],
                acc_ap(tl), ALU.add),
                wait=[tx, acc_toks[tl]])
            out_tok[sl] = self.dma_f(self.out[tl * 128:(tl + 1) * 128, :],
                                     oo[:, sl * D:(sl + 1) * D], wait=[tv],
                                     q=("dsc", "dsd", "dse", "ds")[sl])
            lasts.append(out_tok[sl])
        p.op("sync", None, wait=lasts[-NS:])
        p.pop(mark)

    def build(self):
        p = self.p
        m = self.mode
        if m in ("l0", "fused"):
            self.ln_pass(self.inp["x"])
            aoT = self.attention(0)
            self.proj_pass(0, aoT)
            self.p.free(aoT)
            p.fence()
            self.moe_chain()
            if m == "l0":
                mark = p.push()
                xo = p.sbuf("xo1", [128, 2 * D], F32)
                oo = p.sbuf("oo1", [128, 2 * D], F32)
                ot = [None, None]
                last = None
                for tl in range(NT):
                    sl = tl % 2
                    t1 = self.dma_f(xo[:, sl * D:(sl + 1) * D],
                                    self.acc0_hbm[tl * 128:(tl + 1) * 128, :],
                                    wait=[ot[sl]] if ot[sl] else ())
                    t2 = self.dma_f(oo[:, sl * D:(sl + 1) * D],
                                    self.accf_hbm[tl * 128:(tl + 1) * 128, :],
                                    wait=[ot[sl]] if ot[sl] else ())
                    tv = p.v(lambda e, sl=sl: e.tensor_tensor(
                        oo[:, sl * D:(sl + 1) * D], oo[:, sl * D:(sl + 1) * D],
                        xo[:, sl * D:(sl + 1) * D], ALU.add), wait=[t1, t2])
                    ot[sl] = self.dma_f(self.out_x1[tl * 128:(tl + 1) * 128, :],
                                        oo[:, sl * D:(sl + 1) * D], wait=[tv])
                    last = ot[sl]
                p.op("sync", None, wait=[last])
                p.pop(mark)
            else:
                self.ln_pass(self.acc0_hbm, src2=self.accf_hbm,
                             copy_out=self.x1_hbm)
                oh0, oh1, w01, cm_tok = self.routing()
                comb = p.sbuf("combL1", [128, NT * 8], F32)
                self.offsets_and_comb(comb, oh0, oh1, w01, cm_tok)
                aoT = self.attention_l1()
                acc1 = p.sbuf("acc1", [128, NT * D], F32)

                def acc_ap(tl):
                    return acc1[:, tl * D:(tl + 1) * D]
                pj_mm, pj_ev = self.proj_pass(1, aoT, acc_ap, gate=self.attn_tail)
                acc_toks = self.l1_ffn(comb, acc_ap, gates=(pj_mm, pj_ev))
                self.emit_out(acc_ap, acc_toks, self.x1_hbm)
                p.free(acc1, comb)
                p.free(w01, oh1, oh0)
        elif m == "counts":
            self.ln_pass(self.inp["x1_in"])
            oh0, oh1, w01, cm_tok = self.routing()
            p.op("sync", None, wait=[cm_tok])
            p.free(w01, oh1, oh0)
        elif m == "l1":
            self.ln_pass(self.inp["x1_in"])
            oh0, oh1, w01, cm_tok = self.routing()
            comb = p.sbuf("combL1", [128, NT * 8], F32)
            self.offsets_and_comb(comb, oh0, oh1, w01, cm_tok)
            aoT = self.attention_l1()
            acc1 = p.sbuf("acc1", [128, NT * D], F32)

            def acc_ap(tl):
                return acc1[:, tl * D:(tl + 1) * D]
            pj_mm, pj_ev = self.proj_pass(1, aoT, acc_ap, gate=self.attn_tail)
            acc_toks = self.l1_ffn(comb, acc_ap, gates=(pj_mm, pj_ev))
            self.emit_out(acc_ap, acc_toks, self.inp["x1_in"])
            p.free(acc1, comb)
            p.free(w01, oh1, oh0)
        p.finish()
        return self.nc


_CACHE = {}


def _build(mode, shapes_key, shapes):
    key = (mode, shapes_key)
    if key not in _CACHE:
        c = Core(mode, shapes)
        _CACHE[key] = c.build()
    return _CACHE[key]


def kernel(**inputs):
    ins = host_layouts(inputs)
    x = np.ascontiguousarray(np.asarray(inputs["L0_attn_x"], dtype=np.float32))
    shapes = {k: v.shape for k, v in ins.items()}
    skey = tuple(sorted((k, tuple(v)) for k, v in shapes.items()))
    base = {k: ins[k] for k in CONST_KEYS}
    sel = np.zeros((8, 8, 1), np.float32)
    for c in range(8):
        sel[c, :c, 0] = 1.0

    if FUSED:
        nc = _build("fused", skey, shapes)
        maps = []
        for c in range(8):
            m = dict(base)
            for k in L0_KEYS + L1_KEYS:
                m[k] = ins[k]
            m["x"] = x[c]
            m["selmask"] = sel[c]
            maps.append(m)
        res = run_bass_kernel_spmd(nc, maps, core_ids=list(range(8)))
        return np.stack([r["out"] for r in res.results], 0)

    nc0 = _build("l0", skey, shapes)
    maps = []
    for c in range(8):
        m = {k: ins[k] for k in CONST_KEYS + L0_KEYS}
        m["x"] = x[c]
        maps.append(m)
    r0 = run_bass_kernel_spmd(nc0, maps, core_ids=list(range(8)))
    x1 = [np.ascontiguousarray(r["x1"]) for r in r0.results]

    ncc = _build("counts", skey, shapes)
    maps = [dict({k: ins[k] for k in CONST_KEYS}, gate_wT=ins["gate_wT"],
                 x1_in=x1[c]) for c in range(8)]
    rc = run_bass_kernel_spmd(ncc, maps, core_ids=list(range(8)))
    allc = np.ascontiguousarray(
        np.concatenate([r["counts"] for r in rc.results], 0))

    nc1 = _build("l1", skey, shapes)
    maps = []
    for c in range(8):
        m = {k: ins[k] for k in CONST_KEYS + L1_KEYS}
        m["x1_in"] = x1[c]
        m["allcounts"] = allc
        m["selmask"] = sel[c]
        maps.append(m)
    r1 = run_bass_kernel_spmd(nc1, maps, core_ids=list(range(8)))
    return np.stack([r["out"] for r in r1.results], 0)


FUSED = True


# version 15 (final)
# speedup vs baseline: 1.0451x; 1.0090x over previous
import numpy as np
import ml_dtypes
import concourse.bass as bass
import concourse.mybir as mybir
from concourse.bass_utils import run_bass_kernel_spmd

F32 = mybir.dt.float32
BF16 = mybir.dt.bfloat16
AF = mybir.ActivationFunctionType
ALU = mybir.AluOpType
AXX = mybir.AxisListType.X
BF = ml_dtypes.bfloat16

T = 2048
D = 1024
NT = 16
ND = 8
H = 16
VA = 65
NEG = -1.0e38
EPS = 1e-5
CAP = 2560.0
CSC = 0.1250625


class Prog:

    def __init__(self, nc):
        self.nc = nc
        self.recs = []
        self.cnt = {}
        self.sem_names = []
        self.uid = 0
        self._mgr_map = {}
        self._stacks = {"s": [], "p": []}
        self._kind = {}
        self._fence = ()

    def sem(self, name):
        self.sem_names.append(name)
        self.cnt[name] = 0

    def fence(self):
        self._fence = tuple((s, self.cnt[s]) for s in self.sem_names if self.cnt[s] > 0)

    def op(self, eng, fn, wait=(), inc=None, n=1):
        w = tuple(t for t in wait if t is not None) + self._fence
        tok = None
        if inc is not None:
            self.cnt[inc] += n
            tok = (inc, self.cnt[inc])
        self.recs.append((eng, fn, w, inc, n))
        return tok

    def t(self, fn, wait=(), inc="pe", n=1):
        return self.op("tensor", fn, wait, inc, n)

    def v(self, fn, wait=(), inc="dve", n=1):
        return self.op("vector", fn, wait, inc, n)

    def a(self, fn, wait=(), inc="act", n=1):
        return self.op("scalar", fn, wait, inc, n)

    def g(self, fn, wait=(), inc="dg", n=16):
        w = tuple(wait) + (((inc, self.cnt[inc]),) if self.cnt.get(inc) else ())
        return self.op("gpsimd", fn, w, inc, n)

    def s(self, fn, wait=(), inc="ds", n=16):
        w = tuple(wait) + (((inc, self.cnt[inc]),) if self.cnt.get(inc) else ())
        return self.op("sync", fn, w, inc, n)

    def name(self, base):
        self.uid += 1
        return f"{base}_{self.uid}"

    def sbuf(self, base, shape, dtype):
        mgr = self.nc.sbuf_tensor(self.name(base), shape, dtype)
        h = mgr.__enter__()
        self._mgr_map[id(h)] = mgr
        self._kind[id(h)] = "s"
        self._stacks["s"].append(id(h))
        return h

    def psum(self, base, shape, dtype=F32):
        mgr = self.nc.psum_tensor(self.name(base), shape, dtype)
        h = mgr.__enter__()
        self._mgr_map[id(h)] = mgr
        self._kind[id(h)] = "p"
        self._stacks["p"].append(id(h))
        return h

    def free(self, *handles):
        for kind in ("s", "p"):
            ids = {id(h) for h in handles if self._kind.get(id(h)) == kind}
            if not ids:
                continue
            st = self._stacks[kind]
            assert ids == set(st[-len(ids):]), f"non-stack free ({kind})"
            for _ in range(len(ids)):
                hid = st.pop()
                self._mgr_map.pop(hid).__exit__(None, None, None)
                del self._kind[hid]

    def push(self):
        return (len(self._stacks["s"]), len(self._stacks["p"]))

    def pop(self, marker):
        ms, mp = marker
        for kind, m in (("p", mp), ("s", ms)):
            st = self._stacks[kind]
            while len(st) > m:
                hid = st.pop()
                self._mgr_map.pop(hid).__exit__(None, None, None)
                del self._kind[hid]

    def finish(self):
        sems = {}
        sem_mgrs = []
        for nm in self.sem_names:
            mgr = self.nc.semaphore(nm)
            sems[nm] = mgr.__enter__()
            sem_mgrs.append(mgr)
        blk_mgr = self.nc.Block()
        block = blk_mgr.__enter__()
        waited = {}

        def emitter(eng_name):
            def _(engine):
                for (e, fn, waits, inc, n) in self.recs:
                    if e != eng_name:
                        continue
                    for (snm, c) in waits:
                        if waited.get((eng_name, snm), 0) < c:
                            engine.wait_ge(sems[snm], c)
                            waited[(eng_name, snm)] = c
                    if fn is None:
                        continue
                    inst = fn(engine)
                    if inc is not None:
                        inst.then_inc(sems[inc], n)
            _.__name__ = "_"
            return _

        block.tensor(emitter("tensor"))
        block.vector(emitter("vector"))
        block.scalar(emitter("scalar"))
        block.gpsimd(emitter("gpsimd"))
        block.sync(emitter("sync"))
        blk_mgr.__exit__(None, None, None)
        for mgr in reversed(sem_mgrs):
            mgr.__exit__(None, None, None)


def host_layouts(inputs):
    f = np.float32

    def C(x):
        return np.ascontiguousarray(np.asarray(x, dtype=f))

    ins = {}
    ins["qk_w"] = C(inputs["L0_attn_qk_w"])
    ins["qk_bT"] = C(np.asarray(inputs["L0_attn_qk_b"]).reshape(16, 128).T)
    ins["v_w"] = C(inputs["L0_attn_v_w"])
    ins["cproj_w"] = C(inputs["L0_attn_cproj_w"])
    for nm, key in (("resid64", "L0_attn_resid_gain"), ("skip64", "L0_attn_skip_gain"),
                    ("centre64", "L0_attn_centre_gain")):
        g = np.asarray(inputs[key]).reshape(H)
        ins[nm] = C(np.broadcast_to(g[None, :], (64, H)))
    ins["rw"] = C(inputs["L0_ffn_router_w"])
    rb = np.asarray(inputs["L0_ffn_router_b"], dtype=f)
    ins["rb_rep"] = C(np.broadcast_to(rb[:, None, :], (4, 128, 4)))
    ins["ln_gT"] = C(np.asarray(inputs["L0_ffn_ln_g"]).reshape(4, 4, 8, 128).transpose(0, 1, 3, 2))
    ins["ln_bT"] = C(np.asarray(inputs["L0_ffn_ln_b"]).reshape(4, 4, 8, 128).transpose(0, 1, 3, 2))
    ins["w1"] = C(inputs["L0_ffn_w1"])
    ins["b1T"] = C(np.asarray(inputs["L0_ffn_b1"]).reshape(4, 4, 32, 128).transpose(0, 1, 3, 2))
    ins["w2"] = C(inputs["L0_ffn_w2"])
    ins["b2r"] = C(np.asarray(inputs["L0_ffn_b2"]).reshape(4, 4, 1, 1024))
    ins["qkv_wT"] = C(np.asarray(inputs["L1_attn_qkv_w"]).T)
    for nm, key in (("qn_w128", "L1_attn_qn_w"), ("qn_b128", "L1_attn_qn_b"),
                    ("kn_w128", "L1_attn_kn_w"), ("kn_b128", "L1_attn_kn_b")):
        v = np.asarray(inputs[key], dtype=f)
        ins[nm] = C(np.concatenate([v, v])[:, None])
    ins["sinks"] = C(np.asarray(inputs["L1_attn_sinks"])[None, :])
    ins["proj_wT"] = C(np.asarray(inputs["L1_attn_proj_w"]).T)
    ins["projb_rep"] = C(np.broadcast_to(
        np.asarray(inputs["L1_attn_proj_b"], dtype=f)[None, :], (128, D)))
    ins["gate_wT"] = C(np.asarray(inputs["L1_ffn_gate_w"]).T)
    ins["gpwT"] = C(np.asarray(inputs["L1_ffn_gate_proj_w"]).transpose(0, 2, 1))
    ins["upwT"] = C(np.asarray(inputs["L1_ffn_up_proj_w"]).transpose(0, 2, 1))
    ins["downT"] = C(np.asarray(inputs["L1_ffn_down_proj_w"]).transpose(0, 2, 1))
    ins["identity"] = np.eye(128, dtype=f)
    b = np.arange(128)[:, None]
    a = np.arange(128)[None, :]
    ins["mask_kgtq"] = np.where(b > a, NEG, 0.0).astype(f)
    ins["mask_kleq"] = np.where(b <= a, NEG, 0.0).astype(f)
    inv_idx = (1.0 / (np.arange(T, dtype=np.float64) + 1.0)).astype(f)
    ins["invidx64"] = C(np.broadcast_to(inv_idx[None, :], (64, T)))
    invf = (1.0 / (10000.0 ** (np.arange(0, 64, 2).astype(np.float64) / 64.0))).astype(f)
    emb = np.arange(T, dtype=f)[:, None] * invf[None, :]
    cos = np.cos(np.concatenate([emb, emb], 1)).astype(f).T
    sin = np.sin(np.concatenate([emb, emb], 1)).astype(f).T
    sgn = np.concatenate([-np.ones(32, f), np.ones(32, f)])[:, None]
    ins["cos128"] = C(np.concatenate([cos, cos], 0))
    ins["sin128"] = C(np.concatenate([sin * sgn, sin * sgn], 0))
    ins["ones_row"] = np.ones((1, 128), dtype=f)
    ins["ones_col"] = np.ones((128, 1), dtype=f)
    blk2 = np.zeros((128, 2), dtype=f)
    blk2[0:64, 0] = 1.0
    blk2[64:128, 1] = 1.0
    ins["blk2"] = blk2
    ins["blk2T"] = C(blk2.T)
    return ins


L0_KEYS = ["qk_w", "qk_bT", "v_w", "cproj_w", "resid64", "skip64", "centre64",
           "rw", "rb_rep", "ln_gT", "ln_bT", "w1", "b1T", "w2", "b2r"]
L1_KEYS = ["qkv_wT", "qn_w128", "qn_b128", "kn_w128", "kn_b128", "sinks",
           "proj_wT", "projb_rep", "gate_wT", "gpwT", "upwT", "downT"]
CONST_KEYS = ["identity", "mask_kgtq", "mask_kleq", "invidx64", "cos128", "sin128",
              "ones_row", "ones_col", "blk2", "blk2T"]


class Core:
    def __init__(self, mode, shapes, dbg=()):
        self.mode = mode
        self.dbg = set(dbg)
        self.dbg_names = []
        nc = bass.Bass("TRN2", target_bir_lowering=False)
        self.nc = nc
        p = Prog(nc)
        self.p = p
        for s in ("pe", "act", "dve", "dg", "ds", "cc", "dwa", "dwb", "dw2",
                  "dua", "dub", "dga", "dgb", "ddn", "dpw", "dcn", "dcg",
                  "dsa", "dsb", "dsc", "dsd", "dse"):
            p.sem(s)
        need = list(CONST_KEYS)
        if mode in ("l0", "fused"):
            need += L0_KEYS
        if mode == "counts":
            need += ["gate_wT"]
        if mode in ("l1", "fused"):
            need += L1_KEYS
        self.inp = {}
        for k in need:
            self.inp[k] = nc.dram_tensor(k, list(shapes[k]), F32, kind="ExternalInput").ap()
        if mode in ("l0", "fused"):
            self.inp["x"] = nc.dram_tensor("x", [T, D], F32, kind="ExternalInput").ap()
        if mode in ("counts", "l1"):
            self.inp["x1_in"] = nc.dram_tensor("x1_in", [T, D], F32, kind="ExternalInput").ap()
        if mode == "l1":
            self.inp["allcounts"] = nc.dram_tensor("allcounts", [8, 16], F32,
                                                   kind="ExternalInput").ap()
        if mode in ("l1", "fused"):
            self.inp["selmask"] = nc.dram_tensor("selmask", [8, 1], F32,
                                                 kind="ExternalInput").ap()
        if mode == "l0":
            self.out_x1 = nc.dram_tensor("x1", [T, D], F32, kind="ExternalOutput").ap()
        if mode == "counts":
            self.out_counts = nc.dram_tensor("counts", [1, 16], F32, kind="ExternalOutput").ap()
        if mode in ("l1", "fused"):
            self.out = nc.dram_tensor("out", [T, D], F32, kind="ExternalOutput").ap()
        if mode in ("l0", "fused"):
            self.acc0_hbm = nc.dram_tensor("acc0_hbm", [T, D], F32).ap()
            self.accf_hbm = nc.dram_tensor("accf_hbm", [T, D], F32).ap()
        if mode == "fused":
            self.x1_hbm = nc.dram_tensor("x1_hbm", [T, D], F32).ap()
            self.cc_in = nc.dram_tensor("cc_in", [1, 16], F32).ap()
            self.cc_out = nc.dram_tensor("cc_out", [8, 16], F32).ap()
        self.B1 = p.sbuf("B1", [128, ND * T], BF16)
        self.B2 = p.sbuf("B2", [128, NT * D], BF16)
        self.ident = p.sbuf("ident", [128, 128], BF16)
        self.ident_f = p.sbuf("identf", [128, 128], F32)
        self.m_kgtq = p.sbuf("mkgtq", [128, 128], F32)
        self.m_kleq = p.sbuf("mkleq", [128, 128], F32)
        self.ones_row_f = p.sbuf("onesrf", [1, 128], F32)
        self.ones_row_bf = p.sbuf("onesrb", [1, 128], BF16)
        self.ones_col_bf = p.sbuf("onescb", [128, 1], BF16)
        self.ones_col_f = p.sbuf("onescf", [128, 1], F32)
        p.v(lambda e: e.memset(self.ones_col_f[:, :], 1.0))
        self._joinbuf = p.sbuf("joinb", [1, 1], F32)
        self.epsb = p.sbuf("epsb", [128, 1], F32)
        self.c0125 = p.sbuf("c0125", [128, 1], F32)
        p.v(lambda e: e.memset(self.epsb[:, :], float(EPS)))
        p.v(lambda e: e.memset(self.c0125[:, :], 0.125))
        self.dma_c(self.ident[:, :], self.inp["identity"])
        self.dma_f(self.ident_f[:, :], self.inp["identity"])
        self.dma_f(self.m_kgtq[:, :], self.inp["mask_kgtq"])
        self.dma_f(self.m_kleq[:, :], self.inp["mask_kleq"])
        self.dma_f(self.ones_row_f[:, :], self.inp["ones_row"])
        self.dma_c(self.ones_row_bf[:, :], self.inp["ones_row"])
        self.dma_c(self.ones_col_bf[:, :], self.inp["ones_col"])
        p.fence()

    def dma_f(self, dst, src, wait=(), q="ds"):
        return self.p.s(lambda e: e.dma_start(out=dst, in_=src), wait=wait, inc=q)

    def dma_c(self, dst, src, wait=(), q="dg"):
        return self.p.g(lambda e: e.dma_start(out=dst, in_=src), wait=wait, inc=q)

    def dma_w(self, dst2d, src2d, nt_, wait=(), q="dg"):
        src3 = src2d.rearrange("(dt p) c -> p dt c", p=128)
        dst3 = dst2d.rearrange("p (dt c) -> p dt c", dt=nt_)
        return self.p.g(lambda e: e.dma_start(out=dst3, in_=src3), wait=wait, inc=q)

    def dump(self, name, src_ap, shape, dtype=F32):
        if name not in self.dbg:
            return
        t = self.nc.dram_tensor("dbg_" + name, list(shape), dtype, kind="ExternalOutput").ap()
        self.dbg_names.append(name)
        self.p.fence()
        self.dma_f(t, src_ap)
        self.p.fence()

    def ln_pass(self, src, src2=None, copy_out=None,
                tm_out=None, T_out=None, tiles=range(NT)):
        p = self.p
        tm_out = tm_out if tm_out is not None else self.B2
        T_out = T_out if T_out is not None else self.B1
        mark = p.push()
        NS = 4
        xa = p.sbuf("lnx", [128, NS * D], F32)
        xb = p.sbuf("lnx2", [128, NS * D], F32) if src2 is not None else None
        stat = [p.sbuf(f"lnst{i}", [128, 12], F32) for i in range(NS)]
        mv = [p.sbuf(f"lnmv{i}", [128, 2], F32) for i in range(NS)]
        rs = [p.sbuf(f"lnrs{i}", [128, 1], F32) for i in range(NS)]
        sd = [p.sbuf(f"lnsd{i}", [128, 1], F32) for i in range(NS)]
        nmr = [p.sbuf(f"lnnm{i}", [128, 1], F32) for i in range(NS)]
        ptr = [p.psum("lntr0", [128, 512], BF16), p.psum("lntr1", [128, 512], BF16)]
        free_tok = [None] * NS
        tr_free = [None, None]
        stat_free = [None] * NS
        for t in tiles:
            sl = t % NS
            xap = xa[:, sl * D:(sl + 1) * D]
            tok = self.dma_f(xap, src[t * 128:(t + 1) * 128, :],
                             wait=free_tok[sl] or (),
                             q=("dsa", "dsb", "dsa", "dsb")[sl])
            if src2 is not None:
                bap = xb[:, sl * D:(sl + 1) * D]
                tok2 = self.dma_f(bap, src2[t * 128:(t + 1) * 128, :],
                                  wait=free_tok[sl] or (),
                                  q=("dsc", "dsd", "dsc", "dsd")[sl])
                tok = p.v(lambda e, xap=xap, bap=bap: e.tensor_tensor(
                    xap, xap, bap, ALU.add), wait=[tok, tok2])
                cp_tok = None
                if copy_out is not None:
                    cp_tok = self.dma_f(copy_out[t * 128:(t + 1) * 128, :], xap,
                                        wait=[tok])
            p.v(lambda e, xap=xap, sl=sl: e.bn_stats(stat[sl][:, 0:6], xap[:, 0:512]),
                wait=[tok] + (stat_free[sl] or []))
            t_st = p.v(lambda e, xap=xap, sl=sl: e.bn_stats(stat[sl][:, 6:12],
                                                            xap[:, 512:1024]),
                       wait=[tok] + (stat_free[sl] or []))
            t_mv = p.v(lambda e, sl=sl: e.bn_aggr(mv[sl][:, :], stat[sl][:, :]),
                       wait=[t_st])
            t_sd = p.a(lambda e, sl=sl: e.activation(sd[sl][:, :], mv[sl][:, 1:2],
                                                     AF.Sqrt, bias=self.epsb[:, :]),
                       wait=[t_mv])
            t_rs = p.v(lambda e, sl=sl: e.reciprocal(rs[sl][:, :], sd[sl][:, :]),
                       wait=[t_sd])
            t_nm = p.v(lambda e, sl=sl: e.tensor_scalar(nmr[sl][:, :], mv[sl][:, 0:1],
                                                        rs[sl][:, :], -1.0,
                                                        ALU.mult, ALU.mult),
                       wait=[t_rs])
            t_xn = p.v(lambda e, xap=xap, t=t, sl=sl: e.tensor_scalar(
                tm_out[:, t * D:(t + 1) * D], xap, rs[sl][:, :], nmr[sl][:, :],
                ALU.mult, ALU.add), wait=[t_nm])
            stat_free[sl] = [t_nm]
            free_tok[sl] = [t_xn] if src2 is None else [t_xn, cp_tok]
            for jg in range(2):
                last = None
                pslot = ptr[jg]
                for j4 in range(4):
                    j = jg * 4 + j4
                    last = p.t(lambda e, t=t, j=j, j4=j4, pslot=pslot: e.transpose(
                        pslot[:, j4 * 128:(j4 + 1) * 128],
                        tm_out[:, t * D + j * 128:t * D + (j + 1) * 128],
                        self.ident[:, :]),
                        wait=[t_xn] + ([tr_free[jg]] if tr_free[jg] is not None and
                                       j4 == 0 else []))
                tr_free[jg] = p.v(lambda e, t=t, jg=jg, pslot=pslot: e.tensor_copy(
                    T_out.rearrange("p (j tt) -> p j tt", j=ND)[:, jg * 4:(jg + 1) * 4,
                                                                t * 128:(t + 1) * 128],
                    pslot.rearrange("p (j tt) -> p j tt", j=4)[:, :, :]),
                    wait=[last])
        p.pop(mark)
        p.fence()

    def attention(self, layer):
        p = self.p
        L1 = (layer == 1)
        aoT = self.B2 if L1 else p.sbuf("aoT", [128, ND * T], BF16)
        mark = p.push()
        vt = p.sbuf("vaug", [128, NT * H * VA], BF16)
        vw = p.sbuf("vw", [128, ND * D], BF16)
        if not L1:
            vsrc = self.inp["v_w"]
        else:
            vsrc = self.inp["qkv_wT"][:, 2048:3072]
        tok_vw = self.dma_w(vw[:, :], vsrc, ND)
        pvv = p.psum("pvv", [128, 1024], F32)
        vev = None
        for t in range(NT):
            last = None
            for dc in range(2):
                for d in range(ND):
                    last = p.t(lambda e, t=t, dc=dc, d=d: e.matmul(
                        pvv[:, dc * 512:(dc + 1) * 512],
                        self.B1[:, d * T + t * 128:d * T + (t + 1) * 128],
                        vw[:, d * D + dc * 512:d * D + (dc + 1) * 512],
                        start=(d == 0), stop=(d == ND - 1)),
                        wait=[tok_vw] + ([vev] if vev is not None and dc == 0 and d == 0 else []))
            vev = p.v(lambda e, t=t: e.tensor_copy(
                vt.rearrange("p (tt h c) -> p tt h c", tt=NT, h=H)[:, t, :, 0:64],
                pvv.rearrange("p (h c) -> p h c", h=H)[:, :, :]), wait=[last])
        vready = p.v(lambda e: e.memset(
            vt.rearrange("p (tth c) -> p tth c", c=VA)[:, :, 64:65], 1.0), wait=[vev])
        p.fence()
        p.free(pvv, vw)
        ctx = {}
        if not L1:
            ctx["qkb"] = p.sbuf("qkb", [128, 16], F32)
            self.dma_f(ctx["qkb"][:, :], self.inp["qk_bT"])
            for nm in ("resid64", "skip64", "centre64"):
                ctx[nm] = p.sbuf(nm, [64, H], F32)
                tk = self.dma_f(ctx[nm][:, :], self.inp[nm])
            ctx["negc64"] = p.sbuf("negc64", [64, H], F32)
            p.v(lambda e: e.tensor_scalar(ctx["negc64"][:, :], ctx["centre64"][:, :],
                                          -1.0, None, ALU.mult), wait=[tk])
            ctx["invidx"] = p.sbuf("invidx", [64, T], F32)
            self.dma_f(ctx["invidx"][:, :], self.inp["invidx64"])
            ctx["vTh"] = [p.sbuf("vTh0", [64, T], BF16), p.sbuf("vTh1", [64, T], BF16)]
            ctx["csh"] = [p.sbuf("csh0", [64, T], F32), p.sbuf("csh1", [64, T], F32)]
            ctx["zb"] = [p.sbuf("zb0", [64, 512], F32), p.sbuf("zb1", [64, 512], F32)]
            ctx["zero64"] = p.sbuf("zero64", [64, 1], F32)
            p.v(lambda e: e.memset(ctx["zero64"][:, :], 0.0))
        else:
            ctx["cosb"] = p.sbuf("cosb", [128, T], F32)
            ctx["sinb"] = p.sbuf("sinb", [128, T], F32)
            self.dma_f(ctx["cosb"][:, :], self.inp["cos128"])
            self.dma_f(ctx["sinb"][:, :], self.inp["sin128"])
            ctx["sink_sb"] = p.sbuf("sink", [1, 16], F32)
            self.dma_f(ctx["sink_sb"][:, :], self.inp["sinks"])
            ctx["shufb"] = p.sbuf("shuf", [128, T], BF16)
            ctx["sq16"] = p.sbuf("sq16", [128, T], BF16)
            ctx["lntA"] = p.sbuf("lntA", [128, T], F32)
            ctx["lntB"] = p.sbuf("lntB", [128, T], F32)
            ctx["blk2b"] = p.sbuf("blk2b", [128, 2], BF16)
            self.dma_c(ctx["blk2b"][:, :], self.inp["blk2"])
            ctx["blk2T"] = p.sbuf("blk2Tf", [2, 128], F32)
            self.dma_f(ctx["blk2T"][:, :], self.inp["blk2T"])
            for wn, key in (("qnw", "qn_w128"), ("qnb", "qn_b128"),
                            ("knw", "kn_w128"), ("knb", "kn_b128")):
                ctx[wn] = p.sbuf(wn, [128, 1], F32)
                self.dma_f(ctx[wn][:, :], self.inp[key])
        ctx["neg1"] = p.sbuf("neg1L0", [1, 1], F32)
        p.v(lambda e: e.memset(ctx["neg1"][:, :], -1.0))
        ctx["qb"] = p.sbuf("qbuf", [128, T], BF16)
        ctx["kb"] = p.sbuf("kbuf", [128, T], BF16)
        ctx["pairw"] = p.sbuf("pairw", [128, ND * 256], BF16)
        ctx["sq"] = p.sbuf("sqb", [128, T], BF16)
        ctx["nrm"] = p.sbuf("nrm", [1, T], F32)
        ctx["mx"] = p.sbuf("mxb", [1, 8], F32)
        ctx["chb"] = p.sbuf("chb", [128, 2], F32)
        ctx["sinkx"] = p.sbuf("sinkx", [1, 2], F32)
        ctx["pT"] = [p.sbuf("pT0", [128, 2 * 512], BF16),
                     p.sbuf("pT1", [128, 2 * 512], BF16)]
        ctx["invr"] = [p.sbuf("invr0", [1, 512], F32), p.sbuf("invr1", [1, 512], F32)]
        ctx["ev64"] = [p.sbuf("ev640", [64, 512], F32), p.sbuf("ev641", [64, 512], F32)]
        ctx["t1b"] = [p.sbuf("t1b0", [64, 512], F32), p.sbuf("t1b1", [64, 512], F32)]
        ctx["vready"] = vready
        ctx["vt"] = vt
        ctx["aoT"] = aoT
        p.fence()
        stage = getattr(self, "attn_stage", None)
        tail = None
        if stage != "v":
            npair = 1 if stage in ("pairs", "heads1") else 8
            for jp in range(npair):
                tail = self.attn_pair(layer, jp, ctx, tail)
        p.fence()
        p.pop(mark)
        return aoT

    def attention_l1(self):
        p = self.p
        aoT = self.B2
        mark = p.push()
        p.fence()
        vt = p.sbuf("vaug", [128, NT * H * VA], BF16)
        vw = p.sbuf("vw", [128, ND * D], BF16)
        tok_vw = self.dma_w(vw[:, :], self.inp["qkv_wT"][:, 2048:3072], ND)
        pvv = p.psum("pvv", [128, 1024], F32)
        vev = None
        for t in range(NT):
            last = None
            for dc in range(2):
                for d in range(ND):
                    last = p.t(lambda e, t=t, dc=dc, d=d: e.matmul(
                        pvv[:, dc * 512:(dc + 1) * 512],
                        self.B1[:, d * T + t * 128:d * T + (t + 1) * 128],
                        vw[:, d * D + dc * 512:d * D + (dc + 1) * 512],
                        start=(d == 0), stop=(d == ND - 1)),
                        wait=[tok_vw] + ([vev] if vev is not None and dc == 0 and d == 0 else []))
            vev = p.v(lambda e, t=t: e.tensor_copy(
                vt.rearrange("p (tt h c) -> p tt h c", tt=NT, h=H)[:, t, :, 0:64],
                pvv.rearrange("p (h c) -> p h c", h=H)[:, :, :]), wait=[last])
        vready = p.v(lambda e: e.memset(
            vt.rearrange("p (tth c) -> p tth c", c=VA)[:, :, 64:65], 1.0), wait=[vev])
        p.fence()
        p.free(pvv, vw)
        ctx = {}
        ctx["cosb"] = p.sbuf("cosb", [128, T], F32)
        ctx["sinb"] = p.sbuf("sinb", [128, T], F32)
        self.dma_f(ctx["cosb"][:, :], self.inp["cos128"])
        self.dma_f(ctx["sinb"][:, :], self.inp["sin128"])
        ctx["sink_sb"] = p.sbuf("sink", [1, 16], F32)
        self.dma_f(ctx["sink_sb"][:, :], self.inp["sinks"])
        ctx["shufb"] = p.sbuf("shuf", [128, T], BF16)
        ctx["sq16"] = p.sbuf("sq16", [128, T], BF16)
        ctx["ropeA"] = p.sbuf("ropeA", [128, T], F32)
        ctx["ropeB"] = p.sbuf("ropeB", [128, T], F32)
        ctx["blk2b"] = p.sbuf("blk2b", [128, 2], BF16)
        self.dma_c(ctx["blk2b"][:, :], self.inp["blk2"])
        ctx["blk2T"] = p.sbuf("blk2Tf", [2, 128], F32)
        self.dma_f(ctx["blk2T"][:, :], self.inp["blk2T"])
        for wn, key in (("qnw", "qn_w128"), ("qnb", "qn_b128"),
                        ("knw", "kn_w128"), ("knb", "kn_b128")):
            ctx[wn] = p.sbuf(wn, [128, 1], F32)
            self.dma_f(ctx[wn][:, :], self.inp[key])
        ctx["qbs"] = [p.sbuf("qb0", [128, T], BF16), p.sbuf("qb1", [128, T], BF16)]
        ctx["kbs"] = [p.sbuf("kb0", [128, T], BF16), p.sbuf("kb1", [128, T], BF16)]
        ctx["pairw"] = p.sbuf("pairw", [128, ND * 256], BF16)
        ctx["sq"] = p.sbuf("sqb", [128, T], BF16)
        ctx["nrm"] = p.sbuf("nrm", [1, T], F32)
        ctx["mxs"] = [p.sbuf("mx0", [1, 8], F32), p.sbuf("mx1", [1, 8], F32)]
        ctx["chbs"] = [p.sbuf("chb0", [128, 2], F32), p.sbuf("chb1", [128, 2], F32)]
        ctx["sinkxs"] = [p.sbuf("skx0", [1, 2], F32), p.sbuf("skx1", [1, 2], F32)]
        ctx["lnmu"] = [p.sbuf("lnmu0", [2, 512], BF16), p.sbuf("lnmu1", [2, 512], BF16)]
        ctx["lnvr"] = [p.sbuf("lnvr0", [2, 512], F32), p.sbuf("lnvr1", [2, 512], F32)]
        ctx["lnm2"] = [p.sbuf("lnm20", [2, 512], F32), p.sbuf("lnm21", [2, 512], F32)]
        ctx["lnrs"] = [p.sbuf("lnrs0", [2, 512], BF16), p.sbuf("lnrs1", [2, 512], BF16)]
        ctx["blk2Tb"] = p.sbuf("blk2Tb", [2, 128], BF16)
        self.dma_c(ctx["blk2Tb"][:, :], self.inp["blk2T"])
        ctx["mxsc"] = p.sbuf("mxsc", [1, 16], F32)
        ctx["neg1"] = p.sbuf("neg1", [1, 1], F32)
        p.v(lambda e: e.memset(ctx["neg1"][:, :], -1.0))
        ctx["nhalf2"] = p.sbuf("nhalf2", [2, 1], F32)
        p.v(lambda e: e.memset(ctx["nhalf2"][:, :], -0.5))
        ctx["pT"] = [p.sbuf("pT0", [128, 2 * 512], BF16),
                     p.sbuf("pT1", [128, 2 * 512], BF16)]
        ctx["invr"] = [p.sbuf("invr0", [1, 512], F32), p.sbuf("invr1", [1, 512], F32)]
        ctx["ev64"] = [p.sbuf("ev640", [64, 512], F32), p.sbuf("ev641", [64, 512], F32)]
        ctx["vready"] = vready
        ctx["vt"] = vt
        ctx["aoT"] = aoT
        ctx["W2"] = [p.psum("WA", [128, 512], F32), p.psum("SA", [128, 512], F32),
                     p.psum("WB", [128, 512], F32), p.psum("SB", [128, 512], F32)]
        ctx["stp"] = [p.psum("stA", [128, 512], F32), p.psum("stB", [128, 512], F32)]
        ctx["pvP"] = p.psum("pvP", [VA, 512], F32)
        ctx["pabP"] = p.psum("pabP", [64, 512], F32)
        p.fence()
        state = {"Afree": [[None, None], [None, None]], "pairw_mm": None,
                 "nrm_free": None, "qb_free": [None, None], "rope_sc": None}
        hand = [None] * 8
        btail = None
        for jp in range(8):
            hand[jp] = self.attn_pA_l1(jp, ctx, state)
            if jp >= 1:
                btail = self.attn_pB_l1(jp - 1, ctx, hand[jp - 1], btail)
                state["qb_free"][(jp - 1) % 2] = btail
        btail = self.attn_pB_l1(7, ctx, hand[7], btail)
        self.dump("ao", aoT[:, :], [128, ND * T], BF16)
        self.attn_tail = btail
        p.pop(mark)
        return aoT

    def attn_pA_l1(self, jp, ctx, state):
        p = self.p
        par = jp % 2
        W2 = [ctx["W2"][2 * par], ctx["W2"][2 * par + 1]]
        bank_busy = list(state["Afree"][par])
        qb, kb = ctx["qbs"][par], ctx["kbs"][par]
        pairw = ctx["pairw"]
        qbf = state["qb_free"][par]
        wsrc = self.inp["qkv_wT"]
        wq = self.dma_c(pairw.rearrange("p (d two c) -> p d two c", d=ND, two=2)[:, :, 0, :],
                        wsrc[:, jp * 128:(jp + 1) * 128]
                        .rearrange("(dt p) c -> p dt c", p=128),
                        wait=[state["pairw_mm"]] if state["pairw_mm"] else (), q="dpw")
        wk = self.dma_c(pairw.rearrange("p (d two c) -> p d two c", d=ND, two=2)[:, :, 1, :],
                        wsrc[:, 1024 + jp * 128:1024 + (jp + 1) * 128]
                        .rearrange("(dt p) c -> p dt c", p=128),
                        wait=[state["pairw_mm"]] if state["pairw_mm"] else (), q="dpw")
        cps = [[None] * 4, [None] * 4]
        lastmm = None
        for ci in range(8):
            qk, cc = divmod(ci, 4)
            b = ci % 2
            pp = W2[b]
            wtok = wq if qk == 0 else wk
            for d in range(ND):
                lastmm = p.t(lambda e, qk=qk, cc=cc, d=d, pp=pp: e.matmul(
                    pp[:, :], pairw[:, d * 256 + qk * 128:d * 256 + qk * 128 + 128],
                    self.B1[:, d * T + cc * 512:d * T + (cc + 1) * 512],
                    start=(d == 0), stop=(d == ND - 1)),
                    wait=([wtok] if d == 0 else []) +
                    ([bank_busy[b]] if bank_busy[b] and d == 0 else []))
            buf = qb if qk == 0 else kb
            cp = p.a(lambda e, buf=buf, cc=cc, pp=pp: e.activation(
                buf[:, cc * 512:(cc + 1) * 512], pp[:, :], AF.Copy),
                wait=[lastmm] + ([qbf] if qbf and ci < 2 else []))
            bank_busy[b] = cp
            cps[qk][cc] = cp
        state["pairw_mm"] = lastmm
        if self.dbg and jp == 0:
            self.p.op("sync", None, wait=[cps[0][3], cps[1][3]])
            self.dump("qraw0", qb[:, :], [128, T], BF16)
            self.dump("kraw0", kb[:, :], [128, T], BF16)
        rtoks = []
        rope_prev = state["rope_sc"]
        for qk, (buf, wsb, bsb) in enumerate(
                ((qb, ctx["qnw"], ctx["qnb"]), (kb, ctx["knw"], ctx["knb"]))):
            if self.dbg and jp == 0 and qk == 0:
                self._dbg_ln = "qln0"
                self._dbg_rope = True
            r3 = self.ln64_rope_l1(buf, cps[qk], wsb, bsb, ctx, W2, bank_busy,
                                   rope_prev)
            rope_prev = r3
            rtoks.append(r3)
        state["rope_sc"] = rope_prev
        mx, chb, sinkx = ctx["mxs"][par], ctx["chbs"][par], ctx["sinkxs"][par]
        mxsc, sq = ctx["mxsc"], ctx["sq"]
        tmx = None
        for qk, (buf, tin) in enumerate(((qb, rtoks[0]), (kb, rtoks[1]))):
            tsq = p.a(lambda e, buf=buf: e.activation(sq[:, :], buf[:, :], AF.Square),
                      wait=[tin] + ([tmx] if tmx else []))
            for hh in range(2):
                g = 2 * qk + hh
                red = None
                for cc in range(4):
                    b = cc % 2
                    pnb = W2[b]
                    mmn = p.t(lambda e, hh=hh, cc=cc, pnb=pnb: e.matmul(
                        pnb[0:1, :], self.ones_col_bf[hh * 64:(hh + 1) * 64, :],
                        sq[hh * 64:(hh + 1) * 64, cc * 512:(cc + 1) * 512],
                        start=True, stop=True),
                        wait=[tsq] + ([bank_busy[b]] if bank_busy[b] else []))
                    red = p.v(lambda e, g=g, cc=cc, pnb=pnb: e.reduce_max(
                        mxsc[:, g * 4 + cc:g * 4 + cc + 1], pnb[0:1, :], axis=AXX),
                        wait=[mmn])
                    bank_busy[b] = red
                tmx = p.v(lambda e, g=g, mx=mx: e.reduce_max(
                    mx[:, g:g + 1], mxsc[:, g * 4:g * 4 + 4], axis=AXX),
                    wait=[red])
        pcb = W2[0]
        bc = []
        sk_toks = []
        for hh in range(2):
            n1 = p.v(lambda e, hh=hh, mx=mx: e.tensor_tensor(
                mx[:, 4 + hh:5 + hh], mx[:, hh:hh + 1], mx[:, 2 + hh:3 + hh],
                ALU.mult), wait=[tmx])
            n2 = p.a(lambda e, hh=hh, mx=mx: e.activation(
                mx[:, 4 + hh:5 + hh], mx[:, 4 + hh:5 + hh], AF.Sqrt), wait=[n1])
            ng = p.v(lambda e, hh=hh, mx=mx: e.tensor_scalar(
                mx[:, 4 + hh:5 + hh], mx[:, 4 + hh:5 + hh], -CSC, None,
                ALU.mult), wait=[n2])
            mmb = p.t(lambda e, hh=hh, mx=mx: e.matmul(
                pcb[:, hh:hh + 1], self.ones_row_f[:, :], mx[:, 4 + hh:5 + hh],
                start=True, stop=True),
                wait=[ng] + ([bank_busy[0]] if hh == 0 and bank_busy[0] else []))
            bc.append(p.v(lambda e, hh=hh, chb=chb: e.tensor_copy(
                chb[:, hh:hh + 1], pcb[:, hh:hh + 1]), wait=[mmb]))
            sk_toks.append(p.a(lambda e, hh=hh, jp=jp, sinkx=sinkx, mx=mx: e.activation(
                sinkx[:, hh:hh + 1],
                ctx["sink_sb"][:, 2 * jp + hh:2 * jp + hh + 1], AF.Exp,
                bias=mx[:, 4 + hh:5 + hh]), wait=[ng]))
        bank_busy[0] = bc[-1]
        state["Afree"][par] = bank_busy
        if self.dbg and jp == 0:
            self.dump("q0", qb[:, :], [128, T], BF16)
            self.dump("k0", kb[:, :], [128, T], BF16)
            self.dump("mx0", mx[:, :], [1, 8], F32)
            self.dump("chb0", chb[:, :], [128, 2], F32)
        return {"bc": bc[-1], "sk": sk_toks, "par": par}

    def ln64_rope_l1(self, buf, cp_toks, wsb, bsb, ctx, W2, bank_busy, rope_prev):
        p = self.p
        blk2b, blk2Tb = ctx["blk2b"], ctx["blk2Tb"]
        sq16, ropeA, ropeB = ctx["sq16"], ctx["ropeA"], ctx["ropeB"]
        cosb, sinb, shufb = ctx["cosb"], ctx["sinb"], ctx["shufb"]
        t1s = [None] * 4
        t2s = [None] * 4
        t5s = [None] * 4
        ua_toks = [None] * 4
        for cc in range(5):
            if cc < 4:
                b = cc % 2
                pp = W2[b]
                mu, vr, m2, rs = (ctx["lnmu"][b], ctx["lnvr"][b], ctx["lnm2"][b],
                                  ctx["lnrs"][b])
                mm1 = p.t(lambda e, cc=cc, pp=pp: e.matmul(
                    pp[0:2, :], blk2b[:, :], buf[:, cc * 512:(cc + 1) * 512],
                    start=True, stop=True),
                    wait=[cp_toks[cc]] + ([bank_busy[b]] if bank_busy[b] else []) +
                    ([rope_prev] if rope_prev and cc < 2 else []))
                tsq = p.a(lambda e, cc=cc: e.activation(
                    sq16[:, cc * 512:(cc + 1) * 512], buf[:, cc * 512:(cc + 1) * 512],
                    AF.Square), wait=[cp_toks[cc]] + ([rope_prev] if rope_prev and cc < 2 else []))
                mm2 = p.t(lambda e, cc=cc, pp=pp: e.matmul(
                    pp[32:34, :], blk2b[:, :], sq16[:, cc * 512:(cc + 1) * 512],
                    start=True, stop=True), wait=[tsq])
                t1 = p.v(lambda e, mu=mu, pp=pp: e.tensor_scalar(
                    mu[:, :], pp[0:2, :], 1.0 / 64.0, None, ALU.mult),
                    wait=[mm1])
                t2 = p.v(lambda e, vr=vr, pp=pp: e.tensor_scalar(
                    vr[:, :], pp[32:34, :], 1.0 / 64.0, None, ALU.mult),
                    wait=[mm2])
                t3 = p.v(lambda e, m2=m2, mu=mu: e.tensor_tensor(
                    m2[:, :], mu[:, :], mu[:, :], ALU.mult), wait=[t1])
                t4 = p.v(lambda e, vr=vr, m2=m2: e.tensor_tensor(
                    vr[:, :], vr[:, :], m2[:, :], ALU.subtract), wait=[t2, t3])
                t5a = p.a(lambda e, vr=vr: e.activation(
                    vr[:, :], vr[:, :], AF.Ln,
                    bias=self.epsb[0:2, 0:1]), wait=[t4])
                t5s[cc] = p.a(lambda e, vr=vr, rs=rs: e.activation(
                    rs[:, :], vr[:, :], AF.Exp,
                    scale=ctx["nhalf2"][:, :]), wait=[t5a])
                t1s[cc] = t1
                t2s[cc] = t2
            c2 = cc - 1
            if c2 >= 0:
                b = c2 % 2
                pp = W2[b]
                mu, rs = ctx["lnmu"][b], ctx["lnrs"][b]
                bmu = p.t(lambda e, mu=mu, pp=pp: e.matmul(
                    pp[:, :], blk2Tb[:, :], mu[:, :],
                    start=True, stop=True), wait=[t1s[c2], t2s[c2]])
                u1 = p.v(lambda e, c2=c2, pp=pp: e.tensor_tensor(
                    ropeA[:, c2 * 512:(c2 + 1) * 512], buf[:, c2 * 512:(c2 + 1) * 512],
                    pp[:, :], ALU.subtract), wait=[bmu])
                brs = p.t(lambda e, rs=rs, pp=pp: e.matmul(
                    pp[:, :], blk2Tb[:, :], rs[:, :],
                    start=True, stop=True), wait=[u1, t5s[c2]])
                u2 = p.v(lambda e, c2=c2, pp=pp: e.tensor_tensor(
                    ropeB[:, c2 * 512:(c2 + 1) * 512], ropeA[:, c2 * 512:(c2 + 1) * 512],
                    pp[:, :], ALU.mult), wait=[brs])
                ua_toks[c2] = p.v(lambda e, c2=c2: e.tensor_scalar(
                    buf[:, c2 * 512:(c2 + 1) * 512], ropeB[:, c2 * 512:(c2 + 1) * 512],
                    wsb[:, :], bsb[:, :], ALU.mult, ALU.add), wait=[u2])
                bank_busy[b] = u2
        if getattr(self, "_dbg_ln", None):
            self.p.op("sync", None, wait=ua_toks)
            self.dump(self._dbg_ln, buf[:, :], [128, T], BF16)
            self._dbg_ln = None
        sh = None
        for blk in range(4):
            src = blk + 1 if blk % 2 == 0 else blk - 1
            sh = self.p.s(lambda e, blk=blk, src=src: e.dma_start(
                out=shufb[blk * 32:(blk + 1) * 32, :],
                in_=buf[src * 32:(src + 1) * 32, :]), wait=ua_toks)
        r1 = p.v(lambda e: e.tensor_tensor(ropeA[:, :], buf[:, :], cosb[:, :],
                                           ALU.mult), wait=ua_toks)
        r2 = p.v(lambda e: e.tensor_tensor(ropeB[:, :], shufb[:, :], sinb[:, :],
                                           ALU.mult), wait=[sh])
        r3 = p.v(lambda e: e.tensor_tensor(buf[:, :], ropeA[:, :], ropeB[:, :],
                                           ALU.add), wait=[r1, r2])
        if getattr(self, "_dbg_rope", None):
            self.p.op("sync", None, wait=[r3])
            self.dump("shufd", shufb[:, :], [128, T], BF16)
            self.dump("cosd", cosb[:, :], [128, T], F32)
            self.dump("sind", sinb[:, :], [128, T], F32)
            self.dump("ropeAd", ropeA[:, :], [128, T], F32)
            self.dump("ropeBd", ropeB[:, :], [128, T], F32)
            self._dbg_rope = None
        return r3

    def attn_pB_l1(self, jp, ctx, hand, btail):
        par = hand["par"]
        ctx2 = dict(ctx)
        ctx2["qb"] = ctx["qbs"][par]
        ctx2["kb"] = ctx["kbs"][par]
        ctx2["chb"] = ctx["chbs"][par]
        ctx2["sinkx"] = ctx["sinkxs"][par]
        hp = (ctx["stp"], ctx["pvP"], ctx["pabP"])
        t0 = self.head_l1(2 * jp, 0, ctx2, hand["bc"], btail, hand["sk"][0], hp)
        t1 = self.head_l1(2 * jp + 1, 1, ctx2, hand["bc"], t0, hand["sk"][1], hp)
        return t1

    def attn_pair(self, layer, jp, ctx, prev_tail):
        p = self.p
        L1 = (layer == 1)
        qb, kb, pairw, sq = ctx["qb"], ctx["kb"], ctx["pairw"], ctx["sq"]
        nrm, mx, chb = ctx["nrm"], ctx["mx"], ctx["chb"]
        wsrc = self.inp["qk_w"] if not L1 else self.inp["qkv_wT"]
        wq = self.dma_c(pairw.rearrange("p (d two c) -> p d two c", d=ND, two=2)[:, :, 0, :],
                        wsrc[:, jp * 128:(jp + 1) * 128]
                        .rearrange("(dt p) c -> p dt c", p=128),
                        wait=[prev_tail] if prev_tail else (), q="dpw")
        wk = self.dma_c(pairw.rearrange("p (d two c) -> p d two c", d=ND, two=2)[:, :, 1, :],
                        wsrc[:, 1024 + jp * 128:1024 + (jp + 1) * 128]
                        .rearrange("(dt p) c -> p dt c", p=128),
                        wait=[prev_tail] if prev_tail else (), q="dpw")
        pq = p.psum("pq", [128, T], F32)
        pk = p.psum("pk", [128, T], F32)
        qmm = kmm = None
        for c in range(4):
            for d in range(ND):
                qmm = p.t(lambda e, c=c, d=d: e.matmul(
                    pq[:, c * 512:(c + 1) * 512], pairw[:, d * 256:d * 256 + 128],
                    self.B1[:, d * T + c * 512:d * T + (c + 1) * 512],
                    start=(d == 0), stop=(d == ND - 1)),
                    wait=[wq, ctx["vready"]] + ([prev_tail] if prev_tail else []))
        for c in range(4):
            for d in range(ND):
                kmm = p.t(lambda e, c=c, d=d: e.matmul(
                    pk[:, c * 512:(c + 1) * 512], pairw[:, d * 256 + 128:d * 256 + 256],
                    self.B1[:, d * T + c * 512:d * T + (c + 1) * 512],
                    start=(d == 0), stop=(d == ND - 1)), wait=[wk])
        if not L1:
            tq = p.v(lambda e, jp=jp: e.tensor_scalar(
                qb[:, :], pq[:, :], ctx["qkb"][:, jp:jp + 1], None, ALU.add),
                wait=[qmm])
            tk = p.v(lambda e, jp=jp: e.tensor_scalar(
                kb[:, :], pk[:, :], ctx["qkb"][:, 8 + jp:9 + jp], None, ALU.add),
                wait=[kmm])
        else:
            tq = p.a(lambda e: e.activation(qb[:, :], pq[:, :], AF.Copy), wait=[qmm])
            tk = p.a(lambda e: e.activation(kb[:, :], pk[:, :], AF.Copy), wait=[kmm])
        p.free(pq, pk)
        if L1:
            tq = self.qk_ln_rope(qb, tq, ctx["qnw"], ctx["qnb"], ctx, None)
            tk = self.qk_ln_rope(kb, tk, ctx["knw"], ctx["knb"], ctx, tq)
        pn = p.psum("pn", [1, T], F32)
        tmx = None
        free_pn = None
        for qk, (buf, tin) in enumerate(((qb, tq), (kb, tk))):
            tsq = p.a(lambda e, buf=buf: e.activation(sq[:, :], buf[:, :], AF.Square),
                      wait=[tin, tk if L1 else None] + ([tmx] if tmx else []))
            for hh in range(2):
                mmn = None
                for c in range(4):
                    mmn = p.t(lambda e, hh=hh, c=c: e.matmul(
                        pn[:, c * 512:(c + 1) * 512],
                        self.ones_col_bf[hh * 64:(hh + 1) * 64, :],
                        sq[hh * 64:(hh + 1) * 64, c * 512:(c + 1) * 512],
                        start=True, stop=True),
                        wait=[tsq] + ([free_pn] if free_pn and c == 0 else []))
                free_pn = p.v(lambda e, qk=qk, hh=hh: e.reduce_max(
                    mx[:, 2 * qk + hh:2 * qk + hh + 1], pn[:, :], axis=AXX),
                    wait=[mmn])
                tmx = free_pn
        p.free(pn)
        pcb = p.psum("pcb", [128, 2], F32)
        bc = []
        sk_toks = []
        for hh in range(2):
            n1 = p.v(lambda e, hh=hh: e.tensor_tensor(
                mx[:, 4 + hh:5 + hh], mx[:, hh:hh + 1], mx[:, 2 + hh:3 + hh],
                ALU.mult), wait=[tmx])
            n2 = p.a(lambda e, hh=hh: e.activation(
                mx[:, 4 + hh:5 + hh], mx[:, 4 + hh:5 + hh], AF.Sqrt), wait=[n1])
            ng = p.v(lambda e, hh=hh: e.tensor_scalar(
                mx[:, 4 + hh:5 + hh], mx[:, 4 + hh:5 + hh], -CSC, None,
                ALU.mult), wait=[n2])
            mmb = p.t(lambda e, hh=hh: e.matmul(
                pcb[:, hh:hh + 1], self.ones_row_f[:, :], mx[:, 4 + hh:5 + hh],
                start=True, stop=True), wait=[ng] + (bc[-1:] if bc else []))
            bc.append(p.v(lambda e, hh=hh: e.tensor_copy(chb[:, hh:hh + 1],
                                                         pcb[:, hh:hh + 1]), wait=[mmb]))
            if L1:
                sk_toks.append(p.a(lambda e, hh=hh, jp=jp: e.activation(
                    ctx["sinkx"][:, hh:hh + 1],
                    ctx["sink_sb"][:, 2 * jp + hh:2 * jp + hh + 1], AF.Exp,
                    bias=mx[:, 4 + hh:5 + hh]), wait=[ng]))
        p.free(pcb)
        stage = getattr(self, "attn_stage", None)
        nhh = 0 if stage == "pairs" else (1 if stage == "heads1" else 2)
        cs_toks = [None, None]
        if not L1 and nhh:
            ptv = p.psum("ptv", [64, 512], BF16)
            ev_prev = None
            for hh in range(nhh):
                h = 2 * jp + hh
                vTh = ctx["vTh"][hh]
                for g4 in range(4):
                    last = None
                    for i4 in range(4):
                        kt = g4 * 4 + i4
                        last = p.t(lambda e, kt=kt, i4=i4, h=h: e.transpose(
                            ptv[:, i4 * 128:(i4 + 1) * 128],
                            ctx["vt"][:, kt * H * VA + h * VA:kt * H * VA + h * VA + 64],
                            self.ident[:, :]),
                            wait=[ctx["vready"], bc[-1], prev_tail] +
                            ([ev_prev] if ev_prev and i4 == 0 else []))
                    ev_prev = p.v(lambda e, g4=g4, vTh=vTh: e.tensor_copy(
                        vTh[:, g4 * 512:(g4 + 1) * 512], ptv[:, :]), wait=[last])
                cs_toks[hh] = p.v(lambda e, hh=hh: e.tensor_tensor_scan(
                    ctx["csh"][hh][:, :], ctx["vTh"][hh][:, :], ctx["vTh"][hh][:, :],
                    ctx["zero64"][:, :], ALU.add, ALU.bypass), wait=[ev_prev])
            ctx["cs_gate"] = cs_toks[nhh - 1]
            p.free(ptv)
        hps = []
        for hh in range(nhh):
            hps.append(([p.psum(f"st{hh}A", [128, 512], F32),
                         p.psum(f"st{hh}B", [128, 512], F32)],
                        p.psum(f"pv{hh}", [VA, 512], F32),
                        p.psum(f"pab{hh}", [64, 512], F32)))
        tails = []
        if nhh == 1:
            h = 2 * jp
            if not L1:
                tails.append(self.head_l0(h, 0, ctx, bc[-1], prev_tail, cs_toks[0],
                                          hps[0]))
            else:
                tails.append(self.head_l1(h, 0, ctx, bc[-1], prev_tail, sk_toks[0],
                                          hps[0]))
        elif nhh == 2:
            if not L1:
                tails = self.twin_heads(0, jp, ctx, bc[-1], prev_tail, cs_toks,
                                        None, hps)
            else:
                tails = self.twin_heads(1, jp, ctx, bc[-1], prev_tail, None,
                                        sk_toks, hps)
        for (stp, pv, pab) in reversed(hps):
            p.free(pab, pv, stp[1], stp[0])
        if nhh == 0:
            return bc[-1]
        return tails[-1] if len(tails) == 1 else self._join(tails)

    def _join(self, toks):
        sems = {t[0] for t in toks}
        if len(sems) == 1:
            return max(toks, key=lambda t: t[1])
        return self.p.v(lambda e: e.memset(self._joinbuf[:, :], 0.0), wait=toks)

    def qk_ln_rope(self, buf, tin, wsb, bsb, ctx, prev_ln):
        p = self.p
        sq16, lntA, lntB = ctx["sq16"], ctx["lntA"], ctx["lntB"]
        blk2b, blk2T, cosb, sinb, shufb = (ctx["blk2b"], ctx["blk2T"], ctx["cosb"],
                                           ctx["sinb"], ctx["shufb"])
        mu = lntA[0:2, 0:T]
        vr = lntB[0:2, 0:T]
        m2 = shufb[0:2, 0:T]
        ps = p.psum("l1s", [2, T], F32)
        px = p.psum("l1x", [2, T], F32)
        mm1 = mm2 = None
        for c in range(4):
            mm1 = p.t(lambda e, c=c: e.matmul(ps[:, c * 512:(c + 1) * 512], blk2b[:, :],
                                              buf[:, c * 512:(c + 1) * 512],
                                              start=True, stop=True),
                      wait=[tin, prev_ln])
        tsq = p.a(lambda e: e.activation(sq16[:, :], buf[:, :], AF.Square),
                  wait=[tin, prev_ln])
        for c in range(4):
            mm2 = p.t(lambda e, c=c: e.matmul(px[:, c * 512:(c + 1) * 512], blk2b[:, :],
                                              sq16[:, c * 512:(c + 1) * 512],
                                              start=True, stop=True), wait=[tsq])
        t1 = p.v(lambda e: e.tensor_scalar(mu, ps[:, :], 1.0 / 64.0, None,
                                           ALU.mult), wait=[mm1])
        t2 = p.v(lambda e: e.tensor_scalar(vr, px[:, :], 1.0 / 64.0, None,
                                           ALU.mult), wait=[mm2])
        t3 = p.v(lambda e: e.tensor_tensor(m2, mu, mu, ALU.mult), wait=[t1])
        t4 = p.v(lambda e: e.tensor_tensor(vr, vr, m2, ALU.subtract), wait=[t2, t3])
        t5 = p.a(lambda e: e.activation(vr, vr, AF.Sqrt,
                                        bias=self.epsb[0:2, 0:1]), wait=[t4])
        t6 = p.v(lambda e: e.reciprocal(vr, vr), wait=[t5])
        p.free(px, ps)
        pmu = p.psum("pmu", [128, T], F32)
        prs = p.psum("prs", [128, T], F32)
        bmu = brs = None
        for c in range(4):
            bmu = p.t(lambda e, c=c: e.matmul(pmu[:, c * 512:(c + 1) * 512], blk2T[:, :],
                                              mu[:, c * 512:(c + 1) * 512],
                                              start=True, stop=True), wait=[t1, t6])
            brs = p.t(lambda e, c=c: e.matmul(prs[:, c * 512:(c + 1) * 512], blk2T[:, :],
                                              vr[:, c * 512:(c + 1) * 512],
                                              start=True, stop=True), wait=[t6])
        u1 = p.v(lambda e: e.tensor_tensor(lntA[:, :], buf[:, :], pmu[:, :],
                                           ALU.subtract), wait=[bmu, tin])
        u2 = p.v(lambda e: e.tensor_tensor(lntB[:, :], lntA[:, :], prs[:, :], ALU.mult),
                 wait=[u1, brs])
        ua = p.v(lambda e: e.tensor_scalar(buf[:, :], lntB[:, :], wsb[:, :],
                                           bsb[:, :], ALU.mult, ALU.add), wait=[u2])
        p.free(prs, pmu)
        sh = None
        for blk in range(4):
            src = blk + 1 if blk % 2 == 0 else blk - 1
            sh = self.p.s(lambda e, blk=blk, src=src: e.dma_start(
                out=shufb[blk * 32:(blk + 1) * 32, :],
                in_=buf[src * 32:(src + 1) * 32, :]), wait=[ua])
        r1 = p.v(lambda e: e.tensor_tensor(lntA[:, :], buf[:, :], cosb[:, :], ALU.mult),
                 wait=[ua])
        r2 = p.v(lambda e: e.tensor_tensor(lntB[:, :], shufb[:, :], sinb[:, :], ALU.mult),
                 wait=[sh])
        r3 = p.v(lambda e: e.tensor_tensor(buf[:, :], lntA[:, :], lntB[:, :], ALU.add),
                 wait=[r1, r2])
        return r3

    def head_l0(self, h, hh, ctx, tok_ch, head_prev, t_cs_in, hp):
        p = self.p
        base = 64 * hh
        qb, kb, vt = ctx["qb"], ctx["kb"], ctx["vt"]
        pT, invr, ev64, t1b = (ctx["pT"][hh], ctx["invr"][hh], ctx["ev64"][hh],
                               ctx["t1b"][hh])
        chb = ctx["chb"]
        vTh, csh, zb = ctx["vTh"][hh], ctx["csh"][hh], ctx["zb"][hh]
        invidx = ctx["invidx"]
        hstage = getattr(self, "head_stage", None)
        if hstage == "a":
            return t_cs_in
        nqc = 1 if hstage in ("b", "c") else 4
        (stp, pv, pab) = hp
        exp_tok = [None, None]
        pv_slot = [None, None]
        pv_free = head_prev
        pab_free = head_prev
        comb_tok = None
        tail = None
        kglob = 0
        for qc in range(nqc):
            pv_last = None
            nkt = 4 * qc + 4
            for kt in range(nkt):
                r = kt - 4 * qc
                lo = max(0, r * 128)
                sl = kglob % 2
                kglob += 1
                st = stp[sl]
                mm = p.t(lambda e, kt=kt, qc=qc, lo=lo, st=st: e.matmul(
                    st[:, lo:512], kb[base:base + 64, kt * 128:(kt + 1) * 128],
                    qb[base:base + 64, qc * 512 + lo:(qc + 1) * 512],
                    start=True, stop=True),
                    wait=[tok_ch] + ([exp_tok[sl]] if exp_tok[sl] else
                                     [head_prev, t_cs_in, ctx.get("cs_gate")]))
                if r >= 0:
                    mm = p.v(lambda e, st=st, lo=lo: e.tensor_tensor(
                        st[:, lo:lo + 128], st[:, lo:lo + 128], self.m_kgtq[:, :],
                        ALU.add), wait=[mm])
                exp_tok[sl] = p.a(lambda e, st=st, lo=lo, sl=sl: e.activation(
                    pT[:, sl * 512 + lo:(sl + 1) * 512], st[:, lo:512], AF.Exp,
                    bias=chb[:, hh:hh + 1], scale=self.c0125[:, :]),
                    wait=[mm] + ([pv_slot[sl]] if pv_slot[sl] else []))
                pv_last = p.t(lambda e, kt=kt, h=h, lo=lo, sl=sl, nkt=nkt: e.matmul(
                    pv[:, lo:512],
                    vt[:, kt * H * VA + h * VA:kt * H * VA + (h + 1) * VA],
                    pT[:, sl * 512 + lo:(sl + 1) * 512],
                    start=(kt == 0), stop=(kt == nkt - 1)),
                    wait=[exp_tok[sl], ctx["vready"]] +
                    ([pv_free] if pv_free and kt == 0 else []))
                pv_slot[sl] = pv_last
            if hstage == "b":
                tail = exp_tok[0]
                break
            q0 = qc * 512
            ti = p.v(lambda e: e.reciprocal(invr[:, :], pv[64:65, :]),
                     wait=[pv_last] + ([comb_tok] if comb_tok else []))
            ta = p.v(lambda e, h=h: e.tensor_scalar(
                invr[:, :], invr[:, :], ctx["resid64"][0:1, h:h + 1], None, ALU.mult),
                wait=[ti])
            tb = p.t(lambda e: e.matmul(pab[:, :], self.ones_row_f[:, 0:64], invr[:, :],
                                        start=True, stop=True),
                     wait=[ta] + ([pab_free] if pab_free else []))
            te = p.v(lambda e: e.tensor_copy(ev64[:, :], pv[0:64, :]), wait=[ti])
            pv_free = te
            t1 = p.v(lambda e: e.tensor_tensor(t1b[:, :], ev64[:, :], pab[:, :],
                                               ALU.mult), wait=[te, tb])
            pab_free = t1
            tz = p.v(lambda e, q0=q0: e.tensor_tensor(
                zb[:, :], csh[:, q0:q0 + 512], invidx[:, q0:q0 + 512],
                ALU.mult), wait=[t_cs_in] + ([comb_tok] if comb_tok else []))
            tu = p.v(lambda e, h=h: e.scalar_tensor_tensor(
                zb[:, :], zb[:, :], ctx["negc64"][:, h:h + 1],
                t1b[:, :], ALU.mult, ALU.add), wait=[tz, t1])
            comb_tok = p.v(lambda e, q0=q0, h=h: e.scalar_tensor_tensor(
                ctx["aoT"][base:base + 64, (h // 2) * T + q0:(h // 2) * T + q0 + 512],
                vTh[:, q0:q0 + 512], ctx["skip64"][:, h:h + 1], zb[:, :],
                ALU.mult, ALU.add), wait=[tu])
            tail = comb_tok
        return tail

    def head_l1(self, h, hh, ctx, tok_ch, head_prev, tok_sink, hp):
        p = self.p
        base = 64 * hh
        qb, kb, vt = ctx["qb"], ctx["kb"], ctx["vt"]
        pT, invr, ev64 = ctx["pT"][hh], ctx["invr"][hh], ctx["ev64"][hh]
        chb, sinkx = ctx["chb"], ctx["sinkx"]
        (stp, pv, pab) = hp
        exp_tok = [head_prev, head_prev]
        pv_slot = [None, None]
        pv_free = head_prev
        pab_free = head_prev
        comb_tok = None
        tail = None
        kglob = 0
        for qc in range(4):
            pv_last = None
            kts = list(range(max(0, 4 * qc - 2), 4 * qc + 4))
            for kt in kts:
                r = kt - 4 * qc
                lo = max(0, kt * 128 - qc * 512)
                hi = min(512, kt * 128 + 384 - qc * 512)
                sl = kglob % 2
                kglob += 1
                st = stp[sl]
                mm = p.t(lambda e, kt=kt, qc=qc, lo=lo, hi=hi, st=st: e.matmul(
                    st[:, lo:hi], kb[base:base + 64, kt * 128:(kt + 1) * 128],
                    qb[base:base + 64, qc * 512 + lo:qc * 512 + hi],
                    start=True, stop=True),
                    wait=[tok_ch] + ([exp_tok[sl]] if exp_tok[sl] else []))
                if r >= 0:
                    mm = p.v(lambda e, st=st, r=r: e.tensor_tensor(
                        st[:, r * 128:(r + 1) * 128], st[:, r * 128:(r + 1) * 128],
                        self.m_kgtq[:, :], ALU.add), wait=[mm])
                kcol = (r + 2) * 128
                if kcol < hi:
                    mm = p.v(lambda e, st=st, kcol=kcol: e.tensor_tensor(
                        st[:, kcol:kcol + 128], st[:, kcol:kcol + 128],
                        self.m_kleq[:, :], ALU.add), wait=[mm])
                first = (kt == kts[0])
                if first:
                    tz0 = p.v(lambda e, sl=sl: e.memset(
                        pT[:, sl * 512:(sl + 1) * 512], 0.0),
                        wait=[pv_slot[sl]] if pv_slot[sl] else [head_prev])
                else:
                    tz0 = None
                exp_tok[sl] = p.a(lambda e, st=st, lo=lo, hi=hi, sl=sl: e.activation(
                    pT[:, sl * 512 + lo:sl * 512 + hi], st[:, lo:hi], AF.Exp,
                    bias=chb[:, hh:hh + 1], scale=self.c0125[:, :]),
                    wait=[mm, tz0] + ([pv_slot[sl]] if pv_slot[sl] else []))
                mlo, mhi = (0, 512) if first else (lo, hi)
                pv_last = p.t(lambda e, kt=kt, h=h, mlo=mlo, mhi=mhi, sl=sl,
                              first=first, last_=(kt == kts[-1]): e.matmul(
                    pv[:, mlo:mhi],
                    vt[:, kt * H * VA + h * VA:kt * H * VA + (h + 1) * VA],
                    pT[:, sl * 512 + mlo:sl * 512 + mhi],
                    start=first, stop=last_),
                    wait=[exp_tok[sl], ctx["vready"]] +
                    ([pv_free] if pv_free and kt == kts[0] else []))
                pv_slot[sl] = pv_last
            q0 = qc * 512
            tl_ = p.a(lambda e, hh=hh: e.activation(
                invr[:, :], pv[64:65, :], AF.Ln, bias=sinkx[:, hh:hh + 1]),
                wait=[pv_last, tok_sink] + ([comb_tok] if comb_tok else []))
            ti = p.a(lambda e: e.activation(
                invr[:, :], invr[:, :], AF.Exp, scale=ctx["neg1"][:, :]),
                wait=[tl_])
            tb = p.t(lambda e: e.matmul(pab[:, :], self.ones_row_f[:, 0:64], invr[:, :],
                                        start=True, stop=True),
                     wait=[ti] + ([pab_free] if pab_free else []))
            te = p.v(lambda e: e.tensor_copy(ev64[:, :], pv[0:64, :]), wait=[ti])
            pv_free = te
            comb_tok = p.v(lambda e, q0=q0, h=h: e.tensor_tensor(
                ctx["aoT"][base:base + 64, (h // 2) * T + q0:(h // 2) * T + q0 + 512],
                ev64[:, :], pab[:, :], ALU.mult), wait=[te, tb])
            pab_free = comb_tok
            tail = comb_tok
        return tail


    def twin_heads(self, layer, jp, ctx, tok_ch, head_prev, cs_toks, sk_toks, hps):
        p = self.p
        L1 = (layer == 1)
        qb, kb, vt = ctx["qb"], ctx["kb"], ctx["vt"]
        chb = ctx["chb"]
        st_of = [hps[0][0], hps[1][0]]
        pv_of = [hps[0][1], hps[1][1]]
        pab_of = [hps[0][2], hps[1][2]]
        exp_tok = [[None, None], [None, None]]
        pv_slot = [[None, None], [None, None]]
        pv_free = [head_prev, head_prev]
        pab_free = [head_prev, head_prev]
        comb_tok = [None, None]
        tails = [None, None]
        kglob = [0, 0]
        if L1:
            kt_lists = [list(range(max(0, 4 * qc - 2), 4 * qc + 4)) for qc in range(4)]
        else:
            kt_lists = [list(range(4 * qc + 4)) for qc in range(4)]
        for qc in range(4):
            kts = kt_lists[qc]
            pv_last = [None, None]
            for kt in kts:
                for hh in range(2):
                    h = 2 * jp + hh
                    base = 64 * hh
                    stp = st_of[hh]
                    pv = pv_of[hh]
                    pT = ctx["pT"][hh]
                    r = kt - 4 * qc
                    sl = kglob[hh] % 2
                    kglob[hh] += 1
                    st = stp[sl]
                    if L1:
                        lo = max(0, kt * 128 - qc * 512)
                        hi = min(512, kt * 128 + 384 - qc * 512)
                    else:
                        lo = max(0, r * 128)
                        hi = 512
                    mm = p.t(lambda e, kt=kt, qc=qc, lo=lo, hi=hi, st=st, base=base:
                             e.matmul(
                        st[:, lo:hi], kb[base:base + 64, kt * 128:(kt + 1) * 128],
                        qb[base:base + 64, qc * 512 + lo:qc * 512 + hi],
                        start=True, stop=True),
                        wait=[tok_ch] + ([exp_tok[hh][sl]] if exp_tok[hh][sl] else
                                         [head_prev, ctx.get("cs_gate")]))
                    if r >= 0:
                        mm = p.v(lambda e, st=st, r=r: e.tensor_tensor(
                            st[:, r * 128:(r + 1) * 128], st[:, r * 128:(r + 1) * 128],
                            self.m_kgtq[:, :], ALU.add), wait=[mm])
                    if L1:
                        kcol = (r + 2) * 128
                        if kcol < hi:
                            mm = p.v(lambda e, st=st, kcol=kcol: e.tensor_tensor(
                                st[:, kcol:kcol + 128], st[:, kcol:kcol + 128],
                                self.m_kleq[:, :], ALU.add), wait=[mm])
                    first = (kt == kts[0])
                    tz0 = None
                    if L1 and first:
                        tz0 = p.v(lambda e, sl=sl, pT=pT: e.memset(
                            pT[:, sl * 512:(sl + 1) * 512], 0.0),
                            wait=[pv_slot[hh][sl]] if pv_slot[hh][sl] else [head_prev])
                    exp_tok[hh][sl] = p.a(lambda e, st=st, lo=lo, hi=hi, sl=sl, pT=pT,
                                          hh=hh: e.activation(
                        pT[:, sl * 512 + lo:sl * 512 + hi], st[:, lo:hi], AF.Exp,
                        bias=chb[:, hh:hh + 1], scale=self.c0125[:, :]),
                        wait=[mm, tz0] + ([pv_slot[hh][sl]] if pv_slot[hh][sl] else []))
                    if L1 and first:
                        mlo, mhi = 0, 512
                    else:
                        mlo, mhi = lo, hi
                    pv_last[hh] = p.t(lambda e, kt=kt, h=h, mlo=mlo, mhi=mhi, sl=sl,
                                      pv=pv, pT=pT, first=first,
                                      last_=(kt == kts[-1]): e.matmul(
                        pv[:, mlo:mhi],
                        vt[:, kt * H * VA + h * VA:kt * H * VA + (h + 1) * VA],
                        pT[:, sl * 512 + mlo:sl * 512 + mhi],
                        start=first, stop=last_),
                        wait=[exp_tok[hh][sl], ctx["vready"]] +
                        ([pv_free[hh]] if pv_free[hh] and first else []))
                    pv_slot[hh][sl] = pv_last[hh]
            q0 = qc * 512
            for hh in range(2):
                h = 2 * jp + hh
                base = 64 * hh
                pv = pv_of[hh]
                pab = pab_of[hh]
                invr, ev64, t1b = (ctx["invr"][hh], ctx["ev64"][hh],
                                   ctx["t1b"][hh] if not L1 else None)
                if not L1:
                    vTh, csh, zb = ctx["vTh"][hh], ctx["csh"][hh], ctx["zb"][hh]
                    tl_ = p.a(lambda e, pv=pv, invr=invr: e.activation(
                        invr[:, :], pv[64:65, :], AF.Ln),
                        wait=[pv_last[hh]] + ([comb_tok[hh]] if comb_tok[hh] else []))
                    ti = p.a(lambda e, invr=invr: e.activation(
                        invr[:, :], invr[:, :], AF.Exp, scale=ctx["neg1"][:, :]),
                        wait=[tl_])
                    ta = p.v(lambda e, h=h, invr=invr: e.tensor_scalar(
                        invr[:, :], invr[:, :], ctx["resid64"][0:1, h:h + 1], None,
                        ALU.mult), wait=[ti])
                    tb = p.t(lambda e, pab=pab, invr=invr: e.matmul(
                        pab[:, :], self.ones_row_f[:, 0:64], invr[:, :],
                        start=True, stop=True),
                        wait=[ta] + ([pab_free[hh]] if pab_free[hh] else []))
                    te = p.v(lambda e, pv=pv, ev64=ev64: e.tensor_copy(
                        ev64[:, :], pv[0:64, :]), wait=[ti])
                    pv_free[hh] = te
                    t1 = p.v(lambda e, t1b=t1b, ev64=ev64, pab=pab: e.tensor_tensor(
                        t1b[:, :], ev64[:, :], pab[:, :], ALU.mult), wait=[te, tb])
                    pab_free[hh] = t1
                    tz = p.v(lambda e, q0=q0, zb=zb, csh=csh: e.tensor_tensor(
                        zb[:, :], csh[:, q0:q0 + 512], ctx["invidx"][:, q0:q0 + 512],
                        ALU.mult),
                        wait=[cs_toks[hh]] + ([comb_tok[hh]] if comb_tok[hh] else []))
                    tu = p.v(lambda e, h=h, zb=zb, t1b=t1b: e.scalar_tensor_tensor(
                        zb[:, :], zb[:, :], ctx["negc64"][:, h:h + 1], t1b[:, :],
                        ALU.mult, ALU.add), wait=[tz, t1])
                    comb_tok[hh] = p.v(lambda e, q0=q0, h=h, base=base, zb=zb,
                                       vTh=vTh: e.scalar_tensor_tensor(
                        ctx["aoT"][base:base + 64,
                                   (h // 2) * T + q0:(h // 2) * T + q0 + 512],
                        vTh[:, q0:q0 + 512], ctx["skip64"][:, h:h + 1], zb[:, :],
                        ALU.mult, ALU.add), wait=[tu])
                else:
                    tr = p.v(lambda e, pv=pv, hh=hh, invr=invr: e.tensor_scalar(
                        invr[:, :], pv[64:65, :], ctx["sinkx"][:, hh:hh + 1], None,
                        ALU.add),
                        wait=[pv_last[hh], sk_toks[hh]] +
                        ([comb_tok[hh]] if comb_tok[hh] else []))
                    ti = p.v(lambda e, invr=invr: e.reciprocal(invr[:, :], invr[:, :]),
                             wait=[tr])
                    tb = p.t(lambda e, pab=pab, invr=invr: e.matmul(
                        pab[:, :], self.ones_row_f[:, 0:64], invr[:, :],
                        start=True, stop=True),
                        wait=[ti] + ([pab_free[hh]] if pab_free[hh] else []))
                    te = p.v(lambda e, pv=pv, ev64=ev64: e.tensor_copy(
                        ev64[:, :], pv[0:64, :]), wait=[tr])
                    pv_free[hh] = te
                    comb_tok[hh] = p.v(lambda e, q0=q0, h=h, base=base, ev64=ev64,
                                       pab=pab: e.tensor_tensor(
                        ctx["aoT"][base:base + 64,
                                   (h // 2) * T + q0:(h // 2) * T + q0 + 512],
                        ev64[:, :], pab[:, :], ALU.mult), wait=[te, tb])
                    pab_free[hh] = comb_tok[hh]
                tails[hh] = comb_tok[hh]
        return tails

    def proj_pass(self, layer, aoT, acc_ap=None, gate=None):
        p = self.p
        L1 = (layer == 1)
        mark = p.push()
        cw = p.sbuf("cw", [128, ND * D], BF16)
        wsrc = (self.inp["cproj_w"] if not L1 else self.inp["proj_wT"])
        tok_w = self.dma_w(cw[:, :], wsrc, ND, wait=[gate] if gate else ())
        pbr = None
        if L1:
            pbr = p.sbuf("pbr", [128, D], F32)
            self.dma_f(pbr[:, :], self.inp["projb_rep"],
                       wait=[gate] if gate else ())
        else:
            xin = p.sbuf("xin", [128, 2 * D], F32)
            oacc = p.sbuf("oacc", [128, 2 * D], F32)
        pc = [p.psum("pc0", [128, D], F32), p.psum("pc1", [128, D], F32)]
        ev_tok = [None, None]
        out_tok = [None, None]
        if gate is None:
            p.fence()
        mmt = None
        for t in range(NT):
            sl = t % 2
            if not L1:
                tok_x = self.dma_f(xin[:, sl * D:(sl + 1) * D],
                                   self.inp["x"][t * 128:(t + 1) * 128, :],
                                   wait=[out_tok[sl]] if out_tok[sl] else ())
            for dc in range(2):
                for j in range(ND):
                    mmt = p.t(lambda e, t=t, dc=dc, j=j, sl=sl: e.matmul(
                        pc[sl][:, dc * 512:(dc + 1) * 512],
                        aoT[:, j * T + t * 128:j * T + (t + 1) * 128],
                        cw[:, j * D + dc * 512:j * D + (dc + 1) * 512],
                        start=(j == 0), stop=(j == ND - 1)),
                        wait=[tok_w] + ([ev_tok[sl]] if ev_tok[sl] and dc == 0 and j == 0 else []) +
                        ([gate] if gate and t == 0 and dc == 0 and j == 0 else []))
            if not L1:
                tev = p.v(lambda e, sl=sl: e.tensor_tensor(
                    oacc[:, sl * D:(sl + 1) * D], xin[:, sl * D:(sl + 1) * D],
                    pc[sl][:, :], ALU.add), wait=[mmt, tok_x])
                ev_tok[sl] = tev
                out_tok[sl] = self.dma_f(self.acc0_hbm[t * 128:(t + 1) * 128, :],
                                         oacc[:, sl * D:(sl + 1) * D], wait=[tev])
            else:
                ev_tok[sl] = p.v(lambda e, t=t, sl=sl: e.tensor_tensor(
                    acc_ap(t), pc[sl][:, :], pbr[:, :], ALU.add),
                    wait=[mmt])
        if gate is None:
            p.fence()
        last_ev = ev_tok[(NT - 1) % 2]
        p.pop(mark)
        return mmt, last_ev


    def moe_chain(self):
        p = self.p
        self.w1_cons = [None, None]
        self.w2_cons = None
        self.cn_cons = None
        self.pb_free = None
        m = {}
        m["acch"] = p.sbuf("acch", [128, 8 * D], F32)
        m["hslab"] = p.sbuf("hslab", [128, 16 * 1024], BF16)
        m["w1b"] = p.sbuf("w1b", [128, 2 * ND * 512], BF16)
        m["w2b"] = p.sbuf("w2b", [128, 16 * 1024], BF16)
        m["lgb"] = p.sbuf("lgb", [128, 8], F32)
        m["lbb"] = p.sbuf("lbb", [128, 8], BF16)
        m["b1s"] = p.sbuf("b1s", [128, 32], F32)
        m["b1pp"] = p.sbuf("b1pp", [128, 32], F32)
        m["b2s"] = p.sbuf("b2s", [1, 1024], BF16)
        m["rwb"] = [p.sbuf("rwb0", [128, ND * 4], BF16),
                    p.sbuf("rwb1", [128, ND * 4], BF16)]
        m["rbs"] = [p.sbuf("rbs0", [128, 4], F32), p.sbuf("rbs1", [128, 4], F32)]
        m["srep"] = [p.sbuf("srep0", [128, 4], F32), p.sbuf("srep1", [128, 4], F32)]
        m["lgt"] = p.sbuf("lgt", [128, 4], F32)
        m["p4"] = p.sbuf("p4b", [128, 4], F32)
        m["sc1"] = p.sbuf("sc1", [128, 8], F32)
        m["tmask"] = p.sbuf("tmask", [128, 4], F32)
        m["comb"] = [p.sbuf("comb0", [128, 8 * 4], F32),
                     p.sbuf("comb1", [128, 8 * 4], F32)]
        m["scb"] = [p.sbuf("scb0", [128, 8], F32), p.sbuf("scb1", [128, 8], F32)]
        m["mus"] = [p.sbuf("mus0", [128, 8], F32), p.sbuf("mus1", [128, 8], F32)]
        m["stds"] = [p.sbuf("stds0", [128, 8], F32), p.sbuf("stds1", [128, 8], F32)]
        m["aa"] = p.sbuf("aab", [128, 1], F32)
        m["bb"] = p.sbuf("bbb", [128, 1], F32)
        m["stat"] = p.sbuf("mst", [128, 12], F32)
        m["mv"] = p.sbuf("mmv", [128, 2], F32)
        m["rs"] = p.sbuf("mrs", [128, 1], F32)
        m["nmr"] = p.sbuf("mnm", [128, 1], F32)
        self.mm = m
        self.rw_toks = [None, None]
        self.mmA_last = {}
        self.hbase_free = [None, None]
        self.ep_toks = {}
        self.xg_last = {}
        self.prep_out = {}
        self.pending_gate = None
        self.moe_stat_free = None
        self.moe_y_tail = None
        self.moe_prep(0)
        for hidx in range(8):
            self.moe_experts_half(hidx)
        flat = ([m["acch"], m["hslab"], m["w1b"], m["w2b"], m["lgb"], m["lbb"],
                 m["b1s"], m["b1pp"], m["b2s"]] + m["rwb"] + m["rbs"] + m["srep"]
                + [m["lgt"], m["p4"], m["sc1"], m["tmask"]] + m["comb"] + m["scb"]
                + m["mus"] + m["stds"] + [m["aa"], m["bb"], m["stat"], m["mv"],
                                          m["rs"], m["nmr"]])
        p.free(*flat)
        p.fence()

    def moe_prep(self, hidx):
        p = self.p
        m = self.mm
        l, th = divmod(hidx, 2)
        par = hidx % 2
        lp = l % 2
        t0 = th * 8
        b0 = par * 8192
        pgate = self.moe_y_tail
        if th == 0:
            gate = self.mmA_last.get(hidx - 3)
            tok_rw = self.dma_w(m["rwb"][lp][:, :], self.inp["rw"][l], ND,
                                wait=[gate] if gate else ())
            tok_rb = self.dma_f(m["rbs"][lp][:, :], self.inp["rb_rep"][l],
                                wait=[gate] if gate else ())
            sS = p.sbuf("sS", [1, 4], F32)
            pS = p.psum("pS", [1, 4], F32)
            mmS = None
            for d in range(ND):
                mmS = p.t(lambda e, d=d, lp=lp: e.matmul(
                    pS[:, :], self.ones_col_bf[:, :],
                    m["rwb"][lp][:, d * 4:(d + 1) * 4],
                    start=(d == 0), stop=(d == ND - 1)),
                    wait=[tok_rw] + ([pgate] if pgate and d == 0 else []))
            tS = p.v(lambda e: e.tensor_copy(sS[:, :], pS[:, :]), wait=[mmS])
            p.free(pS)
            pSr = p.psum("pSr", [128, 4], F32)
            mmSr = p.t(lambda e: e.matmul(pSr[:, :], self.ones_row_f[:, :], sS[:, :],
                                          start=True, stop=True), wait=[tS])
            tok_srep = p.v(lambda e, lp=lp: e.tensor_copy(m["srep"][lp][:, :],
                                                          pSr[:, :]), wait=[mmSr])
            p.free(pSr, sS)
            self.rw_toks[lp] = (tok_rb, tok_srep)
            pgate = None
        tok_rb, tok_srep = self.rw_toks[lp]
        stat, mv, rs, nmr = m["stat"], m["mv"], m["rs"], m["nmr"]
        mus, stds = m["mus"][par], m["stds"][par]
        epg = self.ep_toks.get(hidx - 2)
        xgg = self.xg_last.get(hidx - 2)
        xh_tok = []
        stat_free = self.moe_stat_free
        for tl in range(8):
            t = t0 + tl
            extra = ([epg[tl]] if epg else []) + ([xgg] if xgg and tl == 0 else [])
            p.v(lambda e, t=t: e.bn_stats(stat[:, 0:6],
                                          self.B2[:, t * D:t * D + 512]),
                wait=([stat_free] if stat_free else []) + extra)
            t_st = p.v(lambda e, t=t: e.bn_stats(stat[:, 6:12],
                                                 self.B2[:, t * D + 512:(t + 1) * D]),
                       wait=([stat_free] if stat_free else []) + extra)
            t_mv = p.v(lambda e: e.bn_aggr(mv[:, :], stat[:, :]), wait=[t_st])
            t_sd = p.a(lambda e, tl=tl: e.activation(stds[:, tl:tl + 1], mv[:, 1:2],
                                                     AF.Sqrt, bias=self.epsb[:, :]),
                       wait=[t_mv])
            t_rs = p.v(lambda e, tl=tl: e.reciprocal(rs[:, :], stds[:, tl:tl + 1]),
                       wait=[t_sd])
            p.v(lambda e, tl=tl: e.tensor_copy(mus[:, tl:tl + 1], mv[:, 0:1]),
                wait=[t_mv])
            t_nm = p.v(lambda e: e.tensor_scalar(nmr[:, :], mv[:, 0:1], rs[:, :], -1.0,
                                                 ALU.mult, ALU.mult), wait=[t_rs])
            stat_free = t_nm
            xh_tok.append(p.v(lambda e, t=t: e.tensor_scalar(
                self.B2[:, t * D:(t + 1) * D], self.B2[:, t * D:(t + 1) * D],
                rs[:, :], nmr[:, :], ALU.mult, ALU.add), wait=[t_nm]))
        self.moe_stat_free = stat_free
        ptr2 = [p.psum("mtr0", [128, 512], BF16), p.psum("mtr1", [128, 512], BF16)]
        trf2 = [None, None]
        hb = self.hbase_free[par]
        for tl in range(8):
            for jg in range(2):
                last = None
                pslot = ptr2[jg]
                for j4 in range(4):
                    j = jg * 4 + j4
                    last = p.t(lambda e, tl=tl, j=j, j4=j4, t0=t0, pslot=pslot:
                               e.transpose(
                        pslot[:, j4 * 128:(j4 + 1) * 128],
                        self.B2[:, (t0 + tl) * D + j * 128:(t0 + tl) * D + (j + 1) * 128],
                        self.ident[:, :]),
                        wait=[xh_tok[tl]] +
                        ([trf2[jg]] if trf2[jg] is not None and j4 == 0 else
                         ([pgate, hb] if tl == 0 and j4 == 0 else [])))
                trf2[jg] = p.v(lambda e, tl=tl, jg=jg, b0=b0, pslot=pslot:
                               e.tensor_copy(
                    self.B1.rearrange("p (j tt) -> p j tt", tt=1024)
                    [:, (b0 // 1024) + jg * 4:(b0 // 1024) + (jg + 1) * 4,
                     tl * 128:(tl + 1) * 128],
                    pslot.rearrange("p (j tt) -> p j tt", j=4)[:, :, :]), wait=[last])
        trf = self._join([trf2[0], trf2[1]])
        p.free(ptr2[1], ptr2[0])
        lgt, p4, sc1, tmask = m["lgt"], m["p4"], m["sc1"], m["tmask"]
        comb, scb = m["comb"][par], m["scb"][par]
        rbs, srep = m["rbs"][lp], m["srep"][lp]
        pA = p.psum("pA", [128, 4], F32)
        comb_toks = []
        for tl in range(8):
            mmA = None
            for d in range(ND):
                mmA = p.t(lambda e, tl=tl, d=d, b0=b0, lp=lp: e.matmul(
                    pA[:, :],
                    self.B1[:, b0 + d * 1024 + tl * 128:b0 + d * 1024 + (tl + 1) * 128],
                    m["rwb"][lp][:, d * 4:(d + 1) * 4],
                    start=(d == 0), stop=(d == ND - 1)),
                    wait=[trf] + ([comb_toks[-1]] if comb_toks else []))
            tg = p.v(lambda e, tl=tl: e.tensor_scalar(
                lgt[:, :], pA[:, :], stds[:, tl:tl + 1], None, ALU.mult), wait=[mmA])
            tg = p.v(lambda e, tl=tl: e.scalar_tensor_tensor(
                lgt[:, :], srep[:, :], mus[:, tl:tl + 1], lgt[:, :], ALU.mult, ALU.add),
                wait=[tg, tok_srep])
            tg = p.v(lambda e: e.tensor_tensor(lgt[:, :], lgt[:, :], rbs[:, :], ALU.add),
                     wait=[tg, tok_rb])
            tm1 = p.v(lambda e: e.reduce_max(sc1[:, 0:1], lgt[:, :], axis=AXX), wait=[tg])
            tnm = p.v(lambda e: e.tensor_scalar(sc1[:, 1:2], sc1[:, 0:1], -1.0, None,
                                                ALU.mult), wait=[tm1])
            tex = p.a(lambda e: e.activation(p4[:, :], lgt[:, :], AF.Exp,
                                             bias=sc1[:, 1:2]), wait=[tnm])
            tm1p = p.v(lambda e: e.reduce_max(sc1[:, 2:3], p4[:, :], axis=AXX), wait=[tex])
            tt_ = p.v(lambda e: e.tensor_scalar(tmask[:, :], p4[:, :], sc1[:, 2:3], None,
                                                ALU.is_lt), wait=[tm1p])
            tpm = p.v(lambda e: e.tensor_tensor(tmask[:, :], p4[:, :], tmask[:, :],
                                                ALU.mult), wait=[tt_])
            tm2p = p.v(lambda e: e.reduce_max(sc1[:, 3:4], tmask[:, :], axis=AXX),
                       wait=[tpm])
            tind = p.v(lambda e: e.tensor_scalar(tmask[:, :], p4[:, :], sc1[:, 3:4], None,
                                                 ALU.is_ge), wait=[tm2p])
            tden = p.v(lambda e: e.tensor_tensor(sc1[:, 4:5], sc1[:, 2:3], sc1[:, 3:4],
                                                 ALU.add), wait=[tm2p])
            tdin = p.v(lambda e: e.reciprocal(sc1[:, 5:6], sc1[:, 4:5]), wait=[tden])
            tx1 = p.v(lambda e: e.tensor_tensor(tmask[:, :], p4[:, :], tmask[:, :],
                                                ALU.mult), wait=[tind])
            tcb = p.v(lambda e, tl=tl: e.tensor_scalar(
                comb[:, tl * 4:(tl + 1) * 4], tmask[:, :], sc1[:, 5:6], None, ALU.mult),
                wait=[tx1, tdin])
            tsc = p.v(lambda e, tl=tl: e.tensor_reduce(
                scb[:, tl:tl + 1], comb[:, tl * 4:(tl + 1) * 4], axis=AXX, op=ALU.add),
                wait=[tcb])
            comb_toks.append(tsc)
        p.free(pA)
        self.mmA_last[hidx] = mmA
        self.prep_out[hidx] = (xh_tok, comb_toks)
        self.pending_gate = comb_toks[-1]

    def moe_experts_half(self, hidx):
        p = self.p
        m = self.mm
        l, th = divmod(hidx, 2)
        par = hidx % 2
        t0 = th * 8
        b0 = par * 8192
        acch = m["acch"]
        aa, bb = m["aa"], m["bb"]
        mus, stds = m["mus"][par], m["stds"][par]
        scb = m["scb"][par]
        xh_tok, comb_toks = self.prep_out.pop(hidx)
        epm1 = self.ep_toks.get(hidx - 1)
        acc_toks = []
        prev = None
        for tl in range(8):
            ta_ = p.v(lambda e, tl=tl: e.tensor_tensor(aa[:, :], scb[:, tl:tl + 1],
                                                       stds[:, tl:tl + 1], ALU.mult),
                      wait=[comb_toks[tl]] + ([prev] if prev else []) +
                      ([epm1[tl]] if epm1 else []))
            tb_ = p.v(lambda e, tl=tl: e.tensor_tensor(bb[:, :], scb[:, tl:tl + 1],
                                                       mus[:, tl:tl + 1], ALU.mult),
                      wait=[ta_])
            prev = p.v(lambda e, tl=tl, t0=t0: e.tensor_scalar(
                acch[:, tl * D:(tl + 1) * D], self.B2[:, (t0 + tl) * D:(t0 + tl + 1) * D],
                aa[:, :], bb[:, :], ALU.mult, ALU.add), wait=[tb_])
            acc_toks.append(prev)
        y_tail = self.cn_cons
        for e_ in range(4):
            acc_toks, y_tail = self.moe_expert(l, th, e_, m, xh_tok, acc_toks,
                                               y_tail, b0, m["comb"][par])
            if e_ == 1 and hidx < 7:
                self.moe_y_tail = y_tail
                self.moe_prep(hidx + 1)
        self.xg_last[hidx] = self.xg_join
        spill = (l == 3)
        eps = []
        for tl in range(8):
            if not spill:
                eps.append(p.v(lambda e, tl=tl, t0=t0: e.tensor_copy(
                    self.B2[:, (t0 + tl) * D:(t0 + tl + 1) * D],
                    acch[:, tl * D:(tl + 1) * D]), wait=[acc_toks[tl]]))
            else:
                eps.append(self.dma_f(
                    self.accf_hbm[(t0 + tl) * 128:(t0 + tl + 1) * 128, :],
                    acch[:, tl * D:(tl + 1) * D], wait=[acc_toks[tl]]))
        self.ep_toks[hidx] = eps
        self.hbase_free[par] = y_tail

    def moe_expert(self, l, th, e_, m, xh_tok, acc_toks, y_tail, b0, comb):
        (acch, hslab, w1b, w2b, lgb, lbb, b1s, b1pp, b2s) = (
            m["acch"], m["hslab"], m["w1b"], m["w2b"], m["lgb"], m["lbb"],
            m["b1s"], m["b1pp"], m["b2s"])
        p = self.p
        t0 = th * 8
        yw = [self.cn_cons] if self.cn_cons else []
        tok_lg = self.dma_f(lgb[:, :], self.inp["ln_gT"][l, e_], wait=yw, q="dcn")
        tok_lb = self.dma_c(lbb[:, :], self.inp["ln_bT"][l, e_], wait=yw, q="dcg")
        tok_b1 = self.dma_f(b1s[:, :], self.inp["b1T"][l, e_], wait=yw, q="dcn")
        tok_b2 = self.dma_c(b2s[:, :], self.inp["b2r"][l, e_], wait=yw, q="dcg")
        ptr2 = [p.psum("mtr0", [128, 512], BF16), p.psum("mtr1", [128, 512], BF16)]
        trf2 = [None, None]
        pg = self.pending_gate
        self.pending_gate = None
        for tl in range(8):
            for jg in range(2):
                last = None
                pslot = ptr2[jg]
                for j4 in range(4):
                    j = jg * 4 + j4
                    last = p.t(lambda e, tl=tl, j=j, j4=j4, pslot=pslot: e.transpose(
                        pslot[:, j4 * 128:(j4 + 1) * 128],
                        self.B2[:, (t0 + tl) * D + j * 128:(t0 + tl) * D + (j + 1) * 128],
                        self.ident[:, :]),
                        wait=[xh_tok[tl], tok_lg] +
                        ([trf2[jg]] if trf2[jg] is not None and j4 == 0 else
                         ([acc_toks[-1], y_tail, pg] if tl == 0 and j4 == 0 else [])))
                for j4 in range(4):
                    j = jg * 4 + j4
                    trf2[jg] = p.v(lambda e, tl=tl, j=j, j4=j4, b0=b0, pslot=pslot:
                                   e.tensor_scalar(
                        self.B1[:, b0 + j * 1024 + tl * 128:b0 + j * 1024 + (tl + 1) * 128],
                        pslot[:, j4 * 128:(j4 + 1) * 128], lgb[:, j:j + 1], None,
                        ALU.mult), wait=[last])
        xg_tok = self._join([trf2[0], trf2[1]])
        self.xg_join = xg_tok
        p.free(ptr2[1], ptr2[0])
        for fh in range(2):
            hs_toks = []
            pb = p.psum("pb", [128, 4], F32)
            php = [p.psum("phA", [128, 1024], F32), p.psum("phB", [128, 1024], F32)]
            gel_slot = [None, None]
            gel_free = None
            w1_toks = [None, None]
            w1_cons = self.w1_cons
            for fs in range(4):
                fcol = fh * 2048 + fs * 512
                sl = fs % 2
                w1_toks[sl] = self.dma_w(
                    w1b[:, sl * ND * 512:(sl + 1) * ND * 512],
                    self.inp["w1"][l, e_][:, fcol:fcol + 512], ND,
                    wait=[w1_cons[sl]] if w1_cons[sl] else (),
                    q=("dwa", "dwb")[sl])
                bias_tok = None
                if th == 0:
                    mmb = None
                    for f4 in range(4):
                        for d in range(ND):
                            mmb = p.t(lambda e, sl=sl, f4=f4, d=d: e.matmul(
                                pb[:, f4:f4 + 1],
                                w1b[:, sl * ND * 512 + d * 512 + f4 * 128:
                                    sl * ND * 512 + d * 512 + (f4 + 1) * 128],
                                lbb[:, d:d + 1], start=(d == 0), stop=(d == ND - 1)),
                                wait=[w1_toks[sl], tok_lb] +
                                ([self.pb_free] if self.pb_free and f4 == 0 and d == 0
                                 else []) +
                                ([y_tail, xg_tok] if fs == 0 and f4 == 0 and d == 0
                                 else []))
                    fg0 = fh * 16 + fs * 4
                    bias_tok = p.v(lambda e, fg0=fg0: e.tensor_tensor(
                        b1pp[:, fg0:fg0 + 4], pb[:, :], b1s[:, fg0:fg0 + 4], ALU.add),
                        wait=[mmb, tok_b1])
                    self.pb_free = bias_tok
                for f4 in range(4):
                    fglob = fh * 16 + fs * 4 + f4
                    floc = fs * 4 + f4
                    psl = floc % 2
                    ph = php[psl]
                    mmh = None
                    for c in range(2):
                        for d in range(ND):
                            mmh = p.t(lambda e, ph=ph, sl=sl, f4=f4, c=c, d=d: e.matmul(
                                ph[:, c * 512:(c + 1) * 512],
                                w1b[:, sl * ND * 512 + d * 512 + f4 * 128:
                                    sl * ND * 512 + d * 512 + (f4 + 1) * 128],
                                self.B1[:, b0 + d * 1024 + c * 512:
                                        b0 + d * 1024 + (c + 1) * 512],
                                start=(d == 0), stop=(d == ND - 1)),
                                wait=[w1_toks[sl], xg_tok] +
                                ([gel_slot[psl]] if gel_slot[psl] and c == 0 and d == 0
                                 else []) +
                                ([y_tail] if y_tail and fs == 0 and f4 == 0 and c == 0
                                  and d == 0 else []))
                    gel_free = p.a(lambda e, ph=ph, floc=floc, fglob=fglob: e.activation(
                        hslab[:, floc * 1024:(floc + 1) * 1024], ph[:, :], AF.Gelu,
                        bias=b1pp[:, fglob:fglob + 1]),
                        wait=[mmh] + ([bias_tok] if bias_tok else []) +
                        ([y_tail] if y_tail and fs == 0 and f4 == 0 else []))
                    gel_slot[psl] = gel_free
                    hs_toks.append(gel_free)
                    w1_cons[sl] = mmh
            p.free(php[1], php[0], pb)
            tok_w2 = self.dma_w(w2b[:, :],
                                self.inp["w2"][l, e_][fh * 2048:(fh + 1) * 2048, :],
                                16, wait=[self.w2_cons] if self.w2_cons else (),
                                q="dw2")
            for tw in range(2):
                pys = [p.psum(f"py{i}", [128, 1024], F32) for i in range(4)]
                for tt2 in range(4):
                    tl = tw * 4 + tt2
                    for dc in range(2):
                        mmy = None
                        if fh == 0:
                            mmy = p.t(lambda e, tt2=tt2, dc=dc: e.matmul(
                                pys[tt2][:, dc * 512:(dc + 1) * 512],
                                self.ones_row_bf[:, :], b2s[:, dc * 512:(dc + 1) * 512],
                                start=True, stop=False),
                                wait=[tok_b2, hs_toks[-1]] +
                                ([y_tail] if y_tail else []))
                        for floc in range(16):
                            first = (fh != 0) and (floc == 0)
                            mmy = p.t(lambda e, tt2=tt2, dc=dc, floc=floc, tl=tl,
                                      first=first: e.matmul(
                                pys[tt2][:, dc * 512:(dc + 1) * 512],
                                hslab[:, floc * 1024 + tl * 128:
                                      floc * 1024 + (tl + 1) * 128],
                                w2b[:, floc * 1024 + dc * 512:
                                    floc * 1024 + (dc + 1) * 512],
                                start=first, stop=(floc == 15)),
                                wait=[tok_w2, hs_toks[-1]] +
                                ([y_tail] if y_tail and first else []))
                        last_y = mmy
                        self.w2_cons = mmy
                    tl_tok = p.v(lambda e, tt2=tt2, tl=tl, e_=e_: e.scalar_tensor_tensor(
                        acch[:, tl * D:(tl + 1) * D], pys[tt2][:, :],
                        comb[:, tl * 4 + e_:tl * 4 + e_ + 1],
                        acch[:, tl * D:(tl + 1) * D], ALU.mult, ALU.add),
                        wait=[last_y, acc_toks[tl]])
                    acc_toks[tl] = tl_tok
                    y_tail = tl_tok
                p.free(*pys)
        self.cn_cons = y_tail
        return acc_toks, y_tail

    def routing(self):
        p = self.p
        oh0 = p.sbuf("oh0", [128, NT * 8], BF16)
        oh1 = p.sbuf("oh1", [128, NT * 8], BF16)
        w01 = p.sbuf("w01", [128, 2 * NT], F32)
        mark = p.push()
        gate_bf = p.sbuf("gateb", [128, ND * 8], BF16)
        tok_gw = self.dma_w(gate_bf[:, :], self.inp["gate_wT"], ND)
        p8 = p.sbuf("p8b", [128, 8], F32)
        sc = p.sbuf("rsc", [128, 8], F32)
        zacc = p.sbuf("zacc", [128, 1], F32)
        lgt8 = p.sbuf("lgt8", [128, 8], F32)
        pm8 = p.sbuf("pm8", [128, 8], F32)
        cms = p.sbuf("cms", [1, 16], F32)
        pL = p.psum("pL", [128, 8], F32)
        oh_toks = []
        prevA = None
        for tl in range(NT):
            mmL = None
            for d in range(ND):
                mmL = p.t(lambda e, tl=tl, d=d: e.matmul(
                    pL[:, :], self.B1[:, d * T + tl * 128:d * T + (tl + 1) * 128],
                    gate_bf[:, d * 8:(d + 1) * 8], start=(d == 0), stop=(d == ND - 1)),
                    wait=[tok_gw] + ([prevA] if prevA else []))
            tm1 = p.v(lambda e: e.reduce_max(sc[:, 0:1], pL[:, :], axis=AXX), wait=[mmL])
            tg = p.v(lambda e: e.tensor_copy(lgt8[:, :], pL[:, :]), wait=[mmL])
            prevA = tg
            tnm = p.v(lambda e: e.tensor_scalar(sc[:, 1:2], sc[:, 0:1], -1.0, None,
                                                ALU.mult), wait=[tm1])
            tex = p.a(lambda e: e.activation(p8[:, :], lgt8[:, :], AF.Exp,
                                             bias=sc[:, 1:2], accum_out=zacc[:, :]),
                      wait=[tnm, tg])
            tzi = p.v(lambda e: e.reciprocal(sc[:, 2:3], zacc[:, :]), wait=[tex])
            tpn = p.v(lambda e: e.tensor_scalar(p8[:, :], p8[:, :], sc[:, 2:3], None,
                                                ALU.mult), wait=[tzi])
            tm1p = p.v(lambda e: e.reduce_max(sc[:, 3:4], p8[:, :], axis=AXX), wait=[tpn])
            tlt = p.v(lambda e: e.tensor_scalar(
                pm8[:, :], p8[:, :], sc[:, 3:4], None, ALU.is_lt), wait=[tm1p])
            tmm = p.v(lambda e: e.tensor_tensor(
                pm8[:, :], p8[:, :], pm8[:, :], ALU.mult), wait=[tlt])
            tm2p = p.v(lambda e: e.reduce_max(
                sc[:, 4:5], pm8[:, :], axis=AXX), wait=[tmm])
            t_i0 = p.v(lambda e, tl=tl: e.tensor_scalar(
                oh0[:, tl * 8:(tl + 1) * 8], p8[:, :], sc[:, 3:4], None, ALU.is_ge),
                wait=[tm1p])
            t_ig = p.v(lambda e, tl=tl: e.tensor_scalar(
                oh1[:, tl * 8:(tl + 1) * 8], p8[:, :], sc[:, 4:5], None, ALU.is_ge),
                wait=[tm2p])
            t_i1 = p.v(lambda e, tl=tl: e.tensor_tensor(
                oh1[:, tl * 8:(tl + 1) * 8], oh1[:, tl * 8:(tl + 1) * 8],
                oh0[:, tl * 8:(tl + 1) * 8], ALU.subtract), wait=[t_ig, t_i0])
            tdn = p.v(lambda e: e.tensor_tensor(sc[:, 5:6], sc[:, 3:4], sc[:, 4:5],
                                                ALU.add), wait=[tm2p])
            tdn = p.v(lambda e: e.tensor_scalar(sc[:, 5:6], sc[:, 5:6], 1e-9, None,
                                                ALU.add), wait=[tdn])
            tdi = p.v(lambda e: e.reciprocal(sc[:, 6:7], sc[:, 5:6]), wait=[tdn])
            p.v(lambda e, tl=tl: e.tensor_tensor(w01[:, tl:tl + 1], sc[:, 3:4],
                                                 sc[:, 6:7], ALU.mult), wait=[tdi])
            p.v(lambda e, tl=tl: e.tensor_tensor(w01[:, NT + tl:NT + tl + 1], sc[:, 4:5],
                                                 sc[:, 6:7], ALU.mult), wait=[tdi])
            oh_toks.append(t_i1)
        pcm = p.psum("pcm", [1, 16], F32)
        mcm = None
        for k, ohsb in ((0, oh0), (1, oh1)):
            for tl in range(NT):
                mcm = p.t(lambda e, k=k, tl=tl, ohsb=ohsb: e.matmul(
                    pcm[:, k * 8:(k + 1) * 8], self.ones_col_bf[:, :],
                    ohsb[:, tl * 8:(tl + 1) * 8],
                    start=(tl == 0), stop=(tl == NT - 1)),
                    wait=[oh_toks[tl]] + ([mcm] if mcm else []))
        tcm = p.v(lambda e: e.tensor_copy(cms[:, :], pcm[:, :]), wait=[mcm])
        p.free(pcm)
        if self.mode == "fused":
            cm_tok = self.dma_f(self.cc_in, cms[:, :], wait=[tcm])
        elif self.mode == "counts":
            cm_tok = self.dma_f(self.out_counts, cms[:, :], wait=[tcm])
        else:
            cm_tok = tcm
        p.fence()
        p.pop(mark)
        return oh0, oh1, w01, cm_tok

    def offsets_and_comb(self, comb, oh0, oh1, w01, cm_tok, gate=None, pres=None):
        p = self.p
        mark = p.push()
        ag = p.sbuf("agb", [8, 16], F32)
        if self.mode == "fused":
            p.g(lambda e: e.collective_compute(
                "AllGather", ALU.bypass, replica_groups=[list(range(8))],
                ins=[self.cc_in], outs=[self.cc_out]), wait=[cm_tok], inc="cc", n=1)
            tag = self.dma_f(ag[:, :], self.cc_out,
                             wait=[("cc", 1)] + ([gate] if gate else []))
        else:
            tag = self.dma_f(ag[:, :], self.inp["allcounts"],
                             wait=[cm_tok] + ([gate] if gate else []))
        sel = p.sbuf("selb", [8, 1], F32)
        tsel = self.dma_f(sel[:, :], self.inp["selmask"],
                          wait=[gate] if gate else ())
        if pres is None:
            resA = p.psum("po", [128, 512], F32)
            resB = p.psum("por", [128, 512], F32)
            resT = p.psum("ptb", [128, 512], BF16)
        else:
            resA, resB, resT = pres
        po = resA[0:1, 0:16]
        mo = p.t(lambda e: e.matmul(po[:, :], sel[:, :], ag[:, :], start=True, stop=True),
                 wait=[tag, tsel])
        offs = p.sbuf("offs", [1, 16], F32)
        to = p.v(lambda e: e.tensor_scalar(offs[:, :], po[:, :], -1.0, None, ALU.add),
                 wait=[mo])
        por = resB[:, 0:16]
        mor = p.t(lambda e: e.matmul(por[:, :], self.ones_row_f[:, :], offs[:, :],
                                     start=True, stop=True), wait=[to])
        offr = p.sbuf("offr", [128, 16], F32)
        tor = p.v(lambda e: e.tensor_copy(offr[:, :], por[:, :]), wait=[mor])
        cnt0 = p.sbuf("cnt0", [8, T], F32)
        cnt1 = p.sbuf("cnt1", [8, T], F32)
        ohT = p.sbuf("ohT", [8, 2 * T], F32)
        zero8 = p.sbuf("zero8", [8, 1], F32)
        ohb = p.sbuf("ohb", [128, 8], BF16)
        p.v(lambda e: e.memset(zero8[:, :], 0.0))
        ptb = resT[0:8, 0:128]
        tr_free = None
        for tl in range(NT):
            for k, ohsb in ((0, oh0), (1, oh1)):
                tc = p.v(lambda e, tl=tl, ohsb=ohsb: e.tensor_copy(
                    ohb[:, :], ohsb[:, tl * 8:(tl + 1) * 8]),
                    wait=[tr_free] if tr_free else ())
                tt_ = p.t(lambda e: e.transpose(ptb[:, :], ohb[:, :], self.ident[:, :]),
                          wait=[tc])
                tr_free = p.v(lambda e, k=k, tl=tl: e.tensor_copy(
                    ohT[:, k * T + tl * 128:k * T + (tl + 1) * 128], ptb[:, :]),
                    wait=[tt_])
        ts0 = p.v(lambda e: e.tensor_tensor_scan(cnt0[:, :], ohT[:, 0:T], ohT[:, 0:T],
                                                 zero8[:, :], ALU.add, ALU.bypass),
                  wait=[tr_free])
        ts1 = p.v(lambda e: e.tensor_tensor_scan(cnt1[:, :], ohT[:, T:2 * T],
                                                 ohT[:, T:2 * T], zero8[:, :],
                                                 ALU.add, ALU.bypass), wait=[ts0])
        pct = resA[:, 16:32]
        cadj = p.sbuf("cadj", [128, 16], F32)
        r8 = p.sbuf("r8b", [128, 8], F32)
        rk = p.sbuf("rkb", [128, 4], F32)
        prev = ts1
        for tl in range(NT):
            mt = []
            for k, cnt in ((0, cnt0), (1, cnt1)):
                mt.append(p.t(lambda e, tl=tl, k=k, cnt=cnt: e.matmul(
                    pct[:, k * 8:(k + 1) * 8], cnt[:, tl * 128:(tl + 1) * 128],
                    self.ident_f[0:8, 0:8], start=True, stop=True),
                    wait=([prev] if prev else []) + ([to] if tl == 0 else [])))
            tadj = p.v(lambda e: e.tensor_tensor(cadj[:, :], pct[:, :], offr[:, :],
                                                 ALU.add), wait=[mt[0], mt[1], tor])
            prev = tadj
            rk_t = []
            for k, ohsb in ((0, oh0), (1, oh1)):
                tm_ = p.v(lambda e, tl=tl, k=k, ohsb=ohsb: e.tensor_tensor(
                    r8[:, :], ohsb[:, tl * 8:(tl + 1) * 8], cadj[:, k * 8:(k + 1) * 8],
                    ALU.mult), wait=[tadj])
                trr = p.v(lambda e, k=k: e.reduce_sum(rk[:, k:k + 1], r8[:, :],
                                                      axis=AXX), wait=[tm_])
                rk_t.append(p.v(lambda e, k=k: e.tensor_scalar(
                    rk[:, 2 + k:3 + k], rk[:, k:k + 1], CAP, None, ALU.is_lt),
                    wait=[trr]))
            g0 = p.v(lambda e, tl=tl: e.tensor_tensor(rk[:, 0:1], rk[:, 2:3],
                                                      w01[:, tl:tl + 1], ALU.mult),
                     wait=[rk_t[0]])
            g1 = p.v(lambda e, tl=tl: e.tensor_tensor(rk[:, 1:2], rk[:, 3:4],
                                                      w01[:, NT + tl:NT + tl + 1],
                                                      ALU.mult), wait=[rk_t[1]])
            tc1 = p.v(lambda e, tl=tl: e.tensor_scalar(
                comb[:, tl * 8:(tl + 1) * 8], oh1[:, tl * 8:(tl + 1) * 8], rk[:, 1:2],
                None, ALU.mult), wait=[g1])
            prev = p.v(lambda e, tl=tl: e.scalar_tensor_tensor(
                comb[:, tl * 8:(tl + 1) * 8], oh0[:, tl * 8:(tl + 1) * 8], rk[:, 0:1],
                comb[:, tl * 8:(tl + 1) * 8], ALU.mult, ALU.add), wait=[g0, tc1])
        p.pop(mark)
        return prev

    def l1_ffn(self, comb, acc_ap, gates=(None, None, None)):
        p = self.p
        self.g_cons = [gates[0], gates[0]]
        self.u_cons = [gates[0], gates[0]]
        self.dn_cons = gates[0]
        self.pg_gate = [g for g in gates[1:] if g] or None
        hu = p.sbuf("hub", [128, 8 * T], BF16)
        gs = p.sbuf("gsl", [128, 2 * ND * 256], BF16)
        us = p.sbuf("usl", [128, 2 * ND * 256], BF16)
        dn = p.sbuf("dnb", [128, 8 * D], BF16)
        sg = p.sbuf("sgb", [128, T], BF16)
        acc_toks = [None] * NT
        y_tail = None
        for e_ in range(8):
            for fh in range(2):
                hu_toks = []
                pgh = [p.psum("pgA", [128, 1024], F32), p.psum("pgB", [128, 1024], F32)]
                puh = [p.psum("puA", [128, 1024], F32), p.psum("puB", [128, 1024], F32)]
                g_toks = [None, None]
                u_toks = [None, None]
                g_cons = self.g_cons
                u_cons = self.u_cons
                hu_half = [None, None]
                sil_half = [None, None]
                for h2s in range(4):
                    sl = h2s % 2
                    col0 = fh * 1024 + h2s * 256
                    g_toks[sl] = self.dma_w(
                        gs[:, sl * ND * 256:(sl + 1) * ND * 256],
                        self.inp["gpwT"][e_][:, col0:col0 + 256], ND,
                        wait=[g_cons[sl]] if g_cons[sl] else (),
                        q=("dga", "dgb")[sl])
                    u_toks[sl] = self.dma_w(
                        us[:, sl * ND * 256:(sl + 1) * ND * 256],
                        self.inp["upwT"][e_][:, col0:col0 + 256], ND,
                        wait=[u_cons[sl]] if u_cons[sl] else (),
                        q=("dua", "dub")[sl])
                    for h2 in range(2):
                        f2loc = h2s * 2 + h2
                        for ch in range(2):
                            mmg = mmu = None
                            for c in range(2):
                                for d in range(ND):
                                    mmg = p.t(lambda e, sl=sl, h2=h2, ch=ch, c=c, d=d:
                                              e.matmul(
                                        pgh[ch][:, c * 512:(c + 1) * 512],
                                        gs[:, sl * ND * 256 + d * 256 + h2 * 128:
                                            sl * ND * 256 + d * 256 + (h2 + 1) * 128],
                                        self.B1[:, d * T + (2 * ch + c) * 512:
                                                d * T + (2 * ch + c + 1) * 512],
                                        start=(d == 0), stop=(d == ND - 1)),
                                        wait=[g_toks[sl]] +
                                        ([hu_half[ch]] if hu_half[ch] and c == 0 and
                                         d == 0 else []) +
                                        ([y_tail] if y_tail and h2s == 0 and h2 == 0 and
                                         ch == 0 and c == 0 and d == 0 else []) +
                                        (self.pg_gate if self.pg_gate and h2s == 0 and
                                         h2 == 0 and ch == 0 and c == 0 and d == 0 else []))
                            sil_half[ch] = p.a(lambda e, ch=ch: e.activation(
                                sg[:, ch * 1024:(ch + 1) * 1024], pgh[ch][:, :],
                                AF.Silu), wait=[mmg])
                            for c in range(2):
                                for d in range(ND):
                                    mmu = p.t(lambda e, sl=sl, h2=h2, ch=ch, c=c, d=d:
                                              e.matmul(
                                        puh[ch][:, c * 512:(c + 1) * 512],
                                        us[:, sl * ND * 256 + d * 256 + h2 * 128:
                                            sl * ND * 256 + d * 256 + (h2 + 1) * 128],
                                        self.B1[:, d * T + (2 * ch + c) * 512:
                                                d * T + (2 * ch + c + 1) * 512],
                                        start=(d == 0), stop=(d == ND - 1)),
                                        wait=[u_toks[sl]] +
                                        ([hu_half[ch]] if hu_half[ch] and c == 0 and
                                         d == 0 else []))
                            hu_half[ch] = p.v(lambda e, f2loc=f2loc, ch=ch: e.tensor_tensor(
                                hu[:, f2loc * T + ch * 1024:f2loc * T + (ch + 1) * 1024],
                                sg[:, ch * 1024:(ch + 1) * 1024], puh[ch][:, :],
                                ALU.mult), wait=[sil_half[ch], mmu])
                            hu_toks.append(hu_half[ch])
                        g_cons[sl] = mmg
                        u_cons[sl] = mmu
                p.free(puh[1], puh[0], pgh[1], pgh[0])
                tok_dn = self.dma_w(dn[:, :],
                                    self.inp["downT"][e_][fh * 1024:(fh + 1) * 1024, :],
                                    8, wait=[self.dn_cons] if self.dn_cons else (),
                                    q="ddn")
                for tw in range(4):
                    pys = [p.psum(f"qy{i}", [128, D], F32) for i in range(4)]
                    for tt2 in range(4):
                        tl = tw * 4 + tt2
                        for dc in range(2):
                            mmy = None
                            for f2loc in range(8):
                                mmy = p.t(lambda e, tt2=tt2, dc=dc, f2loc=f2loc,
                                          tl=tl: e.matmul(
                                    pys[tt2][:, dc * 512:(dc + 1) * 512],
                                    hu[:, f2loc * T + tl * 128:f2loc * T + (tl + 1) * 128],
                                    dn[:, f2loc * D + dc * 512:f2loc * D + (dc + 1) * 512],
                                    start=(f2loc == 0), stop=(f2loc == 7)),
                                    wait=[tok_dn, hu_toks[-1]] +
                                    ([y_tail] if y_tail and f2loc == 0 else []))
                            last_y = mmy
                            self.dn_cons = mmy
                        tl_tok = p.v(lambda e, tt2=tt2, tl=tl, e_=e_: e.scalar_tensor_tensor(
                            acc_ap(tl), pys[tt2][:, :],
                            comb[:, tl * 8 + e_:tl * 8 + e_ + 1],
                            acc_ap(tl), ALU.mult, ALU.add),
                            wait=[last_y] + ([acc_toks[tl]] if acc_toks[tl] else []))
                        acc_toks[tl] = tl_tok
                        y_tail = tl_tok
                    p.free(*pys)
        p.free(sg, dn, us, gs, hu)
        p.fence()
        return acc_toks

    def emit_out(self, acc_ap, acc_toks, x1_src):
        p = self.p
        mark = p.push()
        NS = 4
        xo = p.sbuf("xob", [128, NS * D], F32)
        oo = p.sbuf("oob", [128, NS * D], F32)
        out_tok = [None] * NS
        lasts = []
        for tl in range(NT):
            sl = tl % NS
            tx = self.dma_f(xo[:, sl * D:(sl + 1) * D],
                            x1_src[tl * 128:(tl + 1) * 128, :],
                            wait=[out_tok[sl]] if out_tok[sl] else (),
                            q=("dsa", "dsb", "dsa", "dsb")[sl])
            tv = p.v(lambda e, tl=tl, sl=sl: e.tensor_tensor(
                oo[:, sl * D:(sl + 1) * D], xo[:, sl * D:(sl + 1) * D],
                acc_ap(tl), ALU.add),
                wait=[tx, acc_toks[tl]])
            out_tok[sl] = self.dma_f(self.out[tl * 128:(tl + 1) * 128, :],
                                     oo[:, sl * D:(sl + 1) * D], wait=[tv],
                                     q=("dsc", "dsd", "dse", "ds")[sl])
            lasts.append(out_tok[sl])
        p.op("sync", None, wait=lasts[-NS:])
        p.pop(mark)

    def build(self):
        p = self.p
        m = self.mode
        if m in ("l0", "fused"):
            self.ln_pass(self.inp["x"])
            aoT = self.attention(0)
            self.proj_pass(0, aoT)
            self.p.free(aoT)
            p.fence()
            self.moe_chain()
            if m == "l0":
                mark = p.push()
                xo = p.sbuf("xo1", [128, 2 * D], F32)
                oo = p.sbuf("oo1", [128, 2 * D], F32)
                ot = [None, None]
                last = None
                for tl in range(NT):
                    sl = tl % 2
                    t1 = self.dma_f(xo[:, sl * D:(sl + 1) * D],
                                    self.acc0_hbm[tl * 128:(tl + 1) * 128, :],
                                    wait=[ot[sl]] if ot[sl] else ())
                    t2 = self.dma_f(oo[:, sl * D:(sl + 1) * D],
                                    self.accf_hbm[tl * 128:(tl + 1) * 128, :],
                                    wait=[ot[sl]] if ot[sl] else ())
                    tv = p.v(lambda e, sl=sl: e.tensor_tensor(
                        oo[:, sl * D:(sl + 1) * D], oo[:, sl * D:(sl + 1) * D],
                        xo[:, sl * D:(sl + 1) * D], ALU.add), wait=[t1, t2])
                    ot[sl] = self.dma_f(self.out_x1[tl * 128:(tl + 1) * 128, :],
                                        oo[:, sl * D:(sl + 1) * D], wait=[tv])
                    last = ot[sl]
                p.op("sync", None, wait=[last])
                p.pop(mark)
            else:
                self.ln_pass(self.acc0_hbm, src2=self.accf_hbm,
                             copy_out=self.x1_hbm)
                oh0, oh1, w01, cm_tok = self.routing()
                comb = p.sbuf("combL1", [128, NT * 8], F32)
                aoT = self.attention_l1()
                pres = (p.psum("resA", [128, 512], F32),
                        p.psum("resB", [128, 512], F32),
                        p.psum("resT", [128, 512], BF16))
                off_tok = self.offsets_and_comb(comb, oh0, oh1, w01, cm_tok,
                                                gate=self.attn_tail, pres=pres)
                acc1 = p.sbuf("acc1", [128, NT * D], F32)

                def acc_ap(tl):
                    return acc1[:, tl * D:(tl + 1) * D]
                pj_mm, pj_ev = self.proj_pass(1, aoT, acc_ap, gate=self.attn_tail)
                p.free(pres[2], pres[1], pres[0])
                acc_toks = self.l1_ffn(comb, acc_ap, gates=(pj_mm, pj_ev, off_tok))
                self.emit_out(acc_ap, acc_toks, self.x1_hbm)
                p.free(acc1, comb)
                p.free(w01, oh1, oh0)
        elif m == "counts":
            self.ln_pass(self.inp["x1_in"])
            oh0, oh1, w01, cm_tok = self.routing()
            p.op("sync", None, wait=[cm_tok])
            p.free(w01, oh1, oh0)
        elif m == "l1":
            self.ln_pass(self.inp["x1_in"])
            oh0, oh1, w01, cm_tok = self.routing()
            comb = p.sbuf("combL1", [128, NT * 8], F32)
            aoT = self.attention_l1()
            pres = (p.psum("resA", [128, 512], F32),
                    p.psum("resB", [128, 512], F32),
                    p.psum("resT", [128, 512], BF16))
            off_tok = self.offsets_and_comb(comb, oh0, oh1, w01, cm_tok,
                                            gate=self.attn_tail, pres=pres)
            acc1 = p.sbuf("acc1", [128, NT * D], F32)

            def acc_ap(tl):
                return acc1[:, tl * D:(tl + 1) * D]
            pj_mm, pj_ev = self.proj_pass(1, aoT, acc_ap, gate=self.attn_tail)
            p.free(pres[2], pres[1], pres[0])
            acc_toks = self.l1_ffn(comb, acc_ap, gates=(pj_mm, pj_ev, off_tok))
            self.emit_out(acc_ap, acc_toks, self.inp["x1_in"])
            p.free(acc1, comb)
            p.free(w01, oh1, oh0)
        p.finish()
        return self.nc


_CACHE = {}


def _build(mode, shapes_key, shapes):
    key = (mode, shapes_key)
    if key not in _CACHE:
        c = Core(mode, shapes)
        _CACHE[key] = c.build()
    return _CACHE[key]


def kernel(**inputs):
    ins = host_layouts(inputs)
    x = np.ascontiguousarray(np.asarray(inputs["L0_attn_x"], dtype=np.float32))
    shapes = {k: v.shape for k, v in ins.items()}
    skey = tuple(sorted((k, tuple(v)) for k, v in shapes.items()))
    base = {k: ins[k] for k in CONST_KEYS}
    sel = np.zeros((8, 8, 1), np.float32)
    for c in range(8):
        sel[c, :c, 0] = 1.0

    if FUSED:
        nc = _build("fused", skey, shapes)
        maps = []
        for c in range(8):
            m = dict(base)
            for k in L0_KEYS + L1_KEYS:
                m[k] = ins[k]
            m["x"] = x[c]
            m["selmask"] = sel[c]
            maps.append(m)
        res = run_bass_kernel_spmd(nc, maps, core_ids=list(range(8)))
        return np.stack([r["out"] for r in res.results], 0)

    nc0 = _build("l0", skey, shapes)
    maps = []
    for c in range(8):
        m = {k: ins[k] for k in CONST_KEYS + L0_KEYS}
        m["x"] = x[c]
        maps.append(m)
    r0 = run_bass_kernel_spmd(nc0, maps, core_ids=list(range(8)))
    x1 = [np.ascontiguousarray(r["x1"]) for r in r0.results]

    ncc = _build("counts", skey, shapes)
    maps = [dict({k: ins[k] for k in CONST_KEYS}, gate_wT=ins["gate_wT"],
                 x1_in=x1[c]) for c in range(8)]
    rc = run_bass_kernel_spmd(ncc, maps, core_ids=list(range(8)))
    allc = np.ascontiguousarray(
        np.concatenate([r["counts"] for r in rc.results], 0))

    nc1 = _build("l1", skey, shapes)
    maps = []
    for c in range(8):
        m = {k: ins[k] for k in CONST_KEYS + L1_KEYS}
        m["x1_in"] = x1[c]
        m["allcounts"] = allc
        m["selmask"] = sel[c]
        maps.append(m)
    r1 = run_bass_kernel_spmd(nc1, maps, core_ids=list(range(8)))
    return np.stack([r["out"] for r in r1.results], 0)


FUSED = True
